# version 33
# baseline (speedup 1.0000x reference)
import math
from collections import defaultdict
import numpy as np
import ml_dtypes
import concourse.bass as bass
import concourse.mybir as mybir
from concourse.bass_utils import run_bass_kernel_spmd

F32 = mybir.dt.float32
BF16 = mybir.dt.bfloat16
AF = mybir.ActivationFunctionType
ALU = mybir.AluOpType
AX = mybir.AxisListType

D = 2048
H = 16
NT = 16
NC = 16
P = 128
DSL = 32
GRID = 16
INV_SQRT_T = 1.0 / math.sqrt(2048.0)
LN_EPS = 1e-5
CLIP_HI = float(np.float32(1.0) - np.float32(1e-6))
BIGM = 65536.0
BIGI = 4096.0
BIGF = 1e30

E_PAD = [16, 14, 12, 10, 8, 6, 4, 2]
G_PAR = {0: [15, 13, 11, 9, 6, 4, 2, 0], 1: [14, 12, 10, 8, 7, 5, 3, 1]}
CNT = [(e % 8) + 1 for e in range(16)]
C_CORR = [float(2048 - 128 * ek) for ek in E_PAD]

DBG = {}

ENGS = ("sync", "vector", "scalar", "tensor", "gpsimd")


class Plan:
    def __init__(self):
        self.ops = {e: [] for e in ENGS}
        self.cnt = defaultdict(int)
        self.marks = {}

    def op(self, eng, fn, waits=(), inc=None, inc_by=1, inc_each=False):
        ws = [(s, v) for w in waits if w is not None for (s, v) in [w] if v > 0]
        if inc is not None:
            self.cnt[inc] += inc_by
        self.ops[eng].append((fn, ws, inc, inc_by, inc_each))

    def at(self, semname):
        return (semname, self.cnt[semname])

    def mark(self, key, semname):
        self.marks[key] = (semname, self.cnt[semname])

    def m(self, key):
        return self.marks.get(key)


def xcol(c, t):
    if t < 1024:
        return 1024 * c + t
    return 16384 + 1024 * c + (t - 1024)


def build_program():
    stage = DBG.get("stage", 99)
    taps = set(DBG.get("taps", []))

    nc = bass.Bass("TRN2", target_bir_lowering=False,
                   dynamic_dma_scratch_size=16384)

    def din(name, shape, dt=F32):
        return nc.dram_tensor(name, list(shape), dt, kind="ExternalInput").ap()

    x_own = din("x_own", (1024, D))
    x_peer = din("x_peer", (1024, D))
    wq = din("wq", (D, D)); wk = din("wk", (D, D))
    wv = din("wv", (D, D)); wo = din("wo", (D, D))
    gamma = din("gamma", (1, D)); beta = din("beta", (1, D))
    bases_i = din("bases", (64, DSL * GRID))
    slopes_i = din("slopes", (64, DSL * GRID))
    oscale_i = din("oscale", (1, 64))
    pm_i = din("pm_scaled", (1, 8))

    y = nc.dram_tensor("y", [1024, D], F32, kind="ExternalOutput").ap()
    cc_in = nc.dram_tensor("cc_in", [1, 128], F32).ap()
    cc_out = nc.dram_tensor("cc_out", [1, 128], F32).ap()
    zscr = nc.dram_tensor("zscr", [1, 1024], F32).ap()
    kpark = nc.dram_tensor("kpark", [P, H * 1024], BF16).ap()
    xnstash = nc.dram_tensor("xnstash", [8 * P, D], F32).ap()
    wkb = nc.dram_tensor("wkb", [D, D], BF16).ap()
    wvb = nc.dram_tensor("wvb", [D, D], BF16).ap()
    wqb = nc.dram_tensor("wqb", [D, D], BF16).ap()
    slscr = nc.dram_tensor("slscr", [1, D], F32).ap()

    dbg = {}
    def dtap(name, shape, dt=F32):
        if name not in dbg:
            dbg[name] = nc.dram_tensor("dbg_" + name, list(shape), dt,
                                       kind="ExternalOutput").ap()
        return dbg[name]

    def inl(arr, name):
        return nc.inline_tensor(np.ascontiguousarray(arr), name).ap()

    ident_f = inl(np.eye(P, dtype=np.float32), "identfc")
    ident_b = inl(np.eye(P, dtype=np.float32).astype(ml_dtypes.bfloat16), "identbc")
    iota64c = inl(np.broadcast_to(np.arange(64, dtype=np.float32), (P, 64)), "iota64c")
    iota64mc = inl(np.broadcast_to(np.arange(64, dtype=np.float32) + BIGI, (P, 64)), "iota64mc")
    iota8mc = inl(np.broadcast_to(np.arange(8, dtype=np.float32) + BIGI, (P, 8)), "iota8mc")
    iota8rc = inl(np.broadcast_to(np.arange(8, dtype=np.float32), (P, 8)), "iota8rc")
    clurowc = inl(np.broadcast_to(np.arange(64, dtype=np.float32) // 8, (P, 64)), "clurowc")
    cmp8c = inl(np.broadcast_to(8.0 * np.arange(1, 8, dtype=np.float32), (P, 7)), "cmp8c")
    gthrc = inl(np.broadcast_to(np.tile(np.arange(1, GRID + 1, dtype=np.float32), DSL), (P, DSL * GRID)), "gthrc")
    gidxc = inl(np.broadcast_to(np.tile(np.arange(GRID, dtype=np.float32), DSL), (P, DSL * GRID)), "gidxc")
    ccorrc = inl(np.broadcast_to(np.asarray(C_CORR, dtype=np.float32), (P, 8)).copy(), "ccorrc")
    corrpat = inl(np.broadcast_to(
        np.repeat(np.asarray([c / 128.0 for c in C_CORR], dtype=np.float32), 128),
        (P, 1024)).astype(ml_dtypes.bfloat16).copy(), "corrpat")
    onesbc = inl(np.ones((P, 1), dtype=ml_dtypes.bfloat16), "onesbc")
    _st = np.arange(P)[:, None]; _tt = np.arange(128)[None, :]
    trilbc = inl((_tt >= _st).astype(ml_dtypes.bfloat16), "trilbc")
    triu1c = inl((_tt < _st).astype(ml_dtypes.bfloat16), "triu1c")

    SB0 = (nc.sbuf_base + 31) & ~31
    cap = nc.sbuf_top - SB0

    def sb(name, off, shape, dt):
        nbytes = int(np.prod(shape[1:])) * mybir.dt.size(dt)
        assert off % 32 == 0 and off + nbytes <= cap, (name, off, nbytes, cap)
        return nc.alloc_sbuf_tensor_at(name, list(shape), dt, offset=SB0 + off)

    xnT = sb("xnT", 0, [P, 32768], BF16)
    attT = sb("attT", 0, [P, H * 1024], BF16)
    QT = sb("QT", 32768, [P, H * 1024], BF16)
    p5_x = [sb("p5_x0", 32768, [P, D], F32), sb("p5_x1", 40960, [P, D], F32)]
    p5_o = [sb("p5_o0", 49152, [P, D], F32), sb("p5_o1", 57344, [P, D], F32)]
    KT = sb("KT", 65536, [P, H * 2048], BF16)
    WoB = sb("WoB", 65536, [P, NC * 2048], BF16)
    p0_x = [sb("p0_x0", 65536, [P, D], F32), sb("p0_x1", 73728, [P, D], F32)]
    p0_xc = [sb("p0_xc0", 81920, [P, D], F32), sb("p0_xc1", 90112, [P, D], F32)]
    p0_xn = [sb("p0_xn0", 98304, [P, D], F32), sb("p0_xn1", 106496, [P, D], F32)]
    gam_rep = sb("gam_rep", 114688, [P, D], F32)
    bet_rep = sb("bet_rep", 122880, [P, D], F32)
    V = sb("V", 131072, [P, NT * 2048], BF16)
    slope_rep = sb("slope_rep", 131072, [P, D], F32)
    f1_xf = sb("f1_xf", 139264, [P, D], F32)
    f1_ts = sb("f1_ts", 147456, [P, 64], F32)
    f1_sc1 = sb("f1_sc1", 147712, [P, 64], F32)
    f1_cs = sb("f1_cs", 147968, [P, 8], F32)
    f1_t8 = sb("f1_t8", 148000, [P, 8], F32)
    f1_bc = sb("f1_bc", 148032, [P, 1], F32)
    f1_e64 = sb("f1_e64", 148064, [P, 64], F32)
    f1_e64b = sb("f1_e64b", 148320, [P, 64], F32)
    f1_g2 = sb("f1_g2", 148576, [P, 256], F32)
    f1_mm = sb("f1_mm", 149600, [P, 128], F32)
    f1_mme = sb("f1_mme", 150112, [P, 128], F32)
    f1_acc = sb("f1_acc", 150624, [P, 1], F32)
    f1_sgn = sb("f1_sgn", 150656, [P, D], F32)
    f1_t7 = sb("f1_t7", 158848, [P, 7], F32)
    kstg = [sb("kstg0", 158880, [P, 1024], BF16), sb("kstg1", 160928, [P, 1024], BF16)]
    f2_bsel = sb("f2_bsel", 131072, [P, 512], F32)
    f2_ssel = sb("f2_ssel", 133120, [P, 512], F32)
    f2_eq16 = sb("f2_eq16", 135168, [P, 512], F32)
    f2_t512 = sb("f2_t512", 137216, [P, 512], F32)
    f2_mn = sb("f2_mn", 139264, [P, 3], F32)
    f2_u = sb("f2_u", 139296, [P, DSL], F32)
    f2_w = sb("f2_w", 139424, [P, DSL], F32)
    f2_gi = sb("f2_gi", 139552, [P, DSL], F32)
    f2_xl = sb("f2_xl", 139680, [P, DSL], F32)
    f2_den = sb("f2_den", 139808, [P, 1], F32)
    f2_ohT = sb("f2_ohT", 139840, [64, 128], F32)
    f2_e64 = sb("f2_e64", 140352, [P, 64], F32)
    f2_ohT8 = sb("f2_ohT8", 163840, [64, 1024], F32)
    s_mn8 = sb("s_mn8", 167936, [P, 24], F32)
    p5_xc = sb("p5_xc", 140864, [P, D], F32)
    p5_xn = sb("p5_xn", 149056, [P, D], F32)
    p5_xn2 = [sb("p5_xn2a", 140864, [P, D], F32), sb("p5_xn2b", 149056, [P, D], F32)]
    doff = [196608]
    def sbd(name, shape, dt):
        off = doff[0]
        h = sb(name, off, shape, dt)
        doff[0] = (off + int(np.prod(shape[1:])) * mybir.dt.size(dt) + 31) & ~31
        return h

    c_ident = sbd("c_ident", [P, 128], F32)
    c_identb = sbd("c_identb", [P, 128], BF16)
    c_iota64 = sbd("c_iota64", [P, 64], F32)
    c_iota64m = sbd("c_iota64m", [P, 64], F32)
    c_clu = sbd("c_clu", [P, 64], F32)
    c_iota8m = sbd("c_iota8m", [P, 8], F32)
    c_iota8r = sbd("c_iota8r", [P, 8], F32)
    c_cmp8 = sbd("c_cmp8", [P, 7], F32)
    c_onesb = sbd("c_onesb", [P, 1], BF16)
    c_trilb = sbd("c_trilb", [P, 128], BF16)
    c_triu1 = sbd("c_triu1", [P, 128], BF16)
    s_pm = sbd("s_pm", [P, 8], F32)
    s_slm = sb("s_slm", 147456, [64, 32], F32)
    s_tab = sb("s_tab", 157248, [64, 1027], F32)
    c_gthr = sb("c_gthr", 161408, [P, 512], F32)
    s_tidx = sbd("s_tidx", [P, 8], F32)
    s_xsl = sbd("s_xsl", [P, 8 * DSL], F32)
    s_delta = sbd("s_delta", [P, 8 * DSL], F32)
    s_eqc = sbd("s_eqc", [P, 64], F32)
    s_eqm = sbd("s_eqm", [P, 64], F32)
    s_vsuf = sbd("s_vsuf", [P, H * 8], F32)
    s_tsum = sbd("s_tsum", [P, 16], F32)
    s_vt = sbd("s_vt", [P, 1024], BF16)
    c_ccorr = sbd("c_ccorr", [P, 8], F32)
    c_sum4 = sbd("c_sum4", [P, 4], F32)
    c_var4 = sbd("c_var4", [P, 4], F32)
    c_a4 = sbd("c_a4", [P, 4], F32)
    c_lt4 = sbd("c_lt4", [P, 4], F32)
    c_y04 = sbd("c_y04", [P, 4], F32)
    c_mu4 = sbd("c_mu4", [P, 4], F32)
    c_vt14 = sbd("c_vt14", [P, 4], F32)
    c_vt24 = sbd("c_vt24", [P, 4], F32)
    c_ft1 = sbd("c_ft1", [P, 1], F32)
    c_ft2 = sbd("c_ft2", [P, 1], F32)
    c_mn = sbd("c_mn", [P, 1], F32)
    c_mx = sbd("c_mx", [P, 1], F32)
    c_m1 = sbd("c_m1", [P, 1], F32)
    c_fe = sbd("c_fe", [P, 1], F32)
    WB0 = doff[0]
    wbuf = [sbd("wbuf0", [P, D], BF16), sbd("wbuf1", [P, D], BF16)]
    rz_rep = sb("rz_rep", WB0, [P, 1024], F32)
    pt = [sb("pt0", WB0 + 4096, [P, 1024], BF16),
          sb("pt1", WB0 + 6144, [P, 1024], BF16)]
    pt3 = [pt[0], pt[1], s_vt]
    assert doff[0] <= cap, ("D overflow", doff[0], cap)

    pw01 = nc.alloc_psum_tensor("pw01", [P, 1024], F32)
    pw23 = nc.alloc_psum_tensor("pw23", [P, 1024], F32)
    p = [pw01[:, 0:512], pw01[:, 512:1024], pw23[:, 0:512], pw23[:, 512:1024]] + \
        [nc.alloc_psum_tensor("pb%d" % i, [P, 512], F32) for i in range(4, 8)]

    def pbank_bf(i):
        return p[i][:, 0:64].bitcast(BF16)[:, 0:128]

    pl = Plan()

    const_list = [
        (c_ident[:, :], ident_f), (c_identb[:, :], ident_b),
        (c_iota64[:, :], iota64c), (c_iota64m[:, :], iota64mc),
        (c_clu[:, :], clurowc), (c_iota8m[:, :], iota8mc),
        (c_iota8r[:, :], iota8rc), (c_cmp8[:, :], cmp8c),
        (c_onesb[:, :], onesbc), (c_ccorr[:, :], ccorrc),
        (c_trilb[:, :], trilbc), (c_triu1[:, :], triu1c),
        (f1_xf[0:64, 0:512], slopes_i),
    ]
    pl.op("sync", lambda e: [e.dma_start(out=d, in_=s) for d, s in const_list],
          inc="qc", inc_by=16 * len(const_list), inc_each=True)
    pl.mark("consts", "qc")

    bcast_list = [
        (gam_rep[:, :], gamma.to_broadcast([P, D])),
        (bet_rep[:, :], beta.to_broadcast([P, D])),
        (s_pm[:, :], pm_i.to_broadcast([P, 8])),
    ]
    pl.op("gpsimd", lambda e: [e.dma_start(out=d, in_=s) for d, s in bcast_list],
          inc="qb", inc_by=16 * len(bcast_list), inc_each=True)
    pl.mark("bcast", "qb")

    def f_slmean(v):
        v.tensor_reduce(s_slm[:, :].rearrange("p (a b) -> p a b", b=1),
                        f1_xf[0:64, 0:512].rearrange("p (a b) -> p a b", b=GRID),
                        AX.X, ALU.add)
        v.drain()
        return v.tensor_scalar(s_slm[:, :], s_slm[:, :], 1.0 / GRID, None, ALU.mult)
    pl.op("vector", f_slmean, waits=[pl.m("consts")], inc="v")
    pl.mark("slmean", "v")

    pl.op("scalar", lambda s: s.activation(s_slm[:, :], s_slm[:, :], AF.Sign),
          waits=[pl.m("slmean")], inc="s")
    pl.mark("slsign", "s")

    pl.op("gpsimd", lambda g: g.dma_start(out=slscr[0:1, :], in_=s_slm[:, :]),
          waits=[pl.m("slsign")], inc="qb", inc_by=16)
    pl.mark("slflat", "qb")
    pl.op("gpsimd", lambda g: g.dma_start(out=slope_rep[:, :],
                                          in_=slscr[0:1, :].to_broadcast([P, D])),
          waits=[pl.m("slflat")], inc="qb", inc_by=16)
    pl.mark("slrep", "qb")

    def emit_ln(tag, xbuf, xcbuf, xnbuf, dump, ss, w_v1, w_sq=(), eng_inc=None):
        def f_v1(v, xbuf=xbuf, xcbuf=xcbuf, ss=ss):
            v.reduce_sum(c_sum4[:, ss:ss + 1], xbuf[:, :], AX.X)
            v.drain()
            v.tensor_scalar(c_mu4[:, ss:ss + 1], c_sum4[:, ss:ss + 1], 1.0 / D, None, ALU.mult)
            v.drain()
            return v.tensor_tensor(xcbuf[:, :], xbuf[:, :],
                                   c_mu4[:, ss:ss + 1].to_broadcast([P, D]), ALU.subtract)
        pl.op("vector", f_v1, waits=w_v1, inc="v")
        pl.mark(tag + "xc", "v")

        pl.op("scalar", lambda s, dump=dump, xcbuf=xcbuf: s.square(dump[:, :], xcbuf[:, :]),
              waits=[pl.m(tag + "xc")] + list(w_sq), inc="s")
        pl.mark(tag + "sq", "s")

        def f_v2(v, dump=dump, ss=ss):
            v.reduce_sum(c_sum4[:, ss:ss + 1], dump[:, :], AX.X)
            v.drain()
            return v.tensor_scalar(c_a4[:, ss:ss + 1], c_sum4[:, ss:ss + 1],
                                   1.0 / D, LN_EPS, ALU.mult, ALU.add)
        pl.op("vector", f_v2, waits=[pl.m(tag + "sq")], inc="v")
        pl.mark(tag + "st", "v")

        def f_s2(s, ss=ss):
            s.activation(c_lt4[:, ss:ss + 1], c_a4[:, ss:ss + 1], AF.Ln)
            s.drain()
            return s.activation(c_y04[:, ss:ss + 1], c_lt4[:, ss:ss + 1], AF.Exp, scale=-0.5)
        pl.op("scalar", f_s2, waits=[pl.m(tag + "st")], inc="s")
        pl.mark(tag + "md", "s")

        def f_v3(v, ss=ss):
            y0 = c_y04[:, ss:ss + 1]
            v.scalar_tensor_tensor(c_vt14[:, ss:ss + 1], y0, c_a4[:, ss:ss + 1], y0,
                                   ALU.mult, ALU.mult)
            v.drain()
            v.tensor_scalar(c_vt24[:, ss:ss + 1], c_vt14[:, ss:ss + 1],
                            -0.5, 1.5, ALU.mult, ALU.add)
            v.drain()
            return v.tensor_tensor(y0, y0, c_vt24[:, ss:ss + 1], ALU.mult)
        pl.op("vector", f_v3, waits=[pl.m(tag + "md")], inc="v")
        pl.mark(tag + "nw", "v")

        pl.op("scalar", lambda s, xnbuf=xnbuf, xcbuf=xcbuf, ss=ss:
              s.activation(xnbuf[:, :], xcbuf[:, :], AF.Identity,
                           scale=c_y04[:, ss:ss + 1]),
              waits=[pl.m(tag + "nw")], inc="s")
        pl.mark(tag + "ap", "s")

    ctr = {"sl": 0}

    WCONV = {id(wk): 0, id(wv): 1, id(wq): 2}
    WBF = {id(wk): None, id(wv): None, id(wq): None}

    def f_wl_mk(w, h, wb):
        wb16 = {0: wkb, 1: wvb, 2: wqb}[WCONV[id(w)]]
        def f_wl(gp):
            srcw = wb16.rearrange("(c p) d -> p c d", p=P)[:, :, 128 * h:128 * h + 128]
            dst = wb[:, :].rearrange("p (c d) -> p c d", d=128)
            return [gp.dma_start(out=dst[:, 4 * q:4 * q + 4, :],
                                 in_=srcw[:, 4 * q:4 * q + 4, :]) for q in range(4)]
        return f_wl

    def emit_kpeer(h):
        sl = ctr["sl"]; ctr["sl"] += 1
        wb = wbuf[sl % 2]
        pl.op("sync", f_wl_mk(wk, h, wb), waits=[pl.m("slcd%d" % (sl - 2)),
                                                  pl.m("wconv0")],
              inc="qw", inc_by=64, inc_each=True)
        pl.mark("wl%d" % sl, "qw")
        for half in range(2):
            q = 2 + half
            bank = 2 + half

            def f_mm(t, wb=wb, q=q, bank=bank):
                r = None
                for c in range(NC):
                    r = t.matmul(p[bank][:, :], wb[:, 128 * c:128 * c + 128],
                                 xnT[:, xcol(c, 512 * q):xcol(c, 512 * q) + 512],
                                 start=(c == 0), stop=(c == NC - 1))
                return r
            w_mm = [pl.m("wl%d" % sl), pl.m("xnT15")]
            if half == 0:
                w_mm += [pl.m("f1ev%d" % (h // 2)), pl.m("kpev%d_0" % (h - 1))]
            else:
                w_mm.append(pl.m("kpev%d_1" % (h - 1)))
            pl.op("tensor", f_mm, waits=w_mm, inc="t")
            pl.mark("kpmm%d_%d" % (h, half), "t")
            if half == 1:
                pl.mark("slcd%d" % sl, "t")
        for half in range(2):
            def f_ev(x, h=h, half=half):
                return x.copy(kstg[h % 2][:, 512 * half:512 * half + 512],
                              p[2 + half][:, :])
            we = [pl.m("kpmm%d_%d" % (h, half))]
            if half == 0:
                we.append(pl.m("kpark%d" % (h - 2)))
            pl.op("scalar", f_ev, waits=we, inc="s")
            pl.mark("kpev%d_%d" % (h, half), "s")

        pl.op("gpsimd", lambda g, h=h: g.dma_start(
            out=kpark[:, 1024 * h:1024 * h + 1024], in_=kstg[h % 2][:, :]),
            waits=[pl.m("kpev%d_1" % h)], inc="qk", inc_by=16)
        pl.mark("kpark%d" % h, "qk")

    def emit_vpeer(h):
        sl = ctr["sl"]; ctr["sl"] += 1
        wb = wbuf[sl % 2]
        pl.op("sync", f_wl_mk(wv, h, wb), waits=[pl.m("slcd%d" % (sl - 2)),
                                                  pl.m("wconv1")],
              inc="qw", inc_by=64, inc_each=True)
        pl.mark("wl%d" % sl, "qw")

        def f_mm(t, wb=wb):
            r = None
            for c in range(NC):
                for q in (2, 3):
                    r = t.matmul(p[4 + q][:, :], wb[:, 128 * c:128 * c + 128],
                                 xnT[:, xcol(c, 512 * q):xcol(c, 512 * q) + 512],
                                 start=(c == 0), stop=(c == NC - 1))
            return r
        pl.op("tensor", f_mm, waits=[pl.m("wl%d" % sl), pl.m("xnT15"),
                                     pl.m("vpsv%d" % (h - 1))], inc="t")
        pl.mark("vpmm%d" % h, "t")

        def f_sv(x):
            x.copy(s_vt[:, 0:512], p[6][:, :])
            return x.copy(s_vt[:, 512:1024], p[7][:, :])
        pl.op("scalar", f_sv, waits=[pl.m("vpmm%d" % h), pl.m("vptr%d_7" % (h - 1))],
              inc="s")
        pl.mark("vpsv%d" % h, "s")

        tb = 4 + h % 2
        def f_vtr(t, tb=tb):
            r = None
            for tt in range(8):
                r = t.transpose(p[tb][:, 64 * tt:64 * tt + 64].bitcast(BF16)[:, 0:128],
                                s_vt[:, 128 * tt:128 * tt + 128], c_identb[:, :])
            return r
        pl.op("tensor", f_vtr, waits=[pl.m("vpsv%d" % h), pl.m("vpevall%d" % (h - 2))],
              inc="t")
        pl.mark("vptr%d_7" % h, "t")
        pl.mark("slcd%d" % sl, "t")

        def f_vev(x, h=h, tb=tb):
            r = None
            for tt in range(8):
                r = x.copy(V[:, 2048 * (8 + tt) + 128 * h:2048 * (8 + tt) + 128 * h + 128],
                           p[tb][:, 64 * tt:64 * tt + 64].bitcast(BF16)[:, 0:128])
            return r
        pl.op("scalar", f_vev, waits=[pl.m("vptr%d_7" % h)], inc="s")
        pl.mark("vpevall%d" % h, "s")
        for tt in range(8):
            pl.mark("vev_%d_%d" % (8 + tt, h), "s")

    def emit_f1acc(ef):
        pl.op("tensor", lambda t: t.transpose(p[2][:, 0:128], f1_mm[:, :], c_ident[:, :]),
              waits=[pl.m("mm%d" % ef), pl.m("f1ev%d" % (ef - 1)),
                     pl.m("kpev%d_0" % (2 * ef - 1))], inc="t")
        pl.mark("f1tr%d" % ef, "t")

        def f_f1ev(v, ef=ef):
            v.tensor_copy(f1_mme[:, :], p[2][:, 0:128])
            v.drain()
            if ef == 0:
                v.tensor_reduce(f1_acc[0:64, :], f1_mme[0:64, :], AX.X, ALU.min)
                r = v.tensor_reduce(f1_acc[64:128, :], f1_mme[64:128, :], AX.X, ALU.max)
            else:
                v.tensor_reduce(c_fe[0:64, :], f1_mme[0:64, :], AX.X, ALU.min)
                v.tensor_reduce(c_fe[64:128, :], f1_mme[64:128, :], AX.X, ALU.max)
                v.drain()
                v.tensor_tensor(f1_acc[0:64, :], f1_acc[0:64, :], c_fe[0:64, :], ALU.min)
                r = v.tensor_tensor(f1_acc[64:128, :], f1_acc[64:128, :], c_fe[64:128, :], ALU.max)
            if ef == 7:
                v.drain()
                r = v.tensor_scalar(f1_acc[64:128, :], f1_acc[64:128, :], -1.0, None, ALU.mult)
            return r
        pl.op("vector", f_f1ev, waits=[pl.m("f1tr%d" % ef)], inc="v")
        pl.mark("f1ev%d" % ef, "v")

    def emit_group(eg):
        emit_kpeer(2 * eg)
        if not DBG.get("nov"):
            emit_vpeer(2 * eg)
        emit_kpeer(2 * eg + 1)
        if not DBG.get("nov"):
            emit_vpeer(2 * eg + 1)

    def emit_wconv_chunk(wsrc, wdst, j, wi_, wait_mark):
        pl.op("gpsimd", lambda g, wsrc=wsrc, wdst=wdst, j=j: g.dma_start(
            out=wdst[512 * j:512 * j + 512, :], in_=wsrc[512 * j:512 * j + 512, :]),
            waits=[pl.m(wait_mark)], inc="qwc", inc_by=16)
        if j == 3:
            pl.mark("wconv%d" % wi_, "qwc")

    P0_SEQ = list(range(8, NT)) + list(range(8))
    for pos, e in enumerate(P0_SEQ):
        p1e = P0_SEQ[pos - 1] if pos >= 1 else None
        p2e = P0_SEQ[pos - 2] if pos >= 2 else None
        par = pos % 2
        xb, xc, xn = p0_x[par], p0_xc[par], p0_xn[par]
        src = x_own if e < 8 else x_peer
        r0 = (e % 8) * 128

        pl.op("sync", lambda sy, src=src, r0=r0, xb=xb:
              sy.dma_start(out=xb[:, :], in_=src[r0:r0 + 128, :]),
              waits=[pl.m("xfree%s" % p2e), pl.m("xn%s" % p2e)],
              inc="qx", inc_by=16)
        pl.mark("xload%d" % e, "qx")

        if stage >= 2:
            if e in (8, 10, 12, 14):
                emit_wconv_chunk(wk, wkb, (e - 8) // 2, 0, "xload%d" % e)
            elif e in (9, 11, 13, 15):
                emit_wconv_chunk(wv, wvb, (e - 9) // 2, 1, "xload%d" % e)
            elif e in (0, 1, 2, 3):
                emit_wconv_chunk(wq, wqb, e, 2, "xload%d" % e)

        emit_ln("l1_%d_" % e, xb, xc, xn, xb,
                par,
                w_v1=[pl.m("xload%d" % e), pl.m("slrep"), pl.m("bcast"),
                      pl.m("xn%s" % p2e), pl.m("f1ap%s" % p2e),
                      pl.m("tr_%s_%d" % (p2e, NC - 1)), pl.m("xnst%s" % p2e)])
        pl.marks["xfree%d" % e] = pl.m("l1_%d_st" % e)
        pl.marks["xn%d" % e] = pl.m("l1_%d_ap" % e)

        if e < 8:
            pl.op("sync", lambda sy, xn=xn, e=e: sy.dma_start(
                out=xnstash[128 * e:128 * e + 128, :], in_=xn[:, :]),
                waits=[pl.m("xn%d" % e)], inc="qx", inc_by=16)
            pl.mark("xnst%d" % e, "qx")

        if e < 8:
            if e >= 1:
                emit_f1acc(e - 1)

            emit_ln("l2_%d_" % e, xn, xc, f1_xf, f1_sgn,
                    2,
                    w_v1=[pl.m("xn%d" % e), pl.m("f1ap%d" % (e - 1))],
                    w_sq=[pl.m("mm%d" % (e - 1))])
            pl.marks["f1ap%d" % e] = pl.m("l2_%d_ap" % e)

            def f_xf(v):
                v.tensor_tensor(f1_xf[:, :], f1_xf[:, :], gam_rep[:, :], ALU.mult)
                return v.tensor_tensor(f1_xf[:, :], f1_xf[:, :], bet_rep[:, :], ALU.add)
            pl.op("vector", f_xf, waits=[pl.m("f1ap%d" % e)], inc="v")
            pl.mark("xf%d" % e, "v")

            pl.op("scalar", lambda s: s.activation(f1_sgn[:, :], f1_xf[:, :], AF.Sign),
                  waits=[pl.m("xf%d" % e)], inc="s")
            pl.mark("sgn%d" % e, "s")

            def f_route(v, e=e):
                v.tensor_tensor(f1_sgn[:, :], f1_sgn[:, :], slope_rep[:, :], ALU.mult)
                v.tensor_reduce(f1_ts[:, :].rearrange("p (a b) -> p a b", b=1),
                                f1_sgn[:, :].rearrange("p (a b) -> p a b", b=32),
                                AX.X, ALU.add)
                v.drain()
                v.tensor_reduce(f1_cs[:, :].rearrange("p (a b) -> p a b", b=1),
                                f1_ts[:, :].rearrange("p (a b) -> p a b", b=8),
                                AX.X, ALU.add)
                v.drain()
                v.reduce_max(c_m1[:, :], f1_cs[:, :], AX.X)
                v.drain()
                v.tensor_tensor(f1_t8[:, :], f1_cs[:, :], c_m1[:, 0:1].to_broadcast([P, 8]), ALU.is_equal)
                v.drain()
                v.scalar_tensor_tensor(f1_t8[:, :], f1_t8[:, :], -BIGI,
                                       c_iota8m[:, :], ALU.mult, ALU.add)
                v.drain()
                v.tensor_reduce(f1_bc[:, :], f1_t8[:, :], AX.X, ALU.min)
                v.drain()
                v.tensor_tensor(f1_e64[:, :], c_clu[:, :], f1_bc[:, 0:1].to_broadcast([P, 64]), ALU.is_equal)
                v.drain()
                v.scalar_tensor_tensor(f1_sc1[:, :], f1_ts[:, :], BIGM,
                                       f1_e64[:, :], ALU.add, ALU.mult)
                v.drain()
                v.reduce_max(c_m1[:, :], f1_sc1[:, :], AX.X)
                v.drain()
                v.tensor_tensor(f1_e64b[:, :], f1_sc1[:, :], c_m1[:, 0:1].to_broadcast([P, 64]), ALU.is_equal)
                v.drain()
                v.scalar_tensor_tensor(f1_e64b[:, :], f1_e64b[:, :], -BIGI,
                                       c_iota64m[:, :], ALU.mult, ALU.add)
                v.drain()
                v.tensor_reduce(s_tidx[:, e:e + 1], f1_e64b[:, :], AX.X, ALU.min)
                v.drain()
                v.tensor_tensor(f1_e64[:, :], c_iota64[:, :], s_tidx[:, e:e + 1].to_broadcast([P, 64]), ALU.is_equal)
                v.tensor_tensor(f1_t7[:, :], s_tidx[:, e:e + 1].to_broadcast([P, 7]), c_cmp8[:, :], ALU.is_ge)
                v.drain()
                v.reduce_sum(c_ft1[:, :], f1_t7[:, :], AX.X)
                v.drain()
                v.scalar_tensor_tensor(c_ft2[:, :], c_ft1[:, :], -8.0, s_tidx[:, e:e + 1], ALU.mult, ALU.add)
                v.drain()
                v.tensor_tensor(s_eqc[:, 8 * e:8 * e + 8], c_iota8r[:, :], c_ft1[:, 0:1].to_broadcast([P, 8]), ALU.is_equal)
                return v.tensor_tensor(s_eqm[:, 8 * e:8 * e + 8], c_iota8r[:, :], c_ft2[:, 0:1].to_broadcast([P, 8]), ALU.is_equal)
            pl.op("vector", f_route, waits=[pl.m("sgn%d" % e)], inc="v")
            pl.mark("route%d" % e, "v")

            def f_gather(v, e=e):
                for c8 in range(8):
                    if c8 == 0:
                        v.tensor_scalar(f1_g2[:, :], f1_xf[:, 0:256], s_eqc[:, 8 * e:8 * e + 1], None, ALU.mult)
                    else:
                        v.scalar_tensor_tensor(f1_g2[:, :], f1_xf[:, 256 * c8:256 * c8 + 256],
                                               s_eqc[:, 8 * e + c8:8 * e + c8 + 1], f1_g2[:, :],
                                               ALU.mult, ALU.add)
                v.drain()
                xsl = s_xsl[:, DSL * e:DSL * e + DSL]
                for m in range(8):
                    v.tensor_scalar(f1_sgn[:, DSL * m:DSL * m + DSL],
                                    f1_g2[:, DSL * m:DSL * m + DSL],
                                    s_eqm[:, 8 * e + m:8 * e + m + 1], None, ALU.mult)
                v.drain()
                v.tensor_reduce(xsl.rearrange("p (d a) -> p d a", a=1),
                                f1_sgn[:, 0:256].rearrange("p (m d) -> p d m", d=DSL),
                                AX.X, ALU.add)
                v.drain()
                v.tensor_reduce(c_mn[:, :], xsl, AX.X, ALU.min)
                v.tensor_reduce(c_mx[:, :], xsl, AX.X, ALU.max)
                v.drain()
                v.tensor_tensor(f1_mm[:, 0:64], f1_e64[:, :], c_mn[:, 0:1].to_broadcast([P, 64]), ALU.mult)
                v.tensor_scalar(f1_e64b[:, :], f1_e64[:, :], -BIGF, BIGF, ALU.mult, ALU.add)
                v.drain()
                v.tensor_tensor(f1_mm[:, 0:64], f1_mm[:, 0:64], f1_e64b[:, :], ALU.add)
                v.tensor_tensor(f1_mm[:, 64:128], f1_e64[:, :], c_mx[:, 0:1].to_broadcast([P, 64]), ALU.mult)
                v.drain()
                v.tensor_scalar(f1_e64b[:, :], f1_e64[:, :], BIGF, -BIGF, ALU.mult, ALU.add)
                v.drain()
                return v.tensor_tensor(f1_mm[:, 64:128], f1_mm[:, 64:128], f1_e64b[:, :], ALU.add)
            pl.op("vector", f_gather, waits=[pl.m("route%d" % e),
                                             pl.m("f1tr%d" % (e - 1))], inc="v")
            pl.mark("mm%d" % e, "v")

            if e >= 1 and stage >= 2:
                emit_group(e - 1)

        for c in range(NC):
            bank = c % 2
            prev = pl.m("trev_%d_%d" % (e, c - 2)) if c >= 2 else \
                pl.m("trev_%s_%d" % (p1e, NC - 2 + c))
            pl.op("tensor", lambda t, xn=xn, c=c, bank=bank:
                  t.transpose(p[bank][:, 0:128], xn[:, 128 * c:128 * c + 128], c_ident[:, :]),
                  waits=[pl.m("xn%d" % e), pl.m("consts"), prev], inc="t")
            pl.mark("tr_%d_%d" % (e, c), "t")

            pl.op("scalar", lambda s, c=c, e=e, bank=bank:
                  s.copy(xnT[:, xcol(c, 128 * e):xcol(c, 128 * e) + 128], p[bank][:, 0:128]),
                  waits=[pl.m("tr_%d_%d" % (e, c))], inc="s")
            pl.mark("trev_%d_%d" % (e, c), "s")
        pl.mark("xnT%d" % e, "s")

    emit_f1acc(7)
    if stage >= 2:
        emit_group(7)

    def f_ccstage(g):
        return [g.dma_start(out=cc_in[0:1, 0:64], in_=f1_acc[0:64, :]),
                g.dma_start(out=cc_in[0:1, 64:128], in_=f1_acc[64:128, :])]
    pl.op("gpsimd", f_ccstage, waits=[pl.m("f1ev7")], inc="qs", inc_by=32, inc_each=True)
    pl.mark("ccstage", "qs")

    pl.op("gpsimd", lambda g: g.collective_compute(
        "AllReduce", ALU.min, replica_groups=[list(range(8))],
        ins=[cc_in.opt()], outs=[cc_out.opt()]),
        waits=[pl.m("ccstage")], inc="g")
    pl.mark("ccdone", "g")



    if stage >= 2:
        pair_last = [None, "kpev15_1"]
        pi = [0]

        def own_mm(wb, sl, qs=(0, 1)):
            b0 = 2 * (pi[0] % 2)
            def f_mm(t, wb=wb, b0=b0):
                r = None
                for c in range(NC):
                    for j, q in enumerate(qs):
                        r = t.matmul(p[b0 + j][:, :], wb[:, 128 * c:128 * c + 128],
                                     xnT[:, xcol(c, 512 * q):xcol(c, 512 * q) + 512],
                                     start=(c == 0), stop=(c == NC - 1))
                return r
            w_mm = [pl.m("wl%d" % sl), pl.m("xnT7"),
                    pl.m(pair_last[pi[0] % 2]) if pair_last[pi[0] % 2] else None]
            pl.op("tensor", f_mm, waits=w_mm, inc="t")
            pl.mark("slcd%d" % sl, "t")
            return b0

        for h in range(H):
            sl = ctr["sl"]; ctr["sl"] += 1
            pl.op("sync", f_wl_mk(wk, h, wbuf[sl % 2]),
                  waits=[pl.m("slcd%d" % (sl - 2))], inc="qw", inc_by=64, inc_each=True)
            pl.mark("wl%d" % sl, "qw")

            w_unp = [pl.m("kpark%d" % h), pl.m("xnT7"), pl.m("xf7")]
            if h == 0:
                w_unp.append(pl.m("ccstage"))
            pl.op("sync", lambda sy, h=h: sy.dma_start(
                out=KT[:, 2048 * h + 1024:2048 * h + 2048],
                in_=kpark[:, 1024 * h:1024 * h + 1024]),
                waits=w_unp, inc="qu", inc_by=16)
            pl.mark("unp%d" % h, "qu")

            b0 = own_mm(wbuf[sl % 2], sl)
            pl.mark("komm%d" % h, "t")

            def f_ev(x, h=h, b0=b0):
                x.copy(KT[:, 2048 * h:2048 * h + 512], p[b0][:, :])
                return x.copy(KT[:, 2048 * h + 512:2048 * h + 1024], p[b0 + 1][:, :])
            we = [pl.m("komm%d" % h), pl.m("unp%d" % h)]
            if h == 0:
                we += [pl.m("xnT7"), pl.m("xf7"), pl.m("ccstage")]
            pl.op("scalar", f_ev, waits=we, inc="s")
            pl.mark("ktevf%d" % h, "s")
            pair_last[pi[0] % 2] = "ktevf%d" % h
            pi[0] += 1

        if not DBG.get("nov"):
            for h in range(H):
                sl = ctr["sl"]; ctr["sl"] += 1
                pl.op("sync", f_wl_mk(wv, h, wbuf[sl % 2]),
                      waits=[pl.m("slcd%d" % (sl - 2))], inc="qw", inc_by=64, inc_each=True)
                pl.mark("wl%d" % sl, "qw")

                b0 = own_mm(wbuf[sl % 2], sl)
                pl.mark("vomm%d" % h, "t")

                def f_sv(x, b0=b0):
                    x.copy(s_vt[:, 0:512], p[b0][:, :])
                    return x.copy(s_vt[:, 512:1024], p[b0 + 1][:, :])
                w_sv = [pl.m("vomm%d" % h),
                        pl.m("votr%d_7" % (h - 1)) if h else pl.m("vptr15_7")]
                if h == 0:
                    w_sv.append(pl.m("kpark15"))
                pl.op("scalar", f_sv, waits=w_sv, inc="s")
                pl.mark("vosv%d" % h, "s")
                pair_last[pi[0] % 2] = "vosv%d" % h
                pi[0] += 1

                tb = 4 + h % 2
                def f_vtr(t, tb=tb):
                    r = None
                    for tt in range(8):
                        r = t.transpose(p[tb][:, 64 * tt:64 * tt + 64].bitcast(BF16)[:, 0:128],
                                        s_vt[:, 128 * tt:128 * tt + 128], c_identb[:, :])
                    return r
                w_tr = [pl.m("vosv%d" % h),
                        pl.m("voevall%d" % (h - 2)) if h >= 2 else
                        pl.m("vpevall%d" % (14 + h % 2))]
                pl.op("tensor", f_vtr, waits=w_tr, inc="t")
                pl.mark("votr%d_7" % h, "t")

                def f_vev(x, h=h, tb=tb):
                    r = None
                    for tt in range(8):
                        r = x.copy(V[:, 2048 * tt + 128 * h:2048 * tt + 128 * h + 128],
                                   p[tb][:, 64 * tt:64 * tt + 64].bitcast(BF16)[:, 0:128])
                    return r
                pl.op("scalar", f_vev, waits=[pl.m("votr%d_7" % h)], inc="s")
                pl.mark("voevall%d" % h, "s")
                for tt in range(8):
                    pl.mark("vev_%d_%d" % (tt, h), "s")

                if not DBG.get("novsuf"):
                    def f_vs(t, h=h):
                        r = None
                        for e in range(NT):
                            r = t.matmul(p[6][:, e:e + 1],
                                         V[:, 2048 * e + 128 * h:2048 * e + 128 * h + 128],
                                         c_onesb[:, :], start=True, stop=True)
                        return r
                    waits = [pl.m("vev_%d_%d" % (e, h)) for e in range(NT)]
                    waits.append(pl.m("vsufev%d" % (h - 1)) if h else pl.m("vpsv15"))
                    pl.op("tensor", f_vs, waits=waits, inc="t")
                    pl.mark("vsufmm%d" % h, "t")

                    def f_vse(v, h=h):
                        v.tensor_copy(s_tsum[:, :], p[6][:, 0:16])
                        v.memset(s_vsuf[:, 8 * h:8 * h + 1], 0.0)
                        r = None
                        for k in range(1, 8):
                            v.drain()
                            v.tensor_tensor(s_vsuf[:, 8 * h + k:8 * h + k + 1],
                                            s_vsuf[:, 8 * h + k - 1:8 * h + k],
                                            s_tsum[:, k - 1:k], ALU.add)
                            v.drain()
                            r = v.tensor_tensor(s_vsuf[:, 8 * h + k:8 * h + k + 1],
                                                s_vsuf[:, 8 * h + k:8 * h + k + 1],
                                                s_tsum[:, 8 + k - 1:8 + k], ALU.add)
                        return r
                    pl.op("vector", f_vse, waits=[pl.m("vsufmm%d" % h)], inc="v")
                    pl.mark("vsufev%d" % h, "v")
            pl.mark("valldone", "s")
        pl.mark("vsufall", "v")

        if not DBG.get("noq"):
            for h in range(H):
                sl = ctr["sl"]; ctr["sl"] += 1
                pl.op("sync", f_wl_mk(wq, h, wbuf[sl % 2]),
                      waits=[pl.m("slcd%d" % (sl - 2)), pl.m("wconv2")],
                      inc="qw", inc_by=64, inc_each=True)
                pl.mark("wl%d" % sl, "qw")

                b0 = own_mm(wbuf[sl % 2], sl)
                pl.mark("qomm%d" % h, "t")

                def f_ev(x, h=h, b0=b0):
                    x.copy(QT[:, 1024 * h:1024 * h + 512], p[b0][:, :])
                    return x.copy(QT[:, 1024 * h + 512:1024 * h + 1024], p[b0 + 1][:, :])
                pl.op("scalar", f_ev, waits=[pl.m("qomm%d" % h)], inc="s")
                pl.mark("qev_%d" % h, "s")
                pair_last[pi[0] % 2] = "qev_%d" % h
                pi[0] += 1

    if stage >= 3:
        for h in range(H):
            def mk_sc(h, e):
                cnt = CNT[e]; n = 128 * cnt; sb_ = 2 * (e % 2)
                def f_sc(t, h=h, e=e, n=n, sb_=sb_):
                    r = t.matmul(p[sb_][:, 0:min(n, 512)],
                                 KT[:, 2048 * h + 128 * e:2048 * h + 128 * e + 128],
                                 QT[:, 1024 * h:1024 * h + min(n, 512)],
                                 start=True, stop=True)
                    if n > 512:
                        r = t.matmul(p[sb_ + 1][:, 0:n - 512],
                                     KT[:, 2048 * h + 128 * e:2048 * h + 128 * e + 128],
                                     QT[:, 1024 * h + 512:1024 * h + n],
                                     start=True, stop=True)
                    return r
                waits = [pl.m("exp_%d_%d" % (h, e - 2)) if e >= 2 else
                         pl.m("exp_%d_%d" % (h - 1, 14 + e)),
                         pl.m("ktevf%d" % h), pl.m("qev_%d" % h)]
                if h == 0 and e < 2:
                    waits += [pl.m("qev_15"), pl.m("valldone"), pl.m("vsufall")]
                pl.op("tensor", f_sc, waits=waits, inc="t")
                pl.mark("sc_%d_%d" % (h, e), "t")

            def mk_exp(h, e):
                cnt = CNT[e]; sb_ = 2 * (e % 2); ptb = pt3[e % 3]
                def f_exp(s, e=e, cnt=cnt, sb_=sb_, ptb=ptb):
                    n = 128 * cnt
                    pwide = pw01 if sb_ == 0 else pw23
                    if e < 8:
                        return s.activation(ptb[:, 0:n], pwide[:, 0:n],
                                            AF.Exp, scale=INV_SQRT_T)
                    dlo = 128 * (cnt - 1)
                    r = None
                    if dlo > 0:
                        r = s.activation(ptb[:, 0:dlo], pwide[:, 0:dlo],
                                         AF.Exp, scale=INV_SQRT_T)
                    r = s.activation(ptb[:, dlo:dlo + 128], pwide[:, dlo:dlo + 128],
                                     AF.Exp, scale=s_pm[:, e - 8:e - 7])
                    return r
                pl.op("scalar", f_exp,
                      waits=[pl.m("sc_%d_%d" % (h, e)),
                             pl.m("av_%d_%d" % (h, e - 3)) if e >= 3 else
                             pl.m("av_%d_%d" % (h - 1, 13 + e)),
                             pl.m("bcast")],
                      inc="s")
                pl.mark("exp_%d_%d" % (h, e), "s")
                if e < 8:
                    def f_fix(v, cnt=cnt, ptb=ptb):
                        dtile = ptb[:, 128 * (cnt - 1):128 * cnt]
                        v.tensor_tensor(dtile, dtile, c_trilb[:, :], ALU.mult)
                        return v.tensor_tensor(dtile, dtile, c_triu1[:, :], ALU.add)
                    pl.op("vector", f_fix, waits=[pl.m("exp_%d_%d" % (h, e))], inc="v")
                    pl.mark("fix_%d_%d" % (h, e), "v")

            def mk_av(h, e):
                cnt = CNT[e]; n = 128 * cnt; ptb = pt3[e % 3]
                def f_av(t, h=h, e=e, n=n, ptb=ptb):
                    vt = V[:, 2048 * e + 128 * h:2048 * e + 128 * h + 128]
                    nlo = min(n, 512)
                    t.matmul(p[4][:, 0:nlo], vt, ptb[:, 0:nlo], start=(e == 0), stop=(e == 15))
                    if n > 512:
                        t.matmul(p[5][:, 0:n - 512], vt, ptb[:, 512:n], start=(e == 4), stop=(e == 15))
                    if e == 15:
                        t.matmul(p[6][0:1, 0:512], c_onesb[:, :], QT[:, 0:512],
                                 start=False, stop=False)
                        t.matmul(p[7][0:1, 0:512], c_onesb[:, :], QT[:, 512:1024],
                                 start=False, stop=False)
                    r = t.matmul(p[6][0:1, 0:nlo], c_onesb[:, :], ptb[:, 0:nlo],
                                 start=(e == 0), stop=(e == 15))
                    if n > 512:
                        r = t.matmul(p[7][0:1, 0:n - 512], c_onesb[:, :], ptb[:, 512:n],
                                     start=(e == 4), stop=(e == 15))
                    return r
                avw = [pl.m("exp_%d_%d" % (h, e))]
                if e == 15:
                    avw.append(pl.m("corrpat"))
                if e < 8:
                    avw.append(pl.m("fix_%d_%d" % (h, e)))
                if e == 0:
                    avw += [pl.m("avadd%d" % (h - 1)), pl.m("zev%d" % (h - 1))]
                pl.op("tensor", f_av, waits=avw, inc="t")
                pl.mark("av_%d_%d" % (h, e), "t")

            def mk_zrec(hz):
                pl.op("sync", lambda sy: sy.dma_start(out=zscr[0:1, :], in_=rz_rep[0:1, :]),
                      waits=[pl.m("zev%d" % hz)], inc="qz", inc_by=16)
                pl.mark("zd3_%d" % hz, "qz")
                pl.op("sync", lambda sy: sy.dma_start(
                    out=rz_rep[:, :], in_=zscr[0:1, :].to_broadcast([P, 1024])),
                    waits=[pl.m("zd3_%d" % hz)], inc="qz", inc_by=16)
                pl.mark("zd4_%d" % hz, "qz")

            def mk_avsc(hz):
                def f_avsc(v, hz=hz):
                    return v.tensor_tensor(attT[:, 1024 * hz:1024 * hz + 1024],
                                           attT[:, 1024 * hz:1024 * hz + 1024],
                                           rz_rep[:, :], ALU.mult)
                pl.op("vector", f_avsc, waits=[pl.m("zd4_%d" % hz), pl.m("avadd%d" % hz)],
                      inc="v")
                pl.mark("avsc%d" % hz, "v")

            mk_sc(h, 0); mk_exp(h, 0)
            mk_sc(h, 1); mk_exp(h, 1)
            mk_sc(h, 2); mk_exp(h, 2)
            for e in range(3, NT):
                mk_sc(h, e)
                mk_av(h, e - 3)
                mk_exp(h, e)
                if e == 5 and h > 0:
                    mk_zrec(h - 1)
                if e == 8 and h > 0:
                    mk_avsc(h - 1)
            mk_av(h, 13)
            mk_av(h, 14)
            if h == 0:
                pl.op("sync", lambda sy: sy.dma_start(out=QT[:, 0:1024], in_=corrpat),
                      waits=[pl.m("sc_0_15")], inc="qz", inc_by=16)
                pl.mark("corrpat", "qz")
            mk_av(h, 15)

            def f_avadd(v, h=h):
                r = None
                for k in range(8):
                    bank = 4 if k < 4 else 5
                    col = 128 * (k % 4)
                    r = v.tensor_scalar(attT[:, 1024 * h + 128 * k:1024 * h + 128 * k + 128],
                                        p[bank][:, col:col + 128],
                                        s_vsuf[:, 8 * h + k:8 * h + k + 1], None, ALU.add)
                return r
            pl.op("vector", f_avadd, waits=[pl.m("av_%d_15" % h)], inc="v")
            pl.mark("avadd%d" % h, "v")

            def f_zev(s):
                s.activation(rz_rep[0:1, 0:512], p[6][0:1, :], AF.Ln)
                s.activation(rz_rep[0:1, 512:1024], p[7][0:1, :], AF.Ln)
                s.drain()
                return s.activation(rz_rep[0:1, :], rz_rep[0:1, :], AF.Exp, scale=-1.0)
            pl.op("scalar", f_zev, waits=[pl.m("av_%d_15" % h), pl.m("zd4_%d" % (h - 1)),
                                          pl.m("avsc%d" % (h - 1))],
                  inc="s")
            pl.mark("zev%d" % h, "s")

            pl.op("gpsimd", lambda g, h=h: g.dma_start(
                out=WoB[:, 2048 * h:2048 * h + 2048], in_=wo[128 * h:128 * h + 128, :]),
                waits=[pl.m("sc_%d_15" % h)], inc="qw", inc_by=16)
            pl.mark("wol%d" % h, "qw")
        mk_zrec(H - 1)
        mk_avsc(H - 1)
        pl.mark("p4done", "v")


    if stage >= 4:
        def f_tabload(g):
            return [g.dma_start(out=c_gthr[:, :], in_=gthrc),
                    g.dma_start(out=s_tab[:, 0:512], in_=bases_i),
                    g.dma_start(out=s_tab[:, 512:1024], in_=slopes_i),
                    g.dma_start(out=s_tab[:, 1026:1027], in_=oscale_i[0:1, :]),
                    g.dma_start(out=s_tab[:, 1024:1025], in_=cc_out[0:1, 0:64]),
                    g.dma_start(out=s_tab[:, 1025:1026], in_=cc_out[0:1, 64:128])]
        pl.op("gpsimd", f_tabload, waits=[pl.m("ccdone"), pl.m("p4done")],
              inc="qs", inc_by=96, inc_each=True)
        pl.mark("ccread", "qs")
        pl.op("vector", lambda v: v.tensor_scalar(s_tab[:, 1025:1026], s_tab[:, 1025:1026],
                                                  -1.0, None, ALU.mult),
              waits=[pl.m("ccread"), pl.m("p4done")], inc="v")
        pl.mark("tabready", "v")

        def emit_ep_pre(i):
            pl.op("sync", lambda sy, i=i: sy.dma_start(
                out=p5_x[i % 2][:, :], in_=x_own[128 * i:128 * i + 128, :]),
                waits=[pl.m("ep_xused%d" % (i - 2)), pl.m("p4done")], inc="qx", inc_by=16)
            pl.mark("xre%d" % i, "qx")

            def f_womm(t, i=i):
                qb = 4 * (i % 2)
                r = None
                for c in range(NC):
                    for q in range(4):
                        r = t.matmul(p[qb + q][:, :],
                                     attT[:, 1024 * c + 128 * i:1024 * c + 128 * i + 128],
                                     WoB[:, 2048 * c + 512 * q:2048 * c + 512 * q + 512],
                                     start=(c == 0), stop=(c == NC - 1))
                return r
            waits = [pl.m("p4done"), pl.m("wol15"), pl.m("woev%d" % (i - 2))]
            if i == 0:
                waits.append(pl.m("exp_15_15"))
            if i == 1:
                waits.append(pl.m("f2ev7"))
            pl.op("tensor", f_womm, waits=waits, inc="t")
            pl.mark("womm%d" % i, "t")

            pl.op("sync", lambda sy, i=i: sy.dma_start(
                out=p5_xn2[i % 2][:, :], in_=xnstash[128 * i:128 * i + 128, :]),
                waits=[pl.m("ep_xused%d" % (i - 2)), pl.m("p4done")], inc="qx", inc_by=16)
            pl.mark("xnld%d" % i, "qx")

        for i in range(8):
            pl.op("vector", lambda v, i=i: v.tensor_tensor(
                f2_e64[:, :], c_iota64[:, :], s_tidx[:, i:i + 1].to_broadcast([P, 64]),
                ALU.is_equal),
                waits=[pl.m("tabready"), pl.m("f2tr%d" % (i - 1))], inc="v")
            pl.mark("f2oh%d" % i, "v")

            pl.op("tensor", lambda t, i=i: t.transpose(
                p[6 + i // 4][0:64, 128 * (i % 4):128 * (i % 4) + 128],
                f2_e64[:, :], c_ident[:, :]),
                waits=[pl.m("f2oh%d" % i)], inc="t")
            pl.mark("f2tr%d" % i, "t")

            pl.op("vector", lambda v, i=i: v.tensor_copy(
                f2_ohT8[:, 128 * i:128 * i + 128],
                p[6 + i // 4][0:64, 128 * (i % 4):128 * (i % 4) + 128]),
                waits=[pl.m("f2tr%d" % i)], inc="v")
            pl.mark("f2ohev%d" % i, "v")

        def f_mn(t):
            r = None
            for i in range(8):
                r = t.matmul(p[5][:, 3 * i:3 * i + 3], f2_ohT8[:, 128 * i:128 * i + 128],
                             s_tab[:, 1024:1027], start=True, stop=True)
            return r
        pl.op("tensor", f_mn, waits=[pl.m("f2ohev7")], inc="t")
        pl.mark("f2mn", "t")
        pl.op("vector", lambda v: v.tensor_copy(s_mn8[:, :], p[5][:, 0:24]),
              waits=[pl.m("f2mn")], inc="v")
        pl.mark("f2mnev", "v")

        for i in range(8):
            bb, sb_b = (4, 5) if i % 2 == 0 else (6, 7)

            def f_gmm(t, i=i, bb=bb, sb_b=sb_b):
                t.matmul(p[bb][:, :], f2_ohT8[:, 128 * i:128 * i + 128],
                         s_tab[:, 0:512], start=True, stop=True)
                return t.matmul(p[sb_b][:, :], f2_ohT8[:, 128 * i:128 * i + 128],
                                s_tab[:, 512:1024], start=True, stop=True)
            w_g = [pl.m("f2mnev"), pl.m("f2ev%d" % (i - 2))]
            pl.op("tensor", f_gmm, waits=w_g, inc="t")
            pl.mark("f2ge%d" % i, "t")

            def f_spl(v, i=i, bb=bb, sb_b=sb_b):
                v.tensor_copy(f2_bsel[:, :], p[bb][:, :])
                v.tensor_copy(f2_ssel[:, :], p[sb_b][:, :])
                v.drain()
                xsl = s_xsl[:, DSL * i:DSL * i + DSL]
                v.tensor_tensor(f2_den[:, :], s_mn8[:, 3 * i + 1:3 * i + 2],
                                s_mn8[:, 3 * i:3 * i + 1], ALU.subtract)
                v.drain()
                v.tensor_scalar(f2_den[:, :], f2_den[:, :], 1e-8, None, ALU.add)
                v.drain()
                v.reciprocal(f2_den[:, :], f2_den[:, :])
                v.tensor_tensor(f2_u[:, :], xsl,
                                s_mn8[:, 3 * i:3 * i + 1].to_broadcast([P, DSL]), ALU.subtract)
                v.drain()
                v.tensor_tensor(f2_u[:, :], f2_u[:, :], f2_den[:, 0:1].to_broadcast([P, DSL]), ALU.mult)
                v.drain()
                v.tensor_scalar(f2_u[:, :], f2_u[:, :], 0.0, CLIP_HI, ALU.max, ALU.min)
                v.drain()
                v.tensor_scalar(f2_w[:, :], f2_u[:, :], float(GRID), None, ALU.mult)
                v.drain()
                v.tensor_tensor(f2_eq16[:, :],
                                f2_w[:, :].rearrange("p (a b) -> p a b", b=1).to_broadcast([P, DSL, GRID]),
                                c_gthr[:, :].rearrange("p (a b) -> p a b", b=GRID), ALU.is_ge)
                v.tensor_reduce(f2_gi[:, :].rearrange("p (a b) -> p a b", b=1),
                                f2_eq16[:, :].rearrange("p (a b) -> p a b", b=GRID),
                                AX.X, ALU.add)
                v.drain()
                v.tensor_scalar(f2_gi[:, :], f2_gi[:, :], 0.0, float(GRID - 1),
                                ALU.max, ALU.min)
                v.drain()
                v.tensor_tensor(f2_xl[:, :], f2_w[:, :], f2_gi[:, :], ALU.subtract)
                v.tensor_scalar(f2_t512[:, 0:DSL], f2_gi[:, :], 1.0, None, ALU.add)
                v.drain()
                v.tensor_tensor(f2_eq16[:, :],
                                f2_t512[:, 0:DSL].rearrange("p (a b) -> p a b", b=1).to_broadcast([P, DSL, GRID]),
                                c_gthr[:, :].rearrange("p (a b) -> p a b", b=GRID), ALU.is_equal)
                v.tensor_tensor(f2_t512[:, :], f2_bsel[:, :], f2_eq16[:, :], ALU.mult)
                v.tensor_reduce(f2_u[:, :].rearrange("p (a b) -> p a b", b=1),
                                f2_t512[:, :].rearrange("p (a b) -> p a b", b=GRID),
                                AX.X, ALU.add)
                v.drain()
                v.tensor_tensor(f2_t512[:, :], f2_ssel[:, :], f2_eq16[:, :], ALU.mult)
                v.tensor_reduce(f2_w[:, :].rearrange("p (a b) -> p a b", b=1),
                                f2_t512[:, :].rearrange("p (a b) -> p a b", b=GRID),
                                AX.X, ALU.add)
                v.drain()
                v.tensor_tensor(f2_xl[:, :], f2_w[:, :], f2_xl[:, :], ALU.mult)
                v.drain()
                v.tensor_tensor(f2_xl[:, :], f2_u[:, :], f2_xl[:, :], ALU.add)
                v.drain()
                return v.tensor_tensor(s_delta[:, DSL * i:DSL * i + DSL], f2_xl[:, :],
                                       s_mn8[:, 3 * i + 2:3 * i + 3].to_broadcast([P, DSL]),
                                       ALU.mult)
            pl.op("vector", f_spl, waits=[pl.m("f2ge%d" % i)], inc="v")
            pl.mark("f2ev%d" % i, "v")
            pl.mark("f2d%d" % i, "v")
            if i == 1:
                emit_ep_pre(0)


        for i in range(8):
            if i > 0:
                emit_ep_pre(i)

            def f_ep_a(v, i=i):
                ob = p5_o[i % 2]
                qb = 4 * (i % 2)
                r = None
                for q in range(4):
                    r = v.tensor_tensor(ob[:, 512 * q:512 * q + 512], p[qb + q][:, :],
                                        p5_x[i % 2][:, 512 * q:512 * q + 512], ALU.add)
                return r
            pl.op("vector", f_ep_a, waits=[pl.m("xre%d" % i), pl.m("womm%d" % i),
                                           pl.m("ep_oused%d" % (i - 2))], inc="v")
            pl.mark("woev%d" % i, "v")
            pl.mark("ep_xused%d" % i, "v")

            def f_ep_b(v, i=i):
                ob = p5_o[i % 2]
                v.tensor_tensor(ob[:, :], ob[:, :], p5_xn2[i % 2][:, :], ALU.add)
                tmp = f2_t512[:, 0:256]
                for m in range(8):
                    v.tensor_scalar(tmp[:, 32 * m:32 * m + 32],
                                    s_delta[:, DSL * i:DSL * i + DSL],
                                    s_eqm[:, 8 * i + m:8 * i + m + 1], None, ALU.mult)
                r = None
                for c8 in range(8):
                    r = v.scalar_tensor_tensor(ob[:, 256 * c8:256 * c8 + 256], tmp,
                                               s_eqc[:, 8 * i + c8:8 * i + c8 + 1],
                                               ob[:, 256 * c8:256 * c8 + 256],
                                               ALU.mult, ALU.add)
                return r
            pl.op("vector", f_ep_b, waits=[pl.m("xnld%d" % i), pl.m("f2d7"),
                                           pl.m("woev%d" % i)], inc="v")
            pl.mark("epd%d" % i, "v")

            pl.op("sync", lambda sy, i=i: sy.dma_start(
                out=y[128 * i:128 * i + 128, :], in_=p5_o[i % 2][:, :]),
                waits=[pl.m("epd%d" % i)], inc="st", inc_by=16)
            pl.mark("yst%d" % i, "st")
            pl.mark("ep_oused%d" % i, "st")

    specs = {
        "xnT": (xnT, [P, 32768], BF16, "xnT15"),
        "tidx": (s_tidx, [P, 8], F32, "f1ev7"),
        "xsl": (s_xsl, [P, 256], F32, "f1ev7"),
        "eqc": (s_eqc, [P, 64], F32, "f1ev7"),
        "eqm": (s_eqm, [P, 64], F32, "f1ev7"),
        "acc": (f1_acc, [P, 1], F32, "f1ev7"),
        "tab": (s_tab, [64, 1027], F32, "ccread"),
        "KT": (KT, [P, H * 2048], BF16, "ktevf15"),
        "V": (V, [P, NT * 2048], BF16, "valldone"),
        "QT": (QT, [P, H * 1024], BF16, "qev_15"),
        "vsuf": (s_vsuf, [P, H * 8], F32, "vsufall"),
        "attT": (attT, [P, H * 1024], BF16, "p4done"),
        "delta": (s_delta, [P, 256], F32, "f2d7"),
    }
    for name in taps:
        if name not in specs:
            continue
        tn, shape, dt, mk = specs[name]
        if pl.m(mk) is None:
            continue
        ap = dtap(name, shape, dt)
        pl.op("sync", lambda sy, ap=ap, tn=tn: sy.dma_start(out=ap, in_=tn[:, :]),
              waits=[pl.m(mk), pl.at("v"), pl.at("s"), pl.at("t"), pl.at("g")],
              inc="st", inc_by=16)

    pl.op("sync", lambda sy: None, waits=[pl.at("st")])

    _emit(nc, pl)
    return nc, dbg


def _emit(nc, pl):
    import contextlib
    ctx = contextlib.ExitStack()
    semnames = sorted(set(list(pl.cnt.keys()) +
                          [w[0] for ops in pl.ops.values()
                           for (_, ws, _, _, _) in ops for w in ws]))
    sems = {s: ctx.enter_context(nc.semaphore(s)) for s in semnames}
    blk = ctx.enter_context(nc.Block())

    def mkbody(engname):
        ops = pl.ops[engname]
        lastw = {}

        def run(eng):
            for fn, ws, inc, inc_by, inc_each in ops:
                for s, v in ws:
                    if lastw.get(s, -1) < v:
                        eng.wait_ge(sems[s], v)
                        lastw[s] = v
                r = fn(eng)
                if inc is not None:
                    if inc_each:
                        rs = r if isinstance(r, (list, tuple)) else [r]
                        amt = inc_by // len(rs)
                        for ri in rs:
                            ri.then_inc(sems[inc], amt)
                    elif r is None:
                        eng.nop().then_inc(sems[inc], inc_by)
                    else:
                        r.then_inc(sems[inc], inc_by)
        return run

    for name, deco in [("sync", blk.sync), ("vector", blk.vector),
                       ("scalar", blk.scalar), ("tensor", blk.tensor),
                       ("gpsimd", blk.gpsimd)]:
        if pl.ops[name]:
            deco(mkbody(name))
    ctx.close()


def _host_layout(inputs):
    x = np.asarray(inputs["L0_attn_x"])
    per_core = []
    for c in range(8):
        b, par = c // 2, c % 2
        g_own, g_peer = G_PAR[par], G_PAR[1 - par]
        xo = np.concatenate([x[b, 128 * g:128 * g + 128, :] for g in g_own], axis=0)
        xp = np.concatenate([x[b, 128 * g:128 * g + 128, :] for g in g_peer], axis=0)
        pm = np.asarray([1.0 if g_peer[k] < g_own[k] else 0.0 for k in range(8)],
                        dtype=np.float32) * np.float32(INV_SQRT_T)
        per_core.append({
            "x_own": np.ascontiguousarray(xo),
            "x_peer": np.ascontiguousarray(xp),
            "wq": np.asarray(inputs["L0_attn_Wq"]),
            "wk": np.asarray(inputs["L0_attn_Wk"]),
            "wv": np.asarray(inputs["L0_attn_Wv"]),
            "wo": np.asarray(inputs["L0_attn_Wo"]),
            "gamma": np.ascontiguousarray(np.asarray(inputs["L0_ffn_ln_gamma"])[None, :]),
            "beta": np.ascontiguousarray(np.asarray(inputs["L0_ffn_ln_beta"])[None, :]),
            "bases": np.ascontiguousarray(np.asarray(inputs["L0_ffn_spline_bases"]).reshape(64, DSL * GRID)),
            "slopes": np.ascontiguousarray(np.asarray(inputs["L0_ffn_spline_slopes"]).reshape(64, DSL * GRID)),
            "oscale": np.ascontiguousarray(np.asarray(inputs["L0_ffn_output_scale"])[None, :]),
            "pm_scaled": pm[None, :],
        })
    return per_core


def kernel(**inputs):
    in_maps = _host_layout(inputs)
    nc, dbg = build_program()
    res = run_bass_kernel_spmd(nc, in_maps, core_ids=list(range(8)))
    if DBG:
        DBG["results"] = res
    out = np.zeros((4, 2048, D), dtype=np.float32)
    for c in range(8):
        b, par = c // 2, c % 2
        yc = res.results[c]["y"]
        for k, g in enumerate(G_PAR[par]):
            out[b, 128 * g:128 * g + 128, :] = yc[128 * k:128 * k + 128, :]
    return out


# version 34
# speedup vs baseline: 1.0034x; 1.0034x over previous
import math
from collections import defaultdict
import numpy as np
import ml_dtypes
import concourse.bass as bass
import concourse.mybir as mybir
from concourse.bass_utils import run_bass_kernel_spmd

F32 = mybir.dt.float32
BF16 = mybir.dt.bfloat16
AF = mybir.ActivationFunctionType
ALU = mybir.AluOpType
AX = mybir.AxisListType

D = 2048
H = 16
NT = 16
NC = 16
P = 128
DSL = 32
GRID = 16
INV_SQRT_T = 1.0 / math.sqrt(2048.0)
LN_EPS = 1e-5
CLIP_HI = float(np.float32(1.0) - np.float32(1e-6))
BIGM = 65536.0
BIGI = 4096.0
BIGF = 1e30

E_PAD = [16, 14, 12, 10, 8, 6, 4, 2]
G_PAR = {0: [15, 13, 11, 9, 6, 4, 2, 0], 1: [14, 12, 10, 8, 7, 5, 3, 1]}
CNT = [(e % 8) + 1 for e in range(16)]
C_CORR = [float(2048 - 128 * ek) for ek in E_PAD]

DBG = {}

ENGS = ("sync", "vector", "scalar", "tensor", "gpsimd")


class Plan:
    def __init__(self):
        self.ops = {e: [] for e in ENGS}
        self.cnt = defaultdict(int)
        self.marks = {}

    def op(self, eng, fn, waits=(), inc=None, inc_by=1, inc_each=False):
        ws = [(s, v) for w in waits if w is not None for (s, v) in [w] if v > 0]
        if inc is not None:
            self.cnt[inc] += inc_by
        self.ops[eng].append((fn, ws, inc, inc_by, inc_each))

    def at(self, semname):
        return (semname, self.cnt[semname])

    def mark(self, key, semname):
        self.marks[key] = (semname, self.cnt[semname])

    def m(self, key):
        return self.marks.get(key)


def xcol(c, t):
    if t < 1024:
        return 1024 * c + t
    return 16384 + 1024 * c + (t - 1024)


def build_program():
    stage = DBG.get("stage", 99)
    taps = set(DBG.get("taps", []))

    nc = bass.Bass("TRN2", target_bir_lowering=False,
                   dynamic_dma_scratch_size=16384)

    def din(name, shape, dt=F32):
        return nc.dram_tensor(name, list(shape), dt, kind="ExternalInput").ap()

    x_own = din("x_own", (1024, D))
    x_peer = din("x_peer", (1024, D))
    wq = din("wq", (D, D)); wk = din("wk", (D, D))
    wv = din("wv", (D, D)); wo = din("wo", (D, D))
    gamma = din("gamma", (1, D)); beta = din("beta", (1, D))
    bases_i = din("bases", (64, DSL * GRID))
    slopes_i = din("slopes", (64, DSL * GRID))
    oscale_i = din("oscale", (1, 64))
    pm_i = din("pm_scaled", (1, 8))

    y = nc.dram_tensor("y", [1024, D], F32, kind="ExternalOutput").ap()
    cc_in = nc.dram_tensor("cc_in", [1, 128], F32).ap()
    cc_out = nc.dram_tensor("cc_out", [1, 128], F32).ap()
    zscr = nc.dram_tensor("zscr", [1, 1024], F32).ap()
    kpark = nc.dram_tensor("kpark", [P, H * 1024], BF16).ap()
    xnstash = nc.dram_tensor("xnstash", [8 * P, D], F32).ap()
    wkb = nc.dram_tensor("wkb", [D, D], BF16).ap()
    wvb = nc.dram_tensor("wvb", [D, D], BF16).ap()
    wqb = nc.dram_tensor("wqb", [D, D], BF16).ap()
    slscr = nc.dram_tensor("slscr", [1, D], F32).ap()

    dbg = {}
    def dtap(name, shape, dt=F32):
        if name not in dbg:
            dbg[name] = nc.dram_tensor("dbg_" + name, list(shape), dt,
                                       kind="ExternalOutput").ap()
        return dbg[name]

    def inl(arr, name):
        return nc.inline_tensor(np.ascontiguousarray(arr), name).ap()

    ident_f = inl(np.eye(P, dtype=np.float32), "identfc")
    ident_b = inl(np.eye(P, dtype=np.float32).astype(ml_dtypes.bfloat16), "identbc")
    iota64c = inl(np.broadcast_to(np.arange(64, dtype=np.float32), (P, 64)), "iota64c")
    iota64mc = inl(np.broadcast_to(np.arange(64, dtype=np.float32) + BIGI, (P, 64)), "iota64mc")
    iota8mc = inl(np.broadcast_to(np.arange(8, dtype=np.float32) + BIGI, (P, 8)), "iota8mc")
    iota8rc = inl(np.broadcast_to(np.arange(8, dtype=np.float32), (P, 8)), "iota8rc")
    clurowc = inl(np.broadcast_to(np.arange(64, dtype=np.float32) // 8, (P, 64)), "clurowc")
    cmp8c = inl(np.broadcast_to(8.0 * np.arange(1, 8, dtype=np.float32), (P, 7)), "cmp8c")
    gthrc = inl(np.broadcast_to(np.tile(np.arange(1, GRID + 1, dtype=np.float32), DSL), (P, DSL * GRID)), "gthrc")
    gidxc = inl(np.broadcast_to(np.tile(np.arange(GRID, dtype=np.float32), DSL), (P, DSL * GRID)), "gidxc")
    ccorrc = inl(np.broadcast_to(np.asarray(C_CORR, dtype=np.float32), (P, 8)).copy(), "ccorrc")
    corrpat = inl(np.broadcast_to(
        np.repeat(np.asarray([c / 128.0 for c in C_CORR], dtype=np.float32), 128),
        (P, 1024)).astype(ml_dtypes.bfloat16).copy(), "corrpat")
    onesbc = inl(np.ones((P, 1), dtype=ml_dtypes.bfloat16), "onesbc")
    _st = np.arange(P)[:, None]; _tt = np.arange(128)[None, :]
    trilbc = inl((_tt >= _st).astype(ml_dtypes.bfloat16), "trilbc")
    triu1c = inl((_tt < _st).astype(ml_dtypes.bfloat16), "triu1c")

    SB0 = (nc.sbuf_base + 31) & ~31
    cap = nc.sbuf_top - SB0

    def sb(name, off, shape, dt):
        nbytes = int(np.prod(shape[1:])) * mybir.dt.size(dt)
        assert off % 32 == 0 and off + nbytes <= cap, (name, off, nbytes, cap)
        return nc.alloc_sbuf_tensor_at(name, list(shape), dt, offset=SB0 + off)

    xnT = sb("xnT", 0, [P, 32768], BF16)
    attT = sb("attT", 0, [P, H * 1024], BF16)
    QT = sb("QT", 32768, [P, H * 1024], BF16)
    p5_x = [sb("p5_x0", 32768, [P, D], F32), sb("p5_x1", 40960, [P, D], F32)]
    p5_o = [sb("p5_o0", 49152, [P, D], F32), sb("p5_o1", 57344, [P, D], F32)]
    KT = sb("KT", 65536, [P, H * 2048], BF16)
    WoB = sb("WoB", 65536, [P, NC * 2048], BF16)
    p0_x = [sb("p0_x0", 65536, [P, D], F32), sb("p0_x1", 73728, [P, D], F32)]
    p0_xc = [sb("p0_xc0", 81920, [P, D], F32), sb("p0_xc1", 90112, [P, D], F32)]
    p0_xn = [sb("p0_xn0", 98304, [P, D], F32), sb("p0_xn1", 106496, [P, D], F32)]
    gam_rep = sb("gam_rep", 114688, [P, D], F32)
    bet_rep = sb("bet_rep", 122880, [P, D], F32)
    V = sb("V", 131072, [P, NT * 2048], BF16)
    slope_rep = sb("slope_rep", 131072, [P, D], F32)
    f1_xf = sb("f1_xf", 139264, [P, D], F32)
    f1_ts = sb("f1_ts", 147456, [P, 64], F32)
    f1_sc1 = sb("f1_sc1", 147712, [P, 64], F32)
    f1_cs = sb("f1_cs", 147968, [P, 8], F32)
    f1_t8 = sb("f1_t8", 148000, [P, 8], F32)
    f1_bc = sb("f1_bc", 148032, [P, 1], F32)
    f1_e64 = sb("f1_e64", 148064, [P, 64], F32)
    f1_e64b = sb("f1_e64b", 148320, [P, 64], F32)
    f1_g2 = sb("f1_g2", 148576, [P, 256], F32)
    f1_mm = sb("f1_mm", 149600, [P, 128], F32)
    f1_mme = sb("f1_mme", 150112, [P, 128], F32)
    f1_acc = sb("f1_acc", 150624, [P, 1], F32)
    f1_sgn = sb("f1_sgn", 150656, [P, D], F32)
    f1_t7 = sb("f1_t7", 158848, [P, 7], F32)
    kstg = [sb("kstg0", 158880, [P, 1024], BF16), sb("kstg1", 160928, [P, 1024], BF16)]
    f2_bsel = sb("f2_bsel", 131072, [P, 512], F32)
    f2_ssel = sb("f2_ssel", 133120, [P, 512], F32)
    f2_eq16 = sb("f2_eq16", 135168, [P, 512], F32)
    f2_t512 = sb("f2_t512", 137216, [P, 512], F32)
    f2_mn = sb("f2_mn", 139264, [P, 3], F32)
    f2_u = sb("f2_u", 139296, [P, DSL], F32)
    f2_w = sb("f2_w", 139424, [P, DSL], F32)
    f2_gi = sb("f2_gi", 139552, [P, DSL], F32)
    f2_xl = sb("f2_xl", 139680, [P, DSL], F32)
    f2_den = sb("f2_den", 139808, [P, 1], F32)
    f2_ohT = sb("f2_ohT", 139840, [64, 128], F32)
    f2_e64 = sb("f2_e64", 140352, [P, 64], F32)
    f2_ohT8 = sb("f2_ohT8", 163840, [64, 1024], F32)
    s_mn8 = sb("s_mn8", 167936, [P, 24], F32)
    p5_xc = sb("p5_xc", 140864, [P, D], F32)
    p5_xn = sb("p5_xn", 149056, [P, D], F32)
    p5_xn2 = [sb("p5_xn2a", 140864, [P, D], F32), sb("p5_xn2b", 149056, [P, D], F32)]
    doff = [196608]
    def sbd(name, shape, dt):
        off = doff[0]
        h = sb(name, off, shape, dt)
        doff[0] = (off + int(np.prod(shape[1:])) * mybir.dt.size(dt) + 31) & ~31
        return h

    c_ident = sbd("c_ident", [P, 128], F32)
    c_identb = sbd("c_identb", [P, 128], BF16)
    c_iota64 = sbd("c_iota64", [P, 64], F32)
    c_iota64m = sbd("c_iota64m", [P, 64], F32)
    c_clu = sbd("c_clu", [P, 64], F32)
    c_iota8m = sbd("c_iota8m", [P, 8], F32)
    c_iota8r = sbd("c_iota8r", [P, 8], F32)
    c_cmp8 = sbd("c_cmp8", [P, 7], F32)
    c_onesb = sbd("c_onesb", [P, 1], BF16)
    c_trilb = sbd("c_trilb", [P, 128], BF16)
    c_triu1 = sbd("c_triu1", [P, 128], BF16)
    s_pm = sbd("s_pm", [P, 8], F32)
    s_slm = sb("s_slm", 147456, [64, 32], F32)
    s_tab = sb("s_tab", 157248, [64, 1027], F32)
    c_gthr = sb("c_gthr", 161408, [P, 512], F32)
    s_tidx = sbd("s_tidx", [P, 8], F32)
    s_xsl = sbd("s_xsl", [P, 8 * DSL], F32)
    s_delta = sbd("s_delta", [P, 8 * DSL], F32)
    s_eqc = sbd("s_eqc", [P, 64], F32)
    s_eqm = sbd("s_eqm", [P, 64], F32)
    s_vsuf = sbd("s_vsuf", [P, H * 8], F32)
    s_tsum = sbd("s_tsum", [P, 16], F32)
    s_vt = sbd("s_vt", [P, 1024], BF16)
    c_ccorr = sbd("c_ccorr", [P, 8], F32)
    c_sum4 = sbd("c_sum4", [P, 4], F32)
    c_var4 = sbd("c_var4", [P, 4], F32)
    c_a4 = sbd("c_a4", [P, 4], F32)
    c_lt4 = sbd("c_lt4", [P, 4], F32)
    c_y04 = sbd("c_y04", [P, 4], F32)
    c_mu4 = sbd("c_mu4", [P, 4], F32)
    c_vt14 = sbd("c_vt14", [P, 4], F32)
    c_vt24 = sbd("c_vt24", [P, 4], F32)
    c_ft1 = sbd("c_ft1", [P, 1], F32)
    c_ft2 = sbd("c_ft2", [P, 1], F32)
    c_mn = sbd("c_mn", [P, 1], F32)
    c_mx = sbd("c_mx", [P, 1], F32)
    c_m1 = sbd("c_m1", [P, 1], F32)
    c_fe = sbd("c_fe", [P, 1], F32)
    WB0 = doff[0]
    wbuf = [sbd("wbuf0", [P, D], BF16), sbd("wbuf1", [P, D], BF16)]
    rz_rep = sb("rz_rep", WB0, [P, 1024], F32)
    pt = [sb("pt0", WB0 + 4096, [P, 1024], BF16),
          sb("pt1", WB0 + 6144, [P, 1024], BF16)]
    pt3 = [pt[0], pt[1], s_vt]
    assert doff[0] <= cap, ("D overflow", doff[0], cap)

    pw01 = nc.alloc_psum_tensor("pw01", [P, 1024], F32)
    pw23 = nc.alloc_psum_tensor("pw23", [P, 1024], F32)
    p = [pw01[:, 0:512], pw01[:, 512:1024], pw23[:, 0:512], pw23[:, 512:1024]] + \
        [nc.alloc_psum_tensor("pb%d" % i, [P, 512], F32) for i in range(4, 8)]

    def pbank_bf(i):
        return p[i][:, 0:64].bitcast(BF16)[:, 0:128]

    pl = Plan()

    const_list = [
        (c_ident[:, :], ident_f), (c_identb[:, :], ident_b),
        (c_iota64[:, :], iota64c), (c_iota64m[:, :], iota64mc),
        (c_clu[:, :], clurowc), (c_iota8m[:, :], iota8mc),
        (c_iota8r[:, :], iota8rc), (c_cmp8[:, :], cmp8c),
        (c_onesb[:, :], onesbc), (c_ccorr[:, :], ccorrc),
        (c_trilb[:, :], trilbc), (c_triu1[:, :], triu1c),
        (f1_xf[0:64, 0:512], slopes_i),
    ]
    pl.op("sync", lambda e: [e.dma_start(out=d, in_=s) for d, s in const_list],
          inc="qc", inc_by=16 * len(const_list), inc_each=True)
    pl.mark("consts", "qc")

    bcast_list = [
        (gam_rep[:, :], gamma.to_broadcast([P, D])),
        (bet_rep[:, :], beta.to_broadcast([P, D])),
        (s_pm[:, :], pm_i.to_broadcast([P, 8])),
    ]
    pl.op("gpsimd", lambda e: [e.dma_start(out=d, in_=s) for d, s in bcast_list],
          inc="qb", inc_by=16 * len(bcast_list), inc_each=True)
    pl.mark("bcast", "qb")

    def f_slmean(v):
        v.tensor_reduce(s_slm[:, :].rearrange("p (a b) -> p a b", b=1),
                        f1_xf[0:64, 0:512].rearrange("p (a b) -> p a b", b=GRID),
                        AX.X, ALU.add)
        v.drain()
        return v.tensor_scalar(s_slm[:, :], s_slm[:, :], 1.0 / GRID, None, ALU.mult)
    pl.op("vector", f_slmean, waits=[pl.m("consts")], inc="v")
    pl.mark("slmean", "v")

    pl.op("scalar", lambda s: s.activation(s_slm[:, :], s_slm[:, :], AF.Sign),
          waits=[pl.m("slmean")], inc="s")
    pl.mark("slsign", "s")

    pl.op("gpsimd", lambda g: g.dma_start(out=slscr[0:1, :], in_=s_slm[:, :]),
          waits=[pl.m("slsign")], inc="qb", inc_by=16)
    pl.mark("slflat", "qb")
    pl.op("gpsimd", lambda g: g.dma_start(out=slope_rep[:, :],
                                          in_=slscr[0:1, :].to_broadcast([P, D])),
          waits=[pl.m("slflat")], inc="qb", inc_by=16)
    pl.mark("slrep", "qb")

    def emit_ln(tag, xbuf, xcbuf, xnbuf, dump, ss, w_v1, w_sq=(), eng_inc=None):
        def f_v1(v, xbuf=xbuf, xcbuf=xcbuf, ss=ss):
            v.reduce_sum(c_sum4[:, ss:ss + 1], xbuf[:, :], AX.X)
            v.drain()
            v.tensor_scalar(c_mu4[:, ss:ss + 1], c_sum4[:, ss:ss + 1], 1.0 / D, None, ALU.mult)
            v.drain()
            return v.tensor_tensor(xcbuf[:, :], xbuf[:, :],
                                   c_mu4[:, ss:ss + 1].to_broadcast([P, D]), ALU.subtract)
        pl.op("vector", f_v1, waits=w_v1, inc="v")
        pl.mark(tag + "xc", "v")

        pl.op("scalar", lambda s, dump=dump, xcbuf=xcbuf: s.square(dump[:, :], xcbuf[:, :]),
              waits=[pl.m(tag + "xc")] + list(w_sq), inc="s")
        pl.mark(tag + "sq", "s")

        def f_v2(v, dump=dump, ss=ss):
            v.reduce_sum(c_sum4[:, ss:ss + 1], dump[:, :], AX.X)
            v.drain()
            return v.tensor_scalar(c_a4[:, ss:ss + 1], c_sum4[:, ss:ss + 1],
                                   1.0 / D, LN_EPS, ALU.mult, ALU.add)
        pl.op("vector", f_v2, waits=[pl.m(tag + "sq")], inc="v")
        pl.mark(tag + "st", "v")

        def f_s2(s, ss=ss):
            s.activation(c_lt4[:, ss:ss + 1], c_a4[:, ss:ss + 1], AF.Ln)
            s.drain()
            return s.activation(c_y04[:, ss:ss + 1], c_lt4[:, ss:ss + 1], AF.Exp, scale=-0.5)
        pl.op("scalar", f_s2, waits=[pl.m(tag + "st")], inc="s")
        pl.mark(tag + "md", "s")

        def f_v3(v, ss=ss):
            y0 = c_y04[:, ss:ss + 1]
            v.scalar_tensor_tensor(c_vt14[:, ss:ss + 1], y0, c_a4[:, ss:ss + 1], y0,
                                   ALU.mult, ALU.mult)
            v.drain()
            v.tensor_scalar(c_vt24[:, ss:ss + 1], c_vt14[:, ss:ss + 1],
                            -0.5, 1.5, ALU.mult, ALU.add)
            v.drain()
            return v.tensor_tensor(y0, y0, c_vt24[:, ss:ss + 1], ALU.mult)
        pl.op("vector", f_v3, waits=[pl.m(tag + "md")], inc="v")
        pl.mark(tag + "nw", "v")

        pl.op("scalar", lambda s, xnbuf=xnbuf, xcbuf=xcbuf, ss=ss:
              s.activation(xnbuf[:, :], xcbuf[:, :], AF.Identity,
                           scale=c_y04[:, ss:ss + 1]),
              waits=[pl.m(tag + "nw")], inc="s")
        pl.mark(tag + "ap", "s")

    ctr = {"sl": 0}

    WCONV = {id(wk): 0, id(wv): 1, id(wq): 2}
    WBF = {id(wk): None, id(wv): None, id(wq): None}

    def f_wl_mk(w, h, wb):
        wb16 = {0: wkb, 1: wvb, 2: wqb}[WCONV[id(w)]]
        def f_wl(gp):
            srcw = wb16.rearrange("(c p) d -> p c d", p=P)[:, :, 128 * h:128 * h + 128]
            dst = wb[:, :].rearrange("p (c d) -> p c d", d=128)
            return [gp.dma_start(out=dst[:, 4 * q:4 * q + 4, :],
                                 in_=srcw[:, 4 * q:4 * q + 4, :]) for q in range(4)]
        return f_wl

    def emit_kpeer(h):
        sl = ctr["sl"]; ctr["sl"] += 1
        wb = wbuf[sl % 2]
        pl.op("sync", f_wl_mk(wk, h, wb), waits=[pl.m("slcd%d" % (sl - 2)),
                                                  pl.m("wconv0")],
              inc="qw", inc_by=64, inc_each=True)
        pl.mark("wl%d" % sl, "qw")
        for half in range(2):
            q = 2 + half
            bank = 2 + half

            def f_mm(t, wb=wb, q=q, bank=bank):
                r = None
                for c in range(NC):
                    r = t.matmul(p[bank][:, :], wb[:, 128 * c:128 * c + 128],
                                 xnT[:, xcol(c, 512 * q):xcol(c, 512 * q) + 512],
                                 start=(c == 0), stop=(c == NC - 1))
                return r
            w_mm = [pl.m("wl%d" % sl), pl.m("xnT15")]
            if half == 0:
                w_mm += [pl.m("f1ev%d" % (h // 2)), pl.m("kpev%d_0" % (h - 1))]
            else:
                w_mm.append(pl.m("kpev%d_1" % (h - 1)))
            pl.op("tensor", f_mm, waits=w_mm, inc="t")
            pl.mark("kpmm%d_%d" % (h, half), "t")
            if half == 1:
                pl.mark("slcd%d" % sl, "t")
        for half in range(2):
            def f_ev(x, h=h, half=half):
                return x.copy(kstg[h % 2][:, 512 * half:512 * half + 512],
                              p[2 + half][:, :])
            we = [pl.m("kpmm%d_%d" % (h, half))]
            if half == 0:
                we.append(pl.m("kpark%d" % (h - 2)))
            pl.op("scalar", f_ev, waits=we, inc="s")
            pl.mark("kpev%d_%d" % (h, half), "s")

        pl.op("gpsimd", lambda g, h=h: g.dma_start(
            out=kpark[:, 1024 * h:1024 * h + 1024], in_=kstg[h % 2][:, :]),
            waits=[pl.m("kpev%d_1" % h)], inc="qk", inc_by=16)
        pl.mark("kpark%d" % h, "qk")

    def emit_vpeer(h):
        sl = ctr["sl"]; ctr["sl"] += 1
        wb = wbuf[sl % 2]
        pl.op("sync", f_wl_mk(wv, h, wb), waits=[pl.m("slcd%d" % (sl - 2)),
                                                  pl.m("wconv1")],
              inc="qw", inc_by=64, inc_each=True)
        pl.mark("wl%d" % sl, "qw")

        def f_mm(t, wb=wb):
            r = None
            for c in range(NC):
                for q in (2, 3):
                    r = t.matmul(p[4 + q][:, :], wb[:, 128 * c:128 * c + 128],
                                 xnT[:, xcol(c, 512 * q):xcol(c, 512 * q) + 512],
                                 start=(c == 0), stop=(c == NC - 1))
            return r
        pl.op("tensor", f_mm, waits=[pl.m("wl%d" % sl), pl.m("xnT15"),
                                     pl.m("vpsv%d" % (h - 1))], inc="t")
        pl.mark("vpmm%d" % h, "t")

        def f_sv(x):
            x.copy(s_vt[:, 0:512], p[6][:, :])
            return x.copy(s_vt[:, 512:1024], p[7][:, :])
        pl.op("scalar", f_sv, waits=[pl.m("vpmm%d" % h), pl.m("vptr%d_7" % (h - 1))],
              inc="s")
        pl.mark("vpsv%d" % h, "s")

        tb = 4 + h % 2
        def f_vtr(t, tb=tb):
            r = None
            for tt in range(8):
                r = t.transpose(p[tb][:, 64 * tt:64 * tt + 64].bitcast(BF16)[:, 0:128],
                                s_vt[:, 128 * tt:128 * tt + 128], c_identb[:, :])
            return r
        pl.op("tensor", f_vtr, waits=[pl.m("vpsv%d" % h), pl.m("vpevall%d" % (h - 2))],
              inc="t")
        pl.mark("vptr%d_7" % h, "t")
        pl.mark("slcd%d" % sl, "t")

        def f_vev(x, h=h, tb=tb):
            r = None
            for tt in range(8):
                r = x.copy(V[:, 2048 * (8 + tt) + 128 * h:2048 * (8 + tt) + 128 * h + 128],
                           p[tb][:, 64 * tt:64 * tt + 64].bitcast(BF16)[:, 0:128])
            return r
        pl.op("scalar", f_vev, waits=[pl.m("vptr%d_7" % h)], inc="s")
        pl.mark("vpevall%d" % h, "s")
        for tt in range(8):
            pl.mark("vev_%d_%d" % (8 + tt, h), "s")

    def emit_f1acc(ef):
        pl.op("tensor", lambda t: t.transpose(p[2][:, 0:128], f1_mm[:, :], c_ident[:, :]),
              waits=[pl.m("mm%d" % ef), pl.m("f1ev%d" % (ef - 1)),
                     pl.m("kpev%d_0" % (2 * ef - 1))], inc="t")
        pl.mark("f1tr%d" % ef, "t")

        def f_f1ev(v, ef=ef):
            v.tensor_copy(f1_mme[:, :], p[2][:, 0:128])
            v.drain()
            if ef == 0:
                v.tensor_reduce(f1_acc[0:64, :], f1_mme[0:64, :], AX.X, ALU.min)
                r = v.tensor_reduce(f1_acc[64:128, :], f1_mme[64:128, :], AX.X, ALU.max)
            else:
                v.tensor_reduce(c_fe[0:64, :], f1_mme[0:64, :], AX.X, ALU.min)
                v.tensor_reduce(c_fe[64:128, :], f1_mme[64:128, :], AX.X, ALU.max)
                v.drain()
                v.tensor_tensor(f1_acc[0:64, :], f1_acc[0:64, :], c_fe[0:64, :], ALU.min)
                r = v.tensor_tensor(f1_acc[64:128, :], f1_acc[64:128, :], c_fe[64:128, :], ALU.max)
            if ef == 7:
                v.drain()
                r = v.tensor_scalar(f1_acc[64:128, :], f1_acc[64:128, :], -1.0, None, ALU.mult)
            return r
        pl.op("vector", f_f1ev, waits=[pl.m("f1tr%d" % ef)], inc="v")
        pl.mark("f1ev%d" % ef, "v")

    def emit_group(eg):
        emit_kpeer(2 * eg)
        if not DBG.get("nov"):
            emit_vpeer(2 * eg)
        emit_kpeer(2 * eg + 1)
        if not DBG.get("nov"):
            emit_vpeer(2 * eg + 1)

    def emit_wconv_chunk(wsrc, wdst, j, wi_, wait_mark):
        pl.op("gpsimd", lambda g, wsrc=wsrc, wdst=wdst, j=j: [g.dma_start(
            out=wdst[512 * j + 256 * s_:512 * j + 256 * s_ + 256, :],
            in_=wsrc[512 * j + 256 * s_:512 * j + 256 * s_ + 256, :]) for s_ in range(2)],
            waits=[pl.m(wait_mark)], inc="qwc", inc_by=32, inc_each=True)
        if j == 3:
            pl.mark("wconv%d" % wi_, "qwc")

    P0_SEQ = list(range(8, NT)) + list(range(8))
    for pos, e in enumerate(P0_SEQ):
        p1e = P0_SEQ[pos - 1] if pos >= 1 else None
        p2e = P0_SEQ[pos - 2] if pos >= 2 else None
        par = pos % 2
        xb, xc, xn = p0_x[par], p0_xc[par], p0_xn[par]
        src = x_own if e < 8 else x_peer
        r0 = (e % 8) * 128

        pl.op("sync", lambda sy, src=src, r0=r0, xb=xb:
              sy.dma_start(out=xb[:, :], in_=src[r0:r0 + 128, :]),
              waits=[pl.m("xfree%s" % p2e), pl.m("xn%s" % p2e)],
              inc="qx", inc_by=16)
        pl.mark("xload%d" % e, "qx")

        if stage >= 2:
            if e in (8, 10, 12, 14):
                emit_wconv_chunk(wk, wkb, (e - 8) // 2, 0, "xload%d" % e)
            elif e in (9, 11, 13, 15):
                emit_wconv_chunk(wv, wvb, (e - 9) // 2, 1, "xload%d" % e)
            elif e in (0, 1, 2, 3):
                emit_wconv_chunk(wq, wqb, e, 2, "xload%d" % e)

        emit_ln("l1_%d_" % e, xb, xc, xn, xb,
                par,
                w_v1=[pl.m("xload%d" % e), pl.m("slrep"), pl.m("bcast"),
                      pl.m("xn%s" % p2e), pl.m("f1ap%s" % p2e),
                      pl.m("tr_%s_%d" % (p2e, NC - 1)), pl.m("xnst%s" % p2e)])
        pl.marks["xfree%d" % e] = pl.m("l1_%d_st" % e)
        pl.marks["xn%d" % e] = pl.m("l1_%d_ap" % e)

        if e < 8:
            pl.op("gpsimd", lambda g, xn=xn, e=e: g.dma_start(
                out=xnstash[128 * e:128 * e + 128, :], in_=xn[:, :]),
                waits=[pl.m("xn%d" % e)], inc="qb", inc_by=16)
            pl.mark("xnst%d" % e, "qb")

        if e < 8:
            if e >= 1:
                emit_f1acc(e - 1)

            emit_ln("l2_%d_" % e, xn, xc, f1_xf, f1_sgn,
                    2,
                    w_v1=[pl.m("xn%d" % e), pl.m("f1ap%d" % (e - 1))],
                    w_sq=[pl.m("mm%d" % (e - 1))])
            pl.marks["f1ap%d" % e] = pl.m("l2_%d_ap" % e)

            def f_xf(v):
                v.tensor_tensor(f1_xf[:, :], f1_xf[:, :], gam_rep[:, :], ALU.mult)
                return v.tensor_tensor(f1_xf[:, :], f1_xf[:, :], bet_rep[:, :], ALU.add)
            pl.op("vector", f_xf, waits=[pl.m("f1ap%d" % e)], inc="v")
            pl.mark("xf%d" % e, "v")

            pl.op("scalar", lambda s: s.activation(f1_sgn[:, :], f1_xf[:, :], AF.Sign),
                  waits=[pl.m("xf%d" % e)], inc="s")
            pl.mark("sgn%d" % e, "s")

            def f_route(v, e=e):
                v.tensor_tensor(f1_sgn[:, :], f1_sgn[:, :], slope_rep[:, :], ALU.mult)
                v.tensor_reduce(f1_ts[:, :].rearrange("p (a b) -> p a b", b=1),
                                f1_sgn[:, :].rearrange("p (a b) -> p a b", b=32),
                                AX.X, ALU.add)
                v.drain()
                v.tensor_reduce(f1_cs[:, :].rearrange("p (a b) -> p a b", b=1),
                                f1_ts[:, :].rearrange("p (a b) -> p a b", b=8),
                                AX.X, ALU.add)
                v.drain()
                v.reduce_max(c_m1[:, :], f1_cs[:, :], AX.X)
                v.drain()
                v.tensor_tensor(f1_t8[:, :], f1_cs[:, :], c_m1[:, 0:1].to_broadcast([P, 8]), ALU.is_equal)
                v.drain()
                v.scalar_tensor_tensor(f1_t8[:, :], f1_t8[:, :], -BIGI,
                                       c_iota8m[:, :], ALU.mult, ALU.add)
                v.drain()
                v.tensor_reduce(f1_bc[:, :], f1_t8[:, :], AX.X, ALU.min)
                v.drain()
                v.tensor_tensor(f1_e64[:, :], c_clu[:, :], f1_bc[:, 0:1].to_broadcast([P, 64]), ALU.is_equal)
                v.drain()
                v.scalar_tensor_tensor(f1_sc1[:, :], f1_ts[:, :], BIGM,
                                       f1_e64[:, :], ALU.add, ALU.mult)
                v.drain()
                v.reduce_max(c_m1[:, :], f1_sc1[:, :], AX.X)
                v.drain()
                v.tensor_tensor(f1_e64b[:, :], f1_sc1[:, :], c_m1[:, 0:1].to_broadcast([P, 64]), ALU.is_equal)
                v.drain()
                v.scalar_tensor_tensor(f1_e64b[:, :], f1_e64b[:, :], -BIGI,
                                       c_iota64m[:, :], ALU.mult, ALU.add)
                v.drain()
                v.tensor_reduce(s_tidx[:, e:e + 1], f1_e64b[:, :], AX.X, ALU.min)
                v.drain()
                v.tensor_tensor(f1_e64[:, :], c_iota64[:, :], s_tidx[:, e:e + 1].to_broadcast([P, 64]), ALU.is_equal)
                v.tensor_tensor(f1_t7[:, :], s_tidx[:, e:e + 1].to_broadcast([P, 7]), c_cmp8[:, :], ALU.is_ge)
                v.drain()
                v.reduce_sum(c_ft1[:, :], f1_t7[:, :], AX.X)
                v.drain()
                v.scalar_tensor_tensor(c_ft2[:, :], c_ft1[:, :], -8.0, s_tidx[:, e:e + 1], ALU.mult, ALU.add)
                v.drain()
                v.tensor_tensor(s_eqc[:, 8 * e:8 * e + 8], c_iota8r[:, :], c_ft1[:, 0:1].to_broadcast([P, 8]), ALU.is_equal)
                return v.tensor_tensor(s_eqm[:, 8 * e:8 * e + 8], c_iota8r[:, :], c_ft2[:, 0:1].to_broadcast([P, 8]), ALU.is_equal)
            pl.op("vector", f_route, waits=[pl.m("sgn%d" % e)], inc="v")
            pl.mark("route%d" % e, "v")

            def f_gather(v, e=e):
                for c8 in range(8):
                    if c8 == 0:
                        v.tensor_scalar(f1_g2[:, :], f1_xf[:, 0:256], s_eqc[:, 8 * e:8 * e + 1], None, ALU.mult)
                    else:
                        v.scalar_tensor_tensor(f1_g2[:, :], f1_xf[:, 256 * c8:256 * c8 + 256],
                                               s_eqc[:, 8 * e + c8:8 * e + c8 + 1], f1_g2[:, :],
                                               ALU.mult, ALU.add)
                v.drain()
                xsl = s_xsl[:, DSL * e:DSL * e + DSL]
                for m in range(8):
                    v.tensor_scalar(f1_sgn[:, DSL * m:DSL * m + DSL],
                                    f1_g2[:, DSL * m:DSL * m + DSL],
                                    s_eqm[:, 8 * e + m:8 * e + m + 1], None, ALU.mult)
                v.drain()
                v.tensor_reduce(xsl.rearrange("p (d a) -> p d a", a=1),
                                f1_sgn[:, 0:256].rearrange("p (m d) -> p d m", d=DSL),
                                AX.X, ALU.add)
                v.drain()
                v.tensor_reduce(c_mn[:, :], xsl, AX.X, ALU.min)
                v.tensor_reduce(c_mx[:, :], xsl, AX.X, ALU.max)
                v.drain()
                v.tensor_tensor(f1_mm[:, 0:64], f1_e64[:, :], c_mn[:, 0:1].to_broadcast([P, 64]), ALU.mult)
                v.tensor_scalar(f1_e64b[:, :], f1_e64[:, :], -BIGF, BIGF, ALU.mult, ALU.add)
                v.drain()
                v.tensor_tensor(f1_mm[:, 0:64], f1_mm[:, 0:64], f1_e64b[:, :], ALU.add)
                v.tensor_tensor(f1_mm[:, 64:128], f1_e64[:, :], c_mx[:, 0:1].to_broadcast([P, 64]), ALU.mult)
                v.drain()
                v.tensor_scalar(f1_e64b[:, :], f1_e64[:, :], BIGF, -BIGF, ALU.mult, ALU.add)
                v.drain()
                return v.tensor_tensor(f1_mm[:, 64:128], f1_mm[:, 64:128], f1_e64b[:, :], ALU.add)
            pl.op("vector", f_gather, waits=[pl.m("route%d" % e),
                                             pl.m("f1tr%d" % (e - 1))], inc="v")
            pl.mark("mm%d" % e, "v")

            if e >= 1 and stage >= 2:
                emit_group(e - 1)

        for c in range(NC):
            bank = c % 2
            prev = pl.m("trev_%d_%d" % (e, c - 2)) if c >= 2 else \
                pl.m("trev_%s_%d" % (p1e, NC - 2 + c))
            pl.op("tensor", lambda t, xn=xn, c=c, bank=bank:
                  t.transpose(p[bank][:, 0:128], xn[:, 128 * c:128 * c + 128], c_ident[:, :]),
                  waits=[pl.m("xn%d" % e), pl.m("consts"), prev], inc="t")
            pl.mark("tr_%d_%d" % (e, c), "t")

            pl.op("scalar", lambda s, c=c, e=e, bank=bank:
                  s.copy(xnT[:, xcol(c, 128 * e):xcol(c, 128 * e) + 128], p[bank][:, 0:128]),
                  waits=[pl.m("tr_%d_%d" % (e, c))], inc="s")
            pl.mark("trev_%d_%d" % (e, c), "s")
        pl.mark("xnT%d" % e, "s")

    emit_f1acc(7)
    if stage >= 2:
        emit_group(7)

    def f_ccstage(g):
        return [g.dma_start(out=cc_in[0:1, 0:64], in_=f1_acc[0:64, :]),
                g.dma_start(out=cc_in[0:1, 64:128], in_=f1_acc[64:128, :])]
    pl.op("gpsimd", f_ccstage, waits=[pl.m("f1ev7")], inc="qs", inc_by=32, inc_each=True)
    pl.mark("ccstage", "qs")

    pl.op("gpsimd", lambda g: g.collective_compute(
        "AllReduce", ALU.min, replica_groups=[list(range(8))],
        ins=[cc_in.opt()], outs=[cc_out.opt()]),
        waits=[pl.m("ccstage")], inc="g")
    pl.mark("ccdone", "g")



    if stage >= 2:
        pair_last = [None, "kpev15_1"]
        pi = [0]

        def own_mm(wb, sl, qs=(0, 1)):
            b0 = 2 * (pi[0] % 2)
            def f_mm(t, wb=wb, b0=b0):
                r = None
                for c in range(NC):
                    for j, q in enumerate(qs):
                        r = t.matmul(p[b0 + j][:, :], wb[:, 128 * c:128 * c + 128],
                                     xnT[:, xcol(c, 512 * q):xcol(c, 512 * q) + 512],
                                     start=(c == 0), stop=(c == NC - 1))
                return r
            w_mm = [pl.m("wl%d" % sl), pl.m("xnT7"),
                    pl.m(pair_last[pi[0] % 2]) if pair_last[pi[0] % 2] else None]
            pl.op("tensor", f_mm, waits=w_mm, inc="t")
            pl.mark("slcd%d" % sl, "t")
            return b0

        for h in range(H):
            sl = ctr["sl"]; ctr["sl"] += 1
            pl.op("sync", f_wl_mk(wk, h, wbuf[sl % 2]),
                  waits=[pl.m("slcd%d" % (sl - 2))], inc="qw", inc_by=64, inc_each=True)
            pl.mark("wl%d" % sl, "qw")

            w_unp = [pl.m("kpark%d" % h), pl.m("xnT7"), pl.m("xf7")]
            if h == 0:
                w_unp.append(pl.m("ccstage"))
            pl.op("sync", lambda sy, h=h: sy.dma_start(
                out=KT[:, 2048 * h + 1024:2048 * h + 2048],
                in_=kpark[:, 1024 * h:1024 * h + 1024]),
                waits=w_unp, inc="qu", inc_by=16)
            pl.mark("unp%d" % h, "qu")

            b0 = own_mm(wbuf[sl % 2], sl)
            pl.mark("komm%d" % h, "t")

            def f_ev(x, h=h, b0=b0):
                x.copy(KT[:, 2048 * h:2048 * h + 512], p[b0][:, :])
                return x.copy(KT[:, 2048 * h + 512:2048 * h + 1024], p[b0 + 1][:, :])
            we = [pl.m("komm%d" % h), pl.m("unp%d" % h)]
            if h == 0:
                we += [pl.m("xnT7"), pl.m("xf7"), pl.m("ccstage")]
            pl.op("scalar", f_ev, waits=we, inc="s")
            pl.mark("ktevf%d" % h, "s")
            pair_last[pi[0] % 2] = "ktevf%d" % h
            pi[0] += 1

        if not DBG.get("nov"):
            for h in range(H):
                sl = ctr["sl"]; ctr["sl"] += 1
                pl.op("sync", f_wl_mk(wv, h, wbuf[sl % 2]),
                      waits=[pl.m("slcd%d" % (sl - 2))], inc="qw", inc_by=64, inc_each=True)
                pl.mark("wl%d" % sl, "qw")

                b0 = own_mm(wbuf[sl % 2], sl)
                pl.mark("vomm%d" % h, "t")

                def f_sv(x, b0=b0):
                    x.copy(s_vt[:, 0:512], p[b0][:, :])
                    return x.copy(s_vt[:, 512:1024], p[b0 + 1][:, :])
                w_sv = [pl.m("vomm%d" % h),
                        pl.m("votr%d_7" % (h - 1)) if h else pl.m("vptr15_7")]
                if h == 0:
                    w_sv.append(pl.m("kpark15"))
                pl.op("scalar", f_sv, waits=w_sv, inc="s")
                pl.mark("vosv%d" % h, "s")
                pair_last[pi[0] % 2] = "vosv%d" % h
                pi[0] += 1

                tb = 4 + h % 2
                def f_vtr(t, tb=tb):
                    r = None
                    for tt in range(8):
                        r = t.transpose(p[tb][:, 64 * tt:64 * tt + 64].bitcast(BF16)[:, 0:128],
                                        s_vt[:, 128 * tt:128 * tt + 128], c_identb[:, :])
                    return r
                w_tr = [pl.m("vosv%d" % h),
                        pl.m("voevall%d" % (h - 2)) if h >= 2 else
                        pl.m("vpevall%d" % (14 + h % 2))]
                pl.op("tensor", f_vtr, waits=w_tr, inc="t")
                pl.mark("votr%d_7" % h, "t")

                def f_vev(x, h=h, tb=tb):
                    r = None
                    for tt in range(8):
                        r = x.copy(V[:, 2048 * tt + 128 * h:2048 * tt + 128 * h + 128],
                                   p[tb][:, 64 * tt:64 * tt + 64].bitcast(BF16)[:, 0:128])
                    return r
                pl.op("scalar", f_vev, waits=[pl.m("votr%d_7" % h)], inc="s")
                pl.mark("voevall%d" % h, "s")
                for tt in range(8):
                    pl.mark("vev_%d_%d" % (tt, h), "s")

                if not DBG.get("novsuf"):
                    def f_vs(t, h=h):
                        r = None
                        for e in range(NT):
                            r = t.matmul(p[6][:, e:e + 1],
                                         V[:, 2048 * e + 128 * h:2048 * e + 128 * h + 128],
                                         c_onesb[:, :], start=True, stop=True)
                        return r
                    waits = [pl.m("vev_%d_%d" % (e, h)) for e in range(NT)]
                    waits.append(pl.m("vsufev%d" % (h - 1)) if h else pl.m("vpsv15"))
                    pl.op("tensor", f_vs, waits=waits, inc="t")
                    pl.mark("vsufmm%d" % h, "t")

                    def f_vse(v, h=h):
                        v.tensor_copy(s_tsum[:, :], p[6][:, 0:16])
                        v.memset(s_vsuf[:, 8 * h:8 * h + 1], 0.0)
                        r = None
                        for k in range(1, 8):
                            v.drain()
                            v.tensor_tensor(s_vsuf[:, 8 * h + k:8 * h + k + 1],
                                            s_vsuf[:, 8 * h + k - 1:8 * h + k],
                                            s_tsum[:, k - 1:k], ALU.add)
                            v.drain()
                            r = v.tensor_tensor(s_vsuf[:, 8 * h + k:8 * h + k + 1],
                                                s_vsuf[:, 8 * h + k:8 * h + k + 1],
                                                s_tsum[:, 8 + k - 1:8 + k], ALU.add)
                        return r
                    pl.op("vector", f_vse, waits=[pl.m("vsufmm%d" % h)], inc="v")
                    pl.mark("vsufev%d" % h, "v")
            pl.mark("valldone", "s")
        pl.mark("vsufall", "v")

        if not DBG.get("noq"):
            for h in range(H):
                sl = ctr["sl"]; ctr["sl"] += 1
                pl.op("sync", f_wl_mk(wq, h, wbuf[sl % 2]),
                      waits=[pl.m("slcd%d" % (sl - 2)), pl.m("wconv2")],
                      inc="qw", inc_by=64, inc_each=True)
                pl.mark("wl%d" % sl, "qw")

                b0 = own_mm(wbuf[sl % 2], sl)
                pl.mark("qomm%d" % h, "t")

                def f_ev(x, h=h, b0=b0):
                    x.copy(QT[:, 1024 * h:1024 * h + 512], p[b0][:, :])
                    return x.copy(QT[:, 1024 * h + 512:1024 * h + 1024], p[b0 + 1][:, :])
                pl.op("scalar", f_ev, waits=[pl.m("qomm%d" % h)], inc="s")
                pl.mark("qev_%d" % h, "s")
                pair_last[pi[0] % 2] = "qev_%d" % h
                pi[0] += 1

    if stage >= 3:
        for h in range(H):
            def mk_sc(h, e):
                cnt = CNT[e]; n = 128 * cnt; sb_ = 2 * (e % 2)
                def f_sc(t, h=h, e=e, n=n, sb_=sb_):
                    r = t.matmul(p[sb_][:, 0:min(n, 512)],
                                 KT[:, 2048 * h + 128 * e:2048 * h + 128 * e + 128],
                                 QT[:, 1024 * h:1024 * h + min(n, 512)],
                                 start=True, stop=True)
                    if n > 512:
                        r = t.matmul(p[sb_ + 1][:, 0:n - 512],
                                     KT[:, 2048 * h + 128 * e:2048 * h + 128 * e + 128],
                                     QT[:, 1024 * h + 512:1024 * h + n],
                                     start=True, stop=True)
                    return r
                waits = [pl.m("exp_%d_%d" % (h, e - 2)) if e >= 2 else
                         pl.m("exp_%d_%d" % (h - 1, 14 + e)),
                         pl.m("ktevf%d" % h), pl.m("qev_%d" % h)]
                if h == 0 and e < 2:
                    waits += [pl.m("qev_15"), pl.m("valldone"), pl.m("vsufall")]
                pl.op("tensor", f_sc, waits=waits, inc="t")
                pl.mark("sc_%d_%d" % (h, e), "t")

            def mk_exp(h, e):
                cnt = CNT[e]; sb_ = 2 * (e % 2); ptb = pt3[e % 3]
                def f_exp(s, e=e, cnt=cnt, sb_=sb_, ptb=ptb):
                    n = 128 * cnt
                    pwide = pw01 if sb_ == 0 else pw23
                    if e < 8:
                        return s.activation(ptb[:, 0:n], pwide[:, 0:n],
                                            AF.Exp, scale=INV_SQRT_T)
                    dlo = 128 * (cnt - 1)
                    r = None
                    if dlo > 0:
                        r = s.activation(ptb[:, 0:dlo], pwide[:, 0:dlo],
                                         AF.Exp, scale=INV_SQRT_T)
                    r = s.activation(ptb[:, dlo:dlo + 128], pwide[:, dlo:dlo + 128],
                                     AF.Exp, scale=s_pm[:, e - 8:e - 7])
                    return r
                pl.op("scalar", f_exp,
                      waits=[pl.m("sc_%d_%d" % (h, e)),
                             pl.m("av_%d_%d" % (h, e - 3)) if e >= 3 else
                             pl.m("av_%d_%d" % (h - 1, 13 + e)),
                             pl.m("bcast")],
                      inc="s")
                pl.mark("exp_%d_%d" % (h, e), "s")
                if e < 8:
                    def f_fix(v, cnt=cnt, ptb=ptb):
                        dtile = ptb[:, 128 * (cnt - 1):128 * cnt]
                        v.tensor_tensor(dtile, dtile, c_trilb[:, :], ALU.mult)
                        return v.tensor_tensor(dtile, dtile, c_triu1[:, :], ALU.add)
                    pl.op("vector", f_fix, waits=[pl.m("exp_%d_%d" % (h, e))], inc="v")
                    pl.mark("fix_%d_%d" % (h, e), "v")

            def mk_av(h, e):
                cnt = CNT[e]; n = 128 * cnt; ptb = pt3[e % 3]
                def f_av(t, h=h, e=e, n=n, ptb=ptb):
                    vt = V[:, 2048 * e + 128 * h:2048 * e + 128 * h + 128]
                    nlo = min(n, 512)
                    t.matmul(p[4][:, 0:nlo], vt, ptb[:, 0:nlo], start=(e == 0), stop=(e == 15))
                    if n > 512:
                        t.matmul(p[5][:, 0:n - 512], vt, ptb[:, 512:n], start=(e == 4), stop=(e == 15))
                    if e == 15:
                        t.matmul(p[6][0:1, 0:512], c_onesb[:, :], QT[:, 0:512],
                                 start=False, stop=False)
                        t.matmul(p[7][0:1, 0:512], c_onesb[:, :], QT[:, 512:1024],
                                 start=False, stop=False)
                    r = t.matmul(p[6][0:1, 0:nlo], c_onesb[:, :], ptb[:, 0:nlo],
                                 start=(e == 0), stop=(e == 15))
                    if n > 512:
                        r = t.matmul(p[7][0:1, 0:n - 512], c_onesb[:, :], ptb[:, 512:n],
                                     start=(e == 4), stop=(e == 15))
                    return r
                avw = [pl.m("exp_%d_%d" % (h, e))]
                if e == 15:
                    avw.append(pl.m("corrpat"))
                if e < 8:
                    avw.append(pl.m("fix_%d_%d" % (h, e)))
                if e == 0:
                    avw += [pl.m("avadd%d" % (h - 1)), pl.m("zev%d" % (h - 1))]
                pl.op("tensor", f_av, waits=avw, inc="t")
                pl.mark("av_%d_%d" % (h, e), "t")

            def mk_zrec(hz):
                pl.op("sync", lambda sy: sy.dma_start(out=zscr[0:1, :], in_=rz_rep[0:1, :]),
                      waits=[pl.m("zev%d" % hz)], inc="qz", inc_by=16)
                pl.mark("zd3_%d" % hz, "qz")
                pl.op("sync", lambda sy: sy.dma_start(
                    out=rz_rep[:, :], in_=zscr[0:1, :].to_broadcast([P, 1024])),
                    waits=[pl.m("zd3_%d" % hz)], inc="qz", inc_by=16)
                pl.mark("zd4_%d" % hz, "qz")

            def mk_avsc(hz):
                def f_avsc(v, hz=hz):
                    return v.tensor_tensor(attT[:, 1024 * hz:1024 * hz + 1024],
                                           attT[:, 1024 * hz:1024 * hz + 1024],
                                           rz_rep[:, :], ALU.mult)
                pl.op("vector", f_avsc, waits=[pl.m("zd4_%d" % hz), pl.m("avadd%d" % hz)],
                      inc="v")
                pl.mark("avsc%d" % hz, "v")

            mk_sc(h, 0); mk_exp(h, 0)
            mk_sc(h, 1); mk_exp(h, 1)
            mk_sc(h, 2); mk_exp(h, 2)
            for e in range(3, NT):
                mk_sc(h, e)
                mk_av(h, e - 3)
                mk_exp(h, e)
                if e == 5 and h > 0:
                    mk_zrec(h - 1)
                if e == 8 and h > 0:
                    mk_avsc(h - 1)
            mk_av(h, 13)
            mk_av(h, 14)
            if h == 0:
                pl.op("sync", lambda sy: sy.dma_start(out=QT[:, 0:1024], in_=corrpat),
                      waits=[pl.m("sc_0_15")], inc="qz", inc_by=16)
                pl.mark("corrpat", "qz")
            mk_av(h, 15)

            def f_avadd(v, h=h):
                r = None
                for k in range(8):
                    bank = 4 if k < 4 else 5
                    col = 128 * (k % 4)
                    r = v.tensor_scalar(attT[:, 1024 * h + 128 * k:1024 * h + 128 * k + 128],
                                        p[bank][:, col:col + 128],
                                        s_vsuf[:, 8 * h + k:8 * h + k + 1], None, ALU.add)
                return r
            pl.op("vector", f_avadd, waits=[pl.m("av_%d_15" % h)], inc="v")
            pl.mark("avadd%d" % h, "v")

            def f_zev(s):
                s.activation(rz_rep[0:1, 0:512], p[6][0:1, :], AF.Ln)
                s.activation(rz_rep[0:1, 512:1024], p[7][0:1, :], AF.Ln)
                s.drain()
                return s.activation(rz_rep[0:1, :], rz_rep[0:1, :], AF.Exp, scale=-1.0)
            pl.op("scalar", f_zev, waits=[pl.m("av_%d_15" % h), pl.m("zd4_%d" % (h - 1)),
                                          pl.m("avsc%d" % (h - 1))],
                  inc="s")
            pl.mark("zev%d" % h, "s")

            pl.op("gpsimd", lambda g, h=h: g.dma_start(
                out=WoB[:, 2048 * h:2048 * h + 2048], in_=wo[128 * h:128 * h + 128, :]),
                waits=[pl.m("sc_%d_15" % h)], inc="qw", inc_by=16)
            pl.mark("wol%d" % h, "qw")
        mk_zrec(H - 1)
        mk_avsc(H - 1)
        pl.mark("p4done", "v")


    if stage >= 4:
        def f_tabload(g):
            return [g.dma_start(out=c_gthr[:, :], in_=gthrc),
                    g.dma_start(out=s_tab[:, 0:512], in_=bases_i),
                    g.dma_start(out=s_tab[:, 512:1024], in_=slopes_i),
                    g.dma_start(out=s_tab[:, 1026:1027], in_=oscale_i[0:1, :]),
                    g.dma_start(out=s_tab[:, 1024:1025], in_=cc_out[0:1, 0:64]),
                    g.dma_start(out=s_tab[:, 1025:1026], in_=cc_out[0:1, 64:128])]
        pl.op("gpsimd", f_tabload, waits=[pl.m("ccdone"), pl.m("p4done")],
              inc="qs", inc_by=96, inc_each=True)
        pl.mark("ccread", "qs")
        pl.op("vector", lambda v: v.tensor_scalar(s_tab[:, 1025:1026], s_tab[:, 1025:1026],
                                                  -1.0, None, ALU.mult),
              waits=[pl.m("ccread"), pl.m("p4done")], inc="v")
        pl.mark("tabready", "v")

        def emit_ep_pre(i):
            pl.op("sync", lambda sy, i=i: sy.dma_start(
                out=p5_x[i % 2][:, :], in_=x_own[128 * i:128 * i + 128, :]),
                waits=[pl.m("ep_xused%d" % (i - 2)), pl.m("p4done")], inc="qx", inc_by=16)
            pl.mark("xre%d" % i, "qx")

            def f_womm(t, i=i):
                qb = 4 * (i % 2)
                r = None
                for c in range(NC):
                    for q in range(4):
                        r = t.matmul(p[qb + q][:, :],
                                     attT[:, 1024 * c + 128 * i:1024 * c + 128 * i + 128],
                                     WoB[:, 2048 * c + 512 * q:2048 * c + 512 * q + 512],
                                     start=(c == 0), stop=(c == NC - 1))
                return r
            waits = [pl.m("p4done"), pl.m("wol15"), pl.m("woev%d" % (i - 2))]
            if i == 0:
                waits.append(pl.m("exp_15_15"))
            if i == 1:
                waits.append(pl.m("f2ev7"))
            pl.op("tensor", f_womm, waits=waits, inc="t")
            pl.mark("womm%d" % i, "t")

            pl.op("sync", lambda sy, i=i: sy.dma_start(
                out=p5_xn2[i % 2][:, :], in_=xnstash[128 * i:128 * i + 128, :]),
                waits=[pl.m("ep_xused%d" % (i - 2)), pl.m("p4done")], inc="qx", inc_by=16)
            pl.mark("xnld%d" % i, "qx")

        for i in range(8):
            pl.op("vector", lambda v, i=i: v.tensor_tensor(
                f2_e64[:, :], c_iota64[:, :], s_tidx[:, i:i + 1].to_broadcast([P, 64]),
                ALU.is_equal),
                waits=[pl.m("tabready"), pl.m("f2tr%d" % (i - 1))], inc="v")
            pl.mark("f2oh%d" % i, "v")

            pl.op("tensor", lambda t, i=i: t.transpose(
                p[6 + i // 4][0:64, 128 * (i % 4):128 * (i % 4) + 128],
                f2_e64[:, :], c_ident[:, :]),
                waits=[pl.m("f2oh%d" % i)], inc="t")
            pl.mark("f2tr%d" % i, "t")

            pl.op("vector", lambda v, i=i: v.tensor_copy(
                f2_ohT8[:, 128 * i:128 * i + 128],
                p[6 + i // 4][0:64, 128 * (i % 4):128 * (i % 4) + 128]),
                waits=[pl.m("f2tr%d" % i)], inc="v")
            pl.mark("f2ohev%d" % i, "v")

        def f_mn(t):
            r = None
            for i in range(8):
                r = t.matmul(p[5][:, 3 * i:3 * i + 3], f2_ohT8[:, 128 * i:128 * i + 128],
                             s_tab[:, 1024:1027], start=True, stop=True)
            return r
        pl.op("tensor", f_mn, waits=[pl.m("f2ohev7")], inc="t")
        pl.mark("f2mn", "t")
        pl.op("vector", lambda v: v.tensor_copy(s_mn8[:, :], p[5][:, 0:24]),
              waits=[pl.m("f2mn")], inc="v")
        pl.mark("f2mnev", "v")

        for i in range(8):
            bb, sb_b = (4, 5) if i % 2 == 0 else (6, 7)

            def f_gmm(t, i=i, bb=bb, sb_b=sb_b):
                t.matmul(p[bb][:, :], f2_ohT8[:, 128 * i:128 * i + 128],
                         s_tab[:, 0:512], start=True, stop=True)
                return t.matmul(p[sb_b][:, :], f2_ohT8[:, 128 * i:128 * i + 128],
                                s_tab[:, 512:1024], start=True, stop=True)
            w_g = [pl.m("f2mnev"), pl.m("f2ev%d" % (i - 2))]
            pl.op("tensor", f_gmm, waits=w_g, inc="t")
            pl.mark("f2ge%d" % i, "t")

            def f_spl(v, i=i, bb=bb, sb_b=sb_b):
                v.tensor_copy(f2_bsel[:, :], p[bb][:, :])
                v.tensor_copy(f2_ssel[:, :], p[sb_b][:, :])
                v.drain()
                xsl = s_xsl[:, DSL * i:DSL * i + DSL]
                v.tensor_tensor(f2_den[:, :], s_mn8[:, 3 * i + 1:3 * i + 2],
                                s_mn8[:, 3 * i:3 * i + 1], ALU.subtract)
                v.drain()
                v.tensor_scalar(f2_den[:, :], f2_den[:, :], 1e-8, None, ALU.add)
                v.drain()
                v.reciprocal(f2_den[:, :], f2_den[:, :])
                v.tensor_tensor(f2_u[:, :], xsl,
                                s_mn8[:, 3 * i:3 * i + 1].to_broadcast([P, DSL]), ALU.subtract)
                v.drain()
                v.tensor_tensor(f2_u[:, :], f2_u[:, :], f2_den[:, 0:1].to_broadcast([P, DSL]), ALU.mult)
                v.drain()
                v.tensor_scalar(f2_u[:, :], f2_u[:, :], 0.0, CLIP_HI, ALU.max, ALU.min)
                v.drain()
                v.tensor_scalar(f2_w[:, :], f2_u[:, :], float(GRID), None, ALU.mult)
                v.drain()
                v.tensor_tensor(f2_eq16[:, :],
                                f2_w[:, :].rearrange("p (a b) -> p a b", b=1).to_broadcast([P, DSL, GRID]),
                                c_gthr[:, :].rearrange("p (a b) -> p a b", b=GRID), ALU.is_ge)
                v.tensor_reduce(f2_gi[:, :].rearrange("p (a b) -> p a b", b=1),
                                f2_eq16[:, :].rearrange("p (a b) -> p a b", b=GRID),
                                AX.X, ALU.add)
                v.drain()
                v.tensor_scalar(f2_gi[:, :], f2_gi[:, :], 0.0, float(GRID - 1),
                                ALU.max, ALU.min)
                v.drain()
                v.tensor_tensor(f2_xl[:, :], f2_w[:, :], f2_gi[:, :], ALU.subtract)
                v.tensor_scalar(f2_t512[:, 0:DSL], f2_gi[:, :], 1.0, None, ALU.add)
                v.drain()
                v.tensor_tensor(f2_eq16[:, :],
                                f2_t512[:, 0:DSL].rearrange("p (a b) -> p a b", b=1).to_broadcast([P, DSL, GRID]),
                                c_gthr[:, :].rearrange("p (a b) -> p a b", b=GRID), ALU.is_equal)
                v.tensor_tensor(f2_t512[:, :], f2_bsel[:, :], f2_eq16[:, :], ALU.mult)
                v.tensor_reduce(f2_u[:, :].rearrange("p (a b) -> p a b", b=1),
                                f2_t512[:, :].rearrange("p (a b) -> p a b", b=GRID),
                                AX.X, ALU.add)
                v.drain()
                v.tensor_tensor(f2_t512[:, :], f2_ssel[:, :], f2_eq16[:, :], ALU.mult)
                v.tensor_reduce(f2_w[:, :].rearrange("p (a b) -> p a b", b=1),
                                f2_t512[:, :].rearrange("p (a b) -> p a b", b=GRID),
                                AX.X, ALU.add)
                v.drain()
                v.tensor_tensor(f2_xl[:, :], f2_w[:, :], f2_xl[:, :], ALU.mult)
                v.drain()
                v.tensor_tensor(f2_xl[:, :], f2_u[:, :], f2_xl[:, :], ALU.add)
                v.drain()
                return v.tensor_tensor(s_delta[:, DSL * i:DSL * i + DSL], f2_xl[:, :],
                                       s_mn8[:, 3 * i + 2:3 * i + 3].to_broadcast([P, DSL]),
                                       ALU.mult)
            pl.op("vector", f_spl, waits=[pl.m("f2ge%d" % i)], inc="v")
            pl.mark("f2ev%d" % i, "v")
            pl.mark("f2d%d" % i, "v")
            if i == 1:
                emit_ep_pre(0)


        for i in range(8):
            if i > 0:
                emit_ep_pre(i)

            def f_ep(v, i=i):
                ob = p5_o[i % 2]
                qb = 4 * (i % 2)
                for q in range(4):
                    v.tensor_tensor(ob[:, 512 * q:512 * q + 512], p[qb + q][:, :],
                                    p5_x[i % 2][:, 512 * q:512 * q + 512], ALU.add)
                v.tensor_tensor(ob[:, :], ob[:, :], p5_xn2[i % 2][:, :], ALU.add)
                tmp = f2_t512[:, 0:256]
                for m in range(8):
                    v.tensor_scalar(tmp[:, 32 * m:32 * m + 32],
                                    s_delta[:, DSL * i:DSL * i + DSL],
                                    s_eqm[:, 8 * i + m:8 * i + m + 1], None, ALU.mult)
                r = None
                for c8 in range(8):
                    r = v.scalar_tensor_tensor(ob[:, 256 * c8:256 * c8 + 256], tmp,
                                               s_eqc[:, 8 * i + c8:8 * i + c8 + 1],
                                               ob[:, 256 * c8:256 * c8 + 256],
                                               ALU.mult, ALU.add)
                return r
            pl.op("vector", f_ep, waits=[pl.m("xnld%d" % i), pl.m("xre%d" % i),
                                         pl.m("f2d7"), pl.m("womm%d" % i),
                                         pl.m("ep_oused%d" % (i - 2))], inc="v")
            pl.mark("woev%d" % i, "v")
            pl.mark("ep_xused%d" % i, "v")
            pl.mark("epd%d" % i, "v")

            pl.op("sync", lambda sy, i=i: sy.dma_start(
                out=y[128 * i:128 * i + 128, :], in_=p5_o[i % 2][:, :]),
                waits=[pl.m("epd%d" % i)], inc="st", inc_by=16)
            pl.mark("yst%d" % i, "st")
            pl.mark("ep_oused%d" % i, "st")

    specs = {
        "xnT": (xnT, [P, 32768], BF16, "xnT15"),
        "tidx": (s_tidx, [P, 8], F32, "f1ev7"),
        "xsl": (s_xsl, [P, 256], F32, "f1ev7"),
        "eqc": (s_eqc, [P, 64], F32, "f1ev7"),
        "eqm": (s_eqm, [P, 64], F32, "f1ev7"),
        "acc": (f1_acc, [P, 1], F32, "f1ev7"),
        "tab": (s_tab, [64, 1027], F32, "ccread"),
        "KT": (KT, [P, H * 2048], BF16, "ktevf15"),
        "V": (V, [P, NT * 2048], BF16, "valldone"),
        "QT": (QT, [P, H * 1024], BF16, "qev_15"),
        "vsuf": (s_vsuf, [P, H * 8], F32, "vsufall"),
        "attT": (attT, [P, H * 1024], BF16, "p4done"),
        "delta": (s_delta, [P, 256], F32, "f2d7"),
    }
    for name in taps:
        if name not in specs:
            continue
        tn, shape, dt, mk = specs[name]
        if pl.m(mk) is None:
            continue
        ap = dtap(name, shape, dt)
        pl.op("sync", lambda sy, ap=ap, tn=tn: sy.dma_start(out=ap, in_=tn[:, :]),
              waits=[pl.m(mk), pl.at("v"), pl.at("s"), pl.at("t"), pl.at("g")],
              inc="st", inc_by=16)

    pl.op("sync", lambda sy: None, waits=[pl.at("st")])

    _emit(nc, pl)
    return nc, dbg


def _emit(nc, pl):
    import contextlib
    ctx = contextlib.ExitStack()
    semnames = sorted(set(list(pl.cnt.keys()) +
                          [w[0] for ops in pl.ops.values()
                           for (_, ws, _, _, _) in ops for w in ws]))
    sems = {s: ctx.enter_context(nc.semaphore(s)) for s in semnames}
    blk = ctx.enter_context(nc.Block())

    def mkbody(engname):
        ops = pl.ops[engname]
        lastw = {}

        def run(eng):
            for fn, ws, inc, inc_by, inc_each in ops:
                for s, v in ws:
                    if lastw.get(s, -1) < v:
                        eng.wait_ge(sems[s], v)
                        lastw[s] = v
                r = fn(eng)
                if inc is not None:
                    if inc_each:
                        rs = r if isinstance(r, (list, tuple)) else [r]
                        amt = inc_by // len(rs)
                        for ri in rs:
                            ri.then_inc(sems[inc], amt)
                    elif r is None:
                        eng.nop().then_inc(sems[inc], inc_by)
                    else:
                        r.then_inc(sems[inc], inc_by)
        return run

    for name, deco in [("sync", blk.sync), ("vector", blk.vector),
                       ("scalar", blk.scalar), ("tensor", blk.tensor),
                       ("gpsimd", blk.gpsimd)]:
        if pl.ops[name]:
            deco(mkbody(name))
    ctx.close()


def _host_layout(inputs):
    x = np.asarray(inputs["L0_attn_x"])
    per_core = []
    for c in range(8):
        b, par = c // 2, c % 2
        g_own, g_peer = G_PAR[par], G_PAR[1 - par]
        xo = np.concatenate([x[b, 128 * g:128 * g + 128, :] for g in g_own], axis=0)
        xp = np.concatenate([x[b, 128 * g:128 * g + 128, :] for g in g_peer], axis=0)
        pm = np.asarray([1.0 if g_peer[k] < g_own[k] else 0.0 for k in range(8)],
                        dtype=np.float32) * np.float32(INV_SQRT_T)
        per_core.append({
            "x_own": np.ascontiguousarray(xo),
            "x_peer": np.ascontiguousarray(xp),
            "wq": np.asarray(inputs["L0_attn_Wq"]),
            "wk": np.asarray(inputs["L0_attn_Wk"]),
            "wv": np.asarray(inputs["L0_attn_Wv"]),
            "wo": np.asarray(inputs["L0_attn_Wo"]),
            "gamma": np.ascontiguousarray(np.asarray(inputs["L0_ffn_ln_gamma"])[None, :]),
            "beta": np.ascontiguousarray(np.asarray(inputs["L0_ffn_ln_beta"])[None, :]),
            "bases": np.ascontiguousarray(np.asarray(inputs["L0_ffn_spline_bases"]).reshape(64, DSL * GRID)),
            "slopes": np.ascontiguousarray(np.asarray(inputs["L0_ffn_spline_slopes"]).reshape(64, DSL * GRID)),
            "oscale": np.ascontiguousarray(np.asarray(inputs["L0_ffn_output_scale"])[None, :]),
            "pm_scaled": pm[None, :],
        })
    return per_core


def kernel(**inputs):
    in_maps = _host_layout(inputs)
    nc, dbg = build_program()
    res = run_bass_kernel_spmd(nc, in_maps, core_ids=list(range(8)))
    if DBG:
        DBG["results"] = res
    out = np.zeros((4, 2048, D), dtype=np.float32)
    for c in range(8):
        b, par = c // 2, c % 2
        yc = res.results[c]["y"]
        for k, g in enumerate(G_PAR[par]):
            out[b, 128 * g:128 * g + 128, :] = yc[128 * k:128 * k + 128, :]
    return out


# version 35 (final)
# speedup vs baseline: 1.0421x; 1.0386x over previous
import math
from collections import defaultdict
import numpy as np
import ml_dtypes
import concourse.bass as bass
import concourse.mybir as mybir
from concourse.bass_utils import run_bass_kernel_spmd

F32 = mybir.dt.float32
BF16 = mybir.dt.bfloat16
AF = mybir.ActivationFunctionType
ALU = mybir.AluOpType
AX = mybir.AxisListType

D = 2048
H = 16
NT = 16
NC = 16
P = 128
DSL = 32
GRID = 16
INV_SQRT_T = 1.0 / math.sqrt(2048.0)
LN_EPS = 1e-5
CLIP_HI = float(np.float32(1.0) - np.float32(1e-6))
BIGM = 65536.0
BIGI = 4096.0
BIGF = 1e30

E_PAD = [16, 14, 12, 10, 8, 6, 4, 2]
G_PAR = {0: [15, 13, 11, 9, 6, 4, 2, 0], 1: [14, 12, 10, 8, 7, 5, 3, 1]}
CNT = [(e % 8) + 1 for e in range(16)]
C_CORR = [float(2048 - 128 * ek) for ek in E_PAD]

DBG = {}

ENGS = ("sync", "vector", "scalar", "tensor", "gpsimd")


class Plan:
    def __init__(self):
        self.ops = {e: [] for e in ENGS}
        self.cnt = defaultdict(int)
        self.marks = {}

    def op(self, eng, fn, waits=(), inc=None, inc_by=1, inc_each=False):
        ws = [(s, v) for w in waits if w is not None for (s, v) in [w] if v > 0]
        if inc is not None:
            self.cnt[inc] += inc_by
        self.ops[eng].append((fn, ws, inc, inc_by, inc_each))

    def at(self, semname):
        return (semname, self.cnt[semname])

    def mark(self, key, semname):
        self.marks[key] = (semname, self.cnt[semname])

    def m(self, key):
        return self.marks.get(key)


def xcol(c, t):
    if t < 1024:
        return 1024 * c + t
    return 16384 + 1024 * c + (t - 1024)


def build_program():
    stage = DBG.get("stage", 99)
    taps = set(DBG.get("taps", []))

    nc = bass.Bass("TRN2", target_bir_lowering=False,
                   dynamic_dma_scratch_size=16384)

    def din(name, shape, dt=F32):
        return nc.dram_tensor(name, list(shape), dt, kind="ExternalInput").ap()

    x_own = din("x_own", (1024, D))
    x_peer = din("x_peer", (1024, D))
    wq = din("wq", (D, D)); wk = din("wk", (D, D))
    wv = din("wv", (D, D)); wo = din("wo", (D, D))
    gamma = din("gamma", (1, D)); beta = din("beta", (1, D))
    bases_i = din("bases", (64, DSL * GRID))
    slopes_i = din("slopes", (64, DSL * GRID))
    oscale_i = din("oscale", (1, 64))
    pm_i = din("pm_scaled", (1, 8))

    y = nc.dram_tensor("y", [1024, D], F32, kind="ExternalOutput").ap()
    cc_in = nc.dram_tensor("cc_in", [1, 128], F32).ap()
    cc_out = nc.dram_tensor("cc_out", [1, 128], F32).ap()
    zscr = nc.dram_tensor("zscr", [1, 1024], F32).ap()
    kpark = nc.dram_tensor("kpark", [P, H * 1024], BF16).ap()
    xnstash = nc.dram_tensor("xnstash", [8 * P, D], F32).ap()
    wkb = nc.dram_tensor("wkb", [D, D], BF16).ap()
    wvb = nc.dram_tensor("wvb", [D, D], BF16).ap()
    wqb = nc.dram_tensor("wqb", [D, D], BF16).ap()
    slscr = nc.dram_tensor("slscr", [1, D], F32).ap()

    dbg = {}
    def dtap(name, shape, dt=F32):
        if name not in dbg:
            dbg[name] = nc.dram_tensor("dbg_" + name, list(shape), dt,
                                       kind="ExternalOutput").ap()
        return dbg[name]

    def inl(arr, name):
        return nc.inline_tensor(np.ascontiguousarray(arr), name).ap()

    ident_f = inl(np.eye(P, dtype=np.float32), "identfc")
    ident_b = inl(np.eye(P, dtype=np.float32).astype(ml_dtypes.bfloat16), "identbc")
    iota64c = inl(np.broadcast_to(np.arange(64, dtype=np.float32), (P, 64)), "iota64c")
    iota64mc = inl(np.broadcast_to(np.arange(64, dtype=np.float32) + BIGI, (P, 64)), "iota64mc")
    iota8mc = inl(np.broadcast_to(np.arange(8, dtype=np.float32) + BIGI, (P, 8)), "iota8mc")
    iota8rc = inl(np.broadcast_to(np.arange(8, dtype=np.float32), (P, 8)), "iota8rc")
    clurowc = inl(np.broadcast_to(np.arange(64, dtype=np.float32) // 8, (P, 64)), "clurowc")
    cmp8c = inl(np.broadcast_to(8.0 * np.arange(1, 8, dtype=np.float32), (P, 7)), "cmp8c")
    gthrc = inl(np.broadcast_to(np.tile(np.arange(1, GRID + 1, dtype=np.float32), DSL), (P, DSL * GRID)), "gthrc")
    gidxc = inl(np.broadcast_to(np.tile(np.arange(GRID, dtype=np.float32), DSL), (P, DSL * GRID)), "gidxc")
    ccorrc = inl(np.broadcast_to(np.asarray(C_CORR, dtype=np.float32), (P, 8)).copy(), "ccorrc")
    corrpat = inl(np.broadcast_to(
        np.repeat(np.asarray([c / 128.0 for c in C_CORR], dtype=np.float32), 128),
        (P, 1024)).astype(ml_dtypes.bfloat16).copy(), "corrpat")
    onesbc = inl(np.ones((P, 1), dtype=ml_dtypes.bfloat16), "onesbc")
    _st = np.arange(P)[:, None]; _tt = np.arange(128)[None, :]
    trilbc = inl((_tt >= _st).astype(ml_dtypes.bfloat16), "trilbc")
    triu1c = inl((_tt < _st).astype(ml_dtypes.bfloat16), "triu1c")

    SB0 = (nc.sbuf_base + 31) & ~31
    cap = nc.sbuf_top - SB0

    def sb(name, off, shape, dt):
        nbytes = int(np.prod(shape[1:])) * mybir.dt.size(dt)
        assert off % 32 == 0 and off + nbytes <= cap, (name, off, nbytes, cap)
        return nc.alloc_sbuf_tensor_at(name, list(shape), dt, offset=SB0 + off)

    xnT = sb("xnT", 0, [P, 32768], BF16)
    attT = sb("attT", 0, [P, H * 1024], BF16)
    QT = sb("QT", 32768, [P, H * 1024], BF16)
    p5_x = [sb("p5_x0", 32768, [P, D], F32), sb("p5_x1", 40960, [P, D], F32)]
    p5_o = [sb("p5_o0", 49152, [P, D], F32), sb("p5_o1", 57344, [P, D], F32)]
    KT = sb("KT", 65536, [P, H * 2048], BF16)
    WoB = sb("WoB", 65536, [P, NC * 2048], BF16)
    p0_x = [sb("p0_x0", 65536, [P, D], F32), sb("p0_x1", 73728, [P, D], F32)]
    p0_xc = [sb("p0_xc0", 81920, [P, D], F32), sb("p0_xc1", 90112, [P, D], F32)]
    p0_xn = [sb("p0_xn0", 98304, [P, D], F32), sb("p0_xn1", 106496, [P, D], F32)]
    gam_rep = sb("gam_rep", 114688, [P, D], F32)
    bet_rep = sb("bet_rep", 122880, [P, D], F32)
    V = sb("V", 131072, [P, NT * 2048], BF16)
    slope_rep = sb("slope_rep", 131072, [P, D], F32)
    f1_xf = sb("f1_xf", 139264, [P, D], F32)
    f1_ts = sb("f1_ts", 147456, [P, 64], F32)
    f1_sc1 = sb("f1_sc1", 147712, [P, 64], F32)
    f1_cs = sb("f1_cs", 147968, [P, 8], F32)
    f1_t8 = sb("f1_t8", 148000, [P, 8], F32)
    f1_bc = sb("f1_bc", 148032, [P, 1], F32)
    f1_e64 = sb("f1_e64", 148064, [P, 64], F32)
    f1_e64b = sb("f1_e64b", 148320, [P, 64], F32)
    f1_g2 = sb("f1_g2", 148576, [P, 256], F32)
    f1_mm = sb("f1_mm", 149600, [P, 128], F32)
    f1_mme = sb("f1_mme", 150112, [P, 128], F32)
    f1_acc = sb("f1_acc", 150624, [P, 1], F32)
    f1_sgn = sb("f1_sgn", 150656, [P, D], F32)
    f1_t7 = sb("f1_t7", 158848, [P, 7], F32)
    kstg = [sb("kstg0", 158880, [P, 1024], BF16), sb("kstg1", 160928, [P, 1024], BF16)]
    f2_bsel = sb("f2_bsel", 131072, [P, 512], F32)
    f2_ssel = sb("f2_ssel", 133120, [P, 512], F32)
    f2_eq16 = sb("f2_eq16", 135168, [P, 512], F32)
    f2_t512 = sb("f2_t512", 137216, [P, 512], F32)
    f2_mn = sb("f2_mn", 139264, [P, 3], F32)
    f2_u = sb("f2_u", 139296, [P, DSL], F32)
    f2_w = sb("f2_w", 139424, [P, DSL], F32)
    f2_gi = sb("f2_gi", 139552, [P, DSL], F32)
    f2_xl = sb("f2_xl", 139680, [P, DSL], F32)
    f2_den = sb("f2_den", 139808, [P, 1], F32)
    f2_ohT = sb("f2_ohT", 139840, [64, 128], F32)
    f2_e64 = sb("f2_e64", 140352, [P, 64], F32)
    f2_ohT8 = sb("f2_ohT8", 163840, [64, 1024], F32)
    s_mn8 = sb("s_mn8", 167936, [P, 24], F32)
    p5_xc = sb("p5_xc", 140864, [P, D], F32)
    p5_xn = sb("p5_xn", 149056, [P, D], F32)
    p5_xn2 = [sb("p5_xn2a", 140864, [P, D], F32), sb("p5_xn2b", 149056, [P, D], F32)]
    doff = [196608]
    def sbd(name, shape, dt):
        off = doff[0]
        h = sb(name, off, shape, dt)
        doff[0] = (off + int(np.prod(shape[1:])) * mybir.dt.size(dt) + 31) & ~31
        return h

    c_ident = sbd("c_ident", [P, 128], F32)
    c_identb = sbd("c_identb", [P, 128], BF16)
    c_iota64 = sbd("c_iota64", [P, 64], F32)
    c_iota64m = sbd("c_iota64m", [P, 64], F32)
    c_clu = sbd("c_clu", [P, 64], F32)
    c_iota8m = sbd("c_iota8m", [P, 8], F32)
    c_iota8r = sbd("c_iota8r", [P, 8], F32)
    c_cmp8 = sbd("c_cmp8", [P, 7], F32)
    c_onesb = sbd("c_onesb", [P, 1], BF16)
    c_trilb = sbd("c_trilb", [P, 128], BF16)
    c_triu1 = sbd("c_triu1", [P, 128], BF16)
    s_pm = sbd("s_pm", [P, 8], F32)
    s_slm = sb("s_slm", 147456, [64, 32], F32)
    s_tab = sb("s_tab", 157248, [64, 1027], F32)
    c_gthr = sb("c_gthr", 161408, [P, 512], F32)
    s_tidx = sbd("s_tidx", [P, 8], F32)
    s_xsl = sbd("s_xsl", [P, 8 * DSL], F32)
    s_delta = sbd("s_delta", [P, 8 * DSL], F32)
    s_eqc = sbd("s_eqc", [P, 64], F32)
    s_eqm = sbd("s_eqm", [P, 64], F32)
    s_vsuf = sbd("s_vsuf", [P, H * 8], F32)
    s_tsum = sbd("s_tsum", [P, 16], F32)
    s_vt = sbd("s_vt", [P, 1024], BF16)
    c_ccorr = sbd("c_ccorr", [P, 8], F32)
    c_sum4 = sbd("c_sum4", [P, 4], F32)
    c_var4 = sbd("c_var4", [P, 4], F32)
    c_a4 = sbd("c_a4", [P, 4], F32)
    c_lt4 = sbd("c_lt4", [P, 4], F32)
    c_y04 = sbd("c_y04", [P, 4], F32)
    c_mu4 = sbd("c_mu4", [P, 4], F32)
    c_vt14 = sbd("c_vt14", [P, 4], F32)
    c_vt24 = sbd("c_vt24", [P, 4], F32)
    c_ft1 = sbd("c_ft1", [P, 1], F32)
    c_ft2 = sbd("c_ft2", [P, 1], F32)
    c_mn = sbd("c_mn", [P, 1], F32)
    c_mx = sbd("c_mx", [P, 1], F32)
    c_m1 = sbd("c_m1", [P, 1], F32)
    c_fe = sbd("c_fe", [P, 1], F32)
    WB0 = doff[0]
    wbuf = [sbd("wbuf0", [P, D], BF16), sbd("wbuf1", [P, D], BF16)]
    rz_rep = sb("rz_rep", WB0, [P, 1024], F32)
    pt = [sb("pt0", WB0 + 4096, [P, 1024], BF16),
          sb("pt1", WB0 + 6144, [P, 1024], BF16)]
    pt3 = [pt[0], pt[1], s_vt]
    assert doff[0] <= cap, ("D overflow", doff[0], cap)

    pw01 = nc.alloc_psum_tensor("pw01", [P, 1024], F32)
    pw23 = nc.alloc_psum_tensor("pw23", [P, 1024], F32)
    p = [pw01[:, 0:512], pw01[:, 512:1024], pw23[:, 0:512], pw23[:, 512:1024]] + \
        [nc.alloc_psum_tensor("pb%d" % i, [P, 512], F32) for i in range(4, 8)]

    def pbank_bf(i):
        return p[i][:, 0:64].bitcast(BF16)[:, 0:128]

    pl = Plan()

    const_list = [
        (c_ident[:, :], ident_f), (c_identb[:, :], ident_b),
        (c_iota64[:, :], iota64c), (c_iota64m[:, :], iota64mc),
        (c_clu[:, :], clurowc), (c_iota8m[:, :], iota8mc),
        (c_iota8r[:, :], iota8rc), (c_cmp8[:, :], cmp8c),
        (c_onesb[:, :], onesbc), (c_ccorr[:, :], ccorrc),
        (c_trilb[:, :], trilbc), (c_triu1[:, :], triu1c),
        (f1_xf[0:64, 0:512], slopes_i),
    ]
    pl.op("sync", lambda e: [e.dma_start(out=d, in_=s) for d, s in const_list],
          inc="qc", inc_by=16 * len(const_list), inc_each=True)
    pl.mark("consts", "qc")

    bcast_list = [
        (gam_rep[:, :], gamma.to_broadcast([P, D])),
        (bet_rep[:, :], beta.to_broadcast([P, D])),
        (s_pm[:, :], pm_i.to_broadcast([P, 8])),
    ]
    pl.op("gpsimd", lambda e: [e.dma_start(out=d, in_=s) for d, s in bcast_list],
          inc="qb", inc_by=16 * len(bcast_list), inc_each=True)
    pl.mark("bcast", "qb")

    def f_slmean(v):
        v.tensor_reduce(s_slm[:, :].rearrange("p (a b) -> p a b", b=1),
                        f1_xf[0:64, 0:512].rearrange("p (a b) -> p a b", b=GRID),
                        AX.X, ALU.add)
        v.drain()
        return v.tensor_scalar(s_slm[:, :], s_slm[:, :], 1.0 / GRID, None, ALU.mult)
    pl.op("vector", f_slmean, waits=[pl.m("consts")], inc="v")
    pl.mark("slmean", "v")

    pl.op("scalar", lambda s: s.activation(s_slm[:, :], s_slm[:, :], AF.Sign),
          waits=[pl.m("slmean")], inc="s")
    pl.mark("slsign", "s")

    pl.op("gpsimd", lambda g: g.dma_start(out=slscr[0:1, :], in_=s_slm[:, :]),
          waits=[pl.m("slsign")], inc="qb", inc_by=16)
    pl.mark("slflat", "qb")
    pl.op("gpsimd", lambda g: g.dma_start(out=slope_rep[:, :],
                                          in_=slscr[0:1, :].to_broadcast([P, D])),
          waits=[pl.m("slflat")], inc="qb", inc_by=16)
    pl.mark("slrep", "qb")

    def emit_ln_vstats(tag, xbuf, xcbuf, dump, ss, w_v1):
        def f_v1(v, xbuf=xbuf, xcbuf=xcbuf, dump=dump, ss=ss):
            v.reduce_sum(c_sum4[:, ss:ss + 1], xbuf[:, :], AX.X)
            v.drain()
            v.tensor_scalar(c_mu4[:, ss:ss + 1], c_sum4[:, ss:ss + 1], 1.0 / D, None, ALU.mult)
            v.drain()
            v.tensor_tensor(xcbuf[:, :], xbuf[:, :],
                            c_mu4[:, ss:ss + 1].to_broadcast([P, D]), ALU.subtract)
            v.tensor_tensor(dump[:, :], xcbuf[:, :], xcbuf[:, :], ALU.mult)
            v.reduce_sum(c_sum4[:, ss:ss + 1], dump[:, :], AX.X)
            v.drain()
            return v.tensor_scalar(c_a4[:, ss:ss + 1], c_sum4[:, ss:ss + 1],
                                   1.0 / D, LN_EPS, ALU.mult, ALU.add)
        pl.op("vector", f_v1, waits=w_v1, inc="v")
        pl.mark(tag + "st", "v")

        def f_s2(s, ss=ss):
            s.activation(c_lt4[:, ss:ss + 1], c_a4[:, ss:ss + 1], AF.Ln)
            s.drain()
            return s.activation(c_y04[:, ss:ss + 1], c_lt4[:, ss:ss + 1], AF.Exp, scale=-0.5)
        pl.op("scalar", f_s2, waits=[pl.m(tag + "st")], inc="s")
        pl.mark(tag + "md", "s")

        def f_v3(v, ss=ss):
            y0 = c_y04[:, ss:ss + 1]
            v.scalar_tensor_tensor(c_vt14[:, ss:ss + 1], y0, c_a4[:, ss:ss + 1], y0,
                                   ALU.mult, ALU.mult)
            v.drain()
            v.tensor_scalar(c_vt24[:, ss:ss + 1], c_vt14[:, ss:ss + 1],
                            -0.5, 1.5, ALU.mult, ALU.add)
            v.drain()
            return v.tensor_tensor(y0, y0, c_vt24[:, ss:ss + 1], ALU.mult)
        pl.op("vector", f_v3, waits=[pl.m(tag + "md")], inc="v")
        pl.mark(tag + "nw", "v")

    def emit_ln(tag, xbuf, xcbuf, xnbuf, dump, ss, w_v1, w_sq=(), eng_inc=None):
        def f_v1(v, xbuf=xbuf, xcbuf=xcbuf, ss=ss):
            v.reduce_sum(c_sum4[:, ss:ss + 1], xbuf[:, :], AX.X)
            v.drain()
            v.tensor_scalar(c_mu4[:, ss:ss + 1], c_sum4[:, ss:ss + 1], 1.0 / D, None, ALU.mult)
            v.drain()
            return v.tensor_tensor(xcbuf[:, :], xbuf[:, :],
                                   c_mu4[:, ss:ss + 1].to_broadcast([P, D]), ALU.subtract)
        pl.op("vector", f_v1, waits=w_v1, inc="v")
        pl.mark(tag + "xc", "v")

        pl.op("scalar", lambda s, dump=dump, xcbuf=xcbuf: s.square(dump[:, :], xcbuf[:, :]),
              waits=[pl.m(tag + "xc")] + list(w_sq), inc="s")
        pl.mark(tag + "sq", "s")

        def f_v2(v, dump=dump, ss=ss):
            v.reduce_sum(c_sum4[:, ss:ss + 1], dump[:, :], AX.X)
            v.drain()
            return v.tensor_scalar(c_a4[:, ss:ss + 1], c_sum4[:, ss:ss + 1],
                                   1.0 / D, LN_EPS, ALU.mult, ALU.add)
        pl.op("vector", f_v2, waits=[pl.m(tag + "sq")], inc="v")
        pl.mark(tag + "st", "v")

        def f_s2(s, ss=ss):
            s.activation(c_lt4[:, ss:ss + 1], c_a4[:, ss:ss + 1], AF.Ln)
            s.drain()
            return s.activation(c_y04[:, ss:ss + 1], c_lt4[:, ss:ss + 1], AF.Exp, scale=-0.5)
        pl.op("scalar", f_s2, waits=[pl.m(tag + "st")], inc="s")
        pl.mark(tag + "md", "s")

        def f_v3(v, ss=ss):
            y0 = c_y04[:, ss:ss + 1]
            v.scalar_tensor_tensor(c_vt14[:, ss:ss + 1], y0, c_a4[:, ss:ss + 1], y0,
                                   ALU.mult, ALU.mult)
            v.drain()
            v.tensor_scalar(c_vt24[:, ss:ss + 1], c_vt14[:, ss:ss + 1],
                            -0.5, 1.5, ALU.mult, ALU.add)
            v.drain()
            return v.tensor_tensor(y0, y0, c_vt24[:, ss:ss + 1], ALU.mult)
        pl.op("vector", f_v3, waits=[pl.m(tag + "md")], inc="v")
        pl.mark(tag + "nw", "v")

        pl.op("scalar", lambda s, xnbuf=xnbuf, xcbuf=xcbuf, ss=ss:
              s.activation(xnbuf[:, :], xcbuf[:, :], AF.Identity,
                           scale=c_y04[:, ss:ss + 1]),
              waits=[pl.m(tag + "nw")], inc="s")
        pl.mark(tag + "ap", "s")

    ctr = {"sl": 0}

    WCONV = {id(wk): 0, id(wv): 1, id(wq): 2}
    WBF = {id(wk): None, id(wv): None, id(wq): None}

    def f_wl_mk(w, h, wb):
        wb16 = {0: wkb, 1: wvb, 2: wqb}[WCONV[id(w)]]
        def f_wl(gp):
            srcw = wb16.rearrange("(c p) d -> p c d", p=P)[:, :, 128 * h:128 * h + 128]
            dst = wb[:, :].rearrange("p (c d) -> p c d", d=128)
            return [gp.dma_start(out=dst[:, 4 * q:4 * q + 4, :],
                                 in_=srcw[:, 4 * q:4 * q + 4, :]) for q in range(4)]
        return f_wl

    def emit_kpeer(h):
        sl = ctr["sl"]; ctr["sl"] += 1
        wb = wbuf[sl % 2]
        pl.op("sync", f_wl_mk(wk, h, wb), waits=[pl.m("slcd%d" % (sl - 2)),
                                                  pl.m("wconv0")],
              inc="qw", inc_by=64, inc_each=True)
        pl.mark("wl%d" % sl, "qw")
        for half in range(2):
            q = 2 + half
            bank = 2 + half

            def f_mm(t, wb=wb, q=q, bank=bank):
                r = None
                for c in range(NC):
                    r = t.matmul(p[bank][:, :], wb[:, 128 * c:128 * c + 128],
                                 xnT[:, xcol(c, 512 * q):xcol(c, 512 * q) + 512],
                                 start=(c == 0), stop=(c == NC - 1))
                return r
            w_mm = [pl.m("wl%d" % sl), pl.m("xnT15")]
            if half == 0:
                w_mm += [pl.m("f1ev%d" % (h // 2)), pl.m("kpev%d_0" % (h - 1))]
            else:
                w_mm.append(pl.m("kpev%d_1" % (h - 1)))
            pl.op("tensor", f_mm, waits=w_mm, inc="t")
            pl.mark("kpmm%d_%d" % (h, half), "t")
            if half == 1:
                pl.mark("slcd%d" % sl, "t")
        for half in range(2):
            def f_ev(x, h=h, half=half):
                return x.copy(kstg[h % 2][:, 512 * half:512 * half + 512],
                              p[2 + half][:, :])
            we = [pl.m("kpmm%d_%d" % (h, half))]
            if half == 0:
                we.append(pl.m("kpark%d" % (h - 2)))
            pl.op("scalar", f_ev, waits=we, inc="s")
            pl.mark("kpev%d_%d" % (h, half), "s")

        pl.op("gpsimd", lambda g, h=h: g.dma_start(
            out=kpark[:, 1024 * h:1024 * h + 1024], in_=kstg[h % 2][:, :]),
            waits=[pl.m("kpev%d_1" % h)], inc="qk", inc_by=16)
        pl.mark("kpark%d" % h, "qk")

    def emit_vpeer(h):
        sl = ctr["sl"]; ctr["sl"] += 1
        wb = wbuf[sl % 2]
        pl.op("sync", f_wl_mk(wv, h, wb), waits=[pl.m("slcd%d" % (sl - 2)),
                                                  pl.m("wconv1")],
              inc="qw", inc_by=64, inc_each=True)
        pl.mark("wl%d" % sl, "qw")

        def f_mm(t, wb=wb):
            r = None
            for c in range(NC):
                for q in (2, 3):
                    r = t.matmul(p[4 + q][:, :], wb[:, 128 * c:128 * c + 128],
                                 xnT[:, xcol(c, 512 * q):xcol(c, 512 * q) + 512],
                                 start=(c == 0), stop=(c == NC - 1))
            return r
        pl.op("tensor", f_mm, waits=[pl.m("wl%d" % sl), pl.m("xnT15"),
                                     pl.m("vpsv%d" % (h - 1))], inc="t")
        pl.mark("vpmm%d" % h, "t")

        def f_sv(x):
            x.copy(s_vt[:, 0:512], p[6][:, :])
            return x.copy(s_vt[:, 512:1024], p[7][:, :])
        pl.op("scalar", f_sv, waits=[pl.m("vpmm%d" % h), pl.m("vptr%d_7" % (h - 1))],
              inc="s")
        pl.mark("vpsv%d" % h, "s")

        tb = 4 + h % 2
        def f_vtr(t, tb=tb):
            r = None
            for tt in range(8):
                r = t.transpose(p[tb][:, 64 * tt:64 * tt + 64].bitcast(BF16)[:, 0:128],
                                s_vt[:, 128 * tt:128 * tt + 128], c_identb[:, :])
            return r
        pl.op("tensor", f_vtr, waits=[pl.m("vpsv%d" % h), pl.m("vpevall%d" % (h - 2))],
              inc="t")
        pl.mark("vptr%d_7" % h, "t")
        pl.mark("slcd%d" % sl, "t")

        def f_vev(x, h=h, tb=tb):
            r = None
            for tt in range(8):
                r = x.copy(V[:, 2048 * (8 + tt) + 128 * h:2048 * (8 + tt) + 128 * h + 128],
                           p[tb][:, 64 * tt:64 * tt + 64].bitcast(BF16)[:, 0:128])
            return r
        pl.op("scalar", f_vev, waits=[pl.m("vptr%d_7" % h)], inc="s")
        pl.mark("vpevall%d" % h, "s")
        for tt in range(8):
            pl.mark("vev_%d_%d" % (8 + tt, h), "s")

    def emit_f1acc(ef):
        pl.op("tensor", lambda t: t.transpose(p[2][:, 0:128], f1_mm[:, :], c_ident[:, :]),
              waits=[pl.m("mm%d" % ef), pl.m("f1ev%d" % (ef - 1)),
                     pl.m("kpev%d_0" % (2 * ef - 1))], inc="t")
        pl.mark("f1tr%d" % ef, "t")

        def f_f1ev(v, ef=ef):
            v.tensor_copy(f1_mme[:, :], p[2][:, 0:128])
            v.drain()
            if ef == 0:
                v.tensor_reduce(f1_acc[0:64, :], f1_mme[0:64, :], AX.X, ALU.min)
                r = v.tensor_reduce(f1_acc[64:128, :], f1_mme[64:128, :], AX.X, ALU.max)
            else:
                v.tensor_reduce(c_fe[0:64, :], f1_mme[0:64, :], AX.X, ALU.min)
                v.tensor_reduce(c_fe[64:128, :], f1_mme[64:128, :], AX.X, ALU.max)
                v.drain()
                v.tensor_tensor(f1_acc[0:64, :], f1_acc[0:64, :], c_fe[0:64, :], ALU.min)
                r = v.tensor_tensor(f1_acc[64:128, :], f1_acc[64:128, :], c_fe[64:128, :], ALU.max)
            if ef == 7:
                v.drain()
                r = v.tensor_scalar(f1_acc[64:128, :], f1_acc[64:128, :], -1.0, None, ALU.mult)
            return r
        pl.op("vector", f_f1ev, waits=[pl.m("f1tr%d" % ef)], inc="v")
        pl.mark("f1ev%d" % ef, "v")

    def emit_group(eg):
        emit_kpeer(2 * eg)
        if not DBG.get("nov"):
            emit_vpeer(2 * eg)
        emit_kpeer(2 * eg + 1)
        if not DBG.get("nov"):
            emit_vpeer(2 * eg + 1)

    def emit_wconv_chunk(wsrc, wdst, j, wi_, wait_mark):
        pl.op("gpsimd", lambda g, wsrc=wsrc, wdst=wdst, j=j: [g.dma_start(
            out=wdst[512 * j + 256 * s_:512 * j + 256 * s_ + 256, :],
            in_=wsrc[512 * j + 256 * s_:512 * j + 256 * s_ + 256, :]) for s_ in range(2)],
            waits=[pl.m(wait_mark)], inc="qwc", inc_by=32, inc_each=True)
        if j == 3:
            pl.mark("wconv%d" % wi_, "qwc")

    P0_SEQ = list(range(8, NT)) + list(range(8))
    for pos, e in enumerate(P0_SEQ):
        p1e = P0_SEQ[pos - 1] if pos >= 1 else None
        p2e = P0_SEQ[pos - 2] if pos >= 2 else None
        par = pos % 2
        xb, xc, xn = p0_x[par], p0_xc[par], p0_xn[par]
        src = x_own if e < 8 else x_peer
        r0 = (e % 8) * 128

        pl.op("sync", lambda sy, src=src, r0=r0, xb=xb:
              sy.dma_start(out=xb[:, :], in_=src[r0:r0 + 128, :]),
              waits=[pl.m("xfree%s" % p2e), pl.m("xn%s" % p2e)],
              inc="qx", inc_by=16)
        pl.mark("xload%d" % e, "qx")

        if stage >= 2:
            if e in (8, 10, 12, 14):
                emit_wconv_chunk(wk, wkb, (e - 8) // 2, 0, "xload%d" % e)
            elif e in (9, 11, 13, 15):
                emit_wconv_chunk(wv, wvb, (e - 9) // 2, 1, "xload%d" % e)
            elif e in (0, 1, 2, 3):
                emit_wconv_chunk(wq, wqb, e, 2, "xload%d" % e)

        emit_ln_vstats("l1_%d_" % e, xb, xc, xb, par,
                       w_v1=[pl.m("xload%d" % e), pl.m("slrep"), pl.m("bcast"),
                             pl.m("xn%s" % p2e), pl.m("f1ap%s" % p2e),
                             pl.m("tr_%s_%d" % (p2e, NC - 1)), pl.m("xnst%s" % p2e)])
        pl.marks["xfree%d" % e] = pl.m("l1_%d_st" % e)
        pl.op("scalar", lambda s, xn=xn, xc=xc, par=par:
              s.activation(xn[:, :], xc[:, :], AF.Identity,
                           scale=c_y04[:, par:par + 1]),
              waits=[pl.m("l1_%d_nw" % e)], inc="s")
        pl.mark("l1_%d_ap" % e, "s")
        pl.marks["xn%d" % e] = pl.m("l1_%d_ap" % e)

        if e < 8:
            pl.op("gpsimd", lambda g, xn=xn, e=e: g.dma_start(
                out=xnstash[128 * e:128 * e + 128, :], in_=xn[:, :]),
                waits=[pl.m("xn%d" % e)], inc="qb", inc_by=16)
            pl.mark("xnst%d" % e, "qb")

        if e < 8:
            if e >= 1:
                emit_f1acc(e - 1)

            emit_ln("l2_%d_" % e, xn, xc, f1_xf, f1_sgn,
                    2,
                    w_v1=[pl.m("xn%d" % e), pl.m("f1ap%d" % (e - 1))],
                    w_sq=[pl.m("mm%d" % (e - 1))])
            pl.marks["f1ap%d" % e] = pl.m("l2_%d_ap" % e)

            def f_xf(v):
                v.tensor_tensor(f1_xf[:, :], f1_xf[:, :], gam_rep[:, :], ALU.mult)
                return v.tensor_tensor(f1_xf[:, :], f1_xf[:, :], bet_rep[:, :], ALU.add)
            pl.op("vector", f_xf, waits=[pl.m("f1ap%d" % e)], inc="v")
            pl.mark("xf%d" % e, "v")

            pl.op("scalar", lambda s: s.activation(f1_sgn[:, :], f1_xf[:, :], AF.Sign),
                  waits=[pl.m("xf%d" % e)], inc="s")
            pl.mark("sgn%d" % e, "s")

            def f_route(v, e=e):
                v.tensor_tensor(f1_sgn[:, :], f1_sgn[:, :], slope_rep[:, :], ALU.mult)
                v.tensor_reduce(f1_ts[:, :].rearrange("p (a b) -> p a b", b=1),
                                f1_sgn[:, :].rearrange("p (a b) -> p a b", b=32),
                                AX.X, ALU.add)
                v.drain()
                v.tensor_reduce(f1_cs[:, :].rearrange("p (a b) -> p a b", b=1),
                                f1_ts[:, :].rearrange("p (a b) -> p a b", b=8),
                                AX.X, ALU.add)
                v.drain()
                v.reduce_max(c_m1[:, :], f1_cs[:, :], AX.X)
                v.drain()
                v.tensor_tensor(f1_t8[:, :], f1_cs[:, :], c_m1[:, 0:1].to_broadcast([P, 8]), ALU.is_equal)
                v.drain()
                v.scalar_tensor_tensor(f1_t8[:, :], f1_t8[:, :], -BIGI,
                                       c_iota8m[:, :], ALU.mult, ALU.add)
                v.drain()
                v.tensor_reduce(f1_bc[:, :], f1_t8[:, :], AX.X, ALU.min)
                v.drain()
                v.tensor_tensor(f1_e64[:, :], c_clu[:, :], f1_bc[:, 0:1].to_broadcast([P, 64]), ALU.is_equal)
                v.drain()
                v.scalar_tensor_tensor(f1_sc1[:, :], f1_ts[:, :], BIGM,
                                       f1_e64[:, :], ALU.add, ALU.mult)
                v.drain()
                v.reduce_max(c_m1[:, :], f1_sc1[:, :], AX.X)
                v.drain()
                v.tensor_tensor(f1_e64b[:, :], f1_sc1[:, :], c_m1[:, 0:1].to_broadcast([P, 64]), ALU.is_equal)
                v.drain()
                v.scalar_tensor_tensor(f1_e64b[:, :], f1_e64b[:, :], -BIGI,
                                       c_iota64m[:, :], ALU.mult, ALU.add)
                v.drain()
                v.tensor_reduce(s_tidx[:, e:e + 1], f1_e64b[:, :], AX.X, ALU.min)
                v.drain()
                v.tensor_tensor(f1_e64[:, :], c_iota64[:, :], s_tidx[:, e:e + 1].to_broadcast([P, 64]), ALU.is_equal)
                v.tensor_tensor(f1_t7[:, :], s_tidx[:, e:e + 1].to_broadcast([P, 7]), c_cmp8[:, :], ALU.is_ge)
                v.drain()
                v.reduce_sum(c_ft1[:, :], f1_t7[:, :], AX.X)
                v.drain()
                v.scalar_tensor_tensor(c_ft2[:, :], c_ft1[:, :], -8.0, s_tidx[:, e:e + 1], ALU.mult, ALU.add)
                v.drain()
                v.tensor_tensor(s_eqc[:, 8 * e:8 * e + 8], c_iota8r[:, :], c_ft1[:, 0:1].to_broadcast([P, 8]), ALU.is_equal)
                return v.tensor_tensor(s_eqm[:, 8 * e:8 * e + 8], c_iota8r[:, :], c_ft2[:, 0:1].to_broadcast([P, 8]), ALU.is_equal)
            pl.op("vector", f_route, waits=[pl.m("sgn%d" % e)], inc="v")
            pl.mark("route%d" % e, "v")

            def f_gather(v, e=e):
                for c8 in range(8):
                    if c8 == 0:
                        v.tensor_scalar(f1_g2[:, :], f1_xf[:, 0:256], s_eqc[:, 8 * e:8 * e + 1], None, ALU.mult)
                    else:
                        v.scalar_tensor_tensor(f1_g2[:, :], f1_xf[:, 256 * c8:256 * c8 + 256],
                                               s_eqc[:, 8 * e + c8:8 * e + c8 + 1], f1_g2[:, :],
                                               ALU.mult, ALU.add)
                v.drain()
                xsl = s_xsl[:, DSL * e:DSL * e + DSL]
                for m in range(8):
                    v.tensor_scalar(f1_sgn[:, DSL * m:DSL * m + DSL],
                                    f1_g2[:, DSL * m:DSL * m + DSL],
                                    s_eqm[:, 8 * e + m:8 * e + m + 1], None, ALU.mult)
                v.drain()
                v.tensor_reduce(xsl.rearrange("p (d a) -> p d a", a=1),
                                f1_sgn[:, 0:256].rearrange("p (m d) -> p d m", d=DSL),
                                AX.X, ALU.add)
                v.drain()
                v.tensor_reduce(c_mn[:, :], xsl, AX.X, ALU.min)
                v.tensor_reduce(c_mx[:, :], xsl, AX.X, ALU.max)
                v.drain()
                v.tensor_tensor(f1_mm[:, 0:64], f1_e64[:, :], c_mn[:, 0:1].to_broadcast([P, 64]), ALU.mult)
                v.tensor_scalar(f1_e64b[:, :], f1_e64[:, :], -BIGF, BIGF, ALU.mult, ALU.add)
                v.drain()
                v.tensor_tensor(f1_mm[:, 0:64], f1_mm[:, 0:64], f1_e64b[:, :], ALU.add)
                v.tensor_tensor(f1_mm[:, 64:128], f1_e64[:, :], c_mx[:, 0:1].to_broadcast([P, 64]), ALU.mult)
                v.drain()
                v.tensor_scalar(f1_e64b[:, :], f1_e64[:, :], BIGF, -BIGF, ALU.mult, ALU.add)
                v.drain()
                return v.tensor_tensor(f1_mm[:, 64:128], f1_mm[:, 64:128], f1_e64b[:, :], ALU.add)
            pl.op("vector", f_gather, waits=[pl.m("route%d" % e),
                                             pl.m("f1tr%d" % (e - 1))], inc="v")
            pl.mark("mm%d" % e, "v")

            if e >= 1 and stage >= 2:
                emit_group(e - 1)

        for c in range(NC):
            bank = c % 2
            prev = pl.m("trev_%d_%d" % (e, c - 2)) if c >= 2 else \
                pl.m("trev_%s_%d" % (p1e, NC - 2 + c))
            pl.op("tensor", lambda t, xn=xn, c=c, bank=bank:
                  t.transpose(p[bank][:, 0:128], xn[:, 128 * c:128 * c + 128], c_ident[:, :]),
                  waits=[pl.m("xn%d" % e), pl.m("consts"), prev], inc="t")
            pl.mark("tr_%d_%d" % (e, c), "t")

            pl.op("scalar", lambda s, c=c, e=e, bank=bank:
                  s.copy(xnT[:, xcol(c, 128 * e):xcol(c, 128 * e) + 128], p[bank][:, 0:128]),
                  waits=[pl.m("tr_%d_%d" % (e, c))], inc="s")
            pl.mark("trev_%d_%d" % (e, c), "s")
        pl.mark("xnT%d" % e, "s")

    emit_f1acc(7)
    if stage >= 2:
        emit_group(7)

    def f_ccstage(g):
        return [g.dma_start(out=cc_in[0:1, 0:64], in_=f1_acc[0:64, :]),
                g.dma_start(out=cc_in[0:1, 64:128], in_=f1_acc[64:128, :])]
    pl.op("gpsimd", f_ccstage, waits=[pl.m("f1ev7")], inc="qs", inc_by=32, inc_each=True)
    pl.mark("ccstage", "qs")

    pl.op("gpsimd", lambda g: g.collective_compute(
        "AllReduce", ALU.min, replica_groups=[list(range(8))],
        ins=[cc_in.opt()], outs=[cc_out.opt()]),
        waits=[pl.m("ccstage")], inc="g")
    pl.mark("ccdone", "g")



    if stage >= 2:
        pair_last = [None, "kpev15_1"]
        pi = [0]

        def own_mm(wb, sl, qs=(0, 1)):
            b0 = 2 * (pi[0] % 2)
            def f_mm(t, wb=wb, b0=b0):
                r = None
                for c in range(NC):
                    for j, q in enumerate(qs):
                        r = t.matmul(p[b0 + j][:, :], wb[:, 128 * c:128 * c + 128],
                                     xnT[:, xcol(c, 512 * q):xcol(c, 512 * q) + 512],
                                     start=(c == 0), stop=(c == NC - 1))
                return r
            w_mm = [pl.m("wl%d" % sl), pl.m("xnT7"),
                    pl.m(pair_last[pi[0] % 2]) if pair_last[pi[0] % 2] else None]
            pl.op("tensor", f_mm, waits=w_mm, inc="t")
            pl.mark("slcd%d" % sl, "t")
            return b0

        for h in range(H):
            sl = ctr["sl"]; ctr["sl"] += 1
            pl.op("sync", f_wl_mk(wk, h, wbuf[sl % 2]),
                  waits=[pl.m("slcd%d" % (sl - 2))], inc="qw", inc_by=64, inc_each=True)
            pl.mark("wl%d" % sl, "qw")

            w_unp = [pl.m("kpark%d" % h), pl.m("xnT7"), pl.m("xf7")]
            if h == 0:
                w_unp.append(pl.m("ccstage"))
            pl.op("sync", lambda sy, h=h: sy.dma_start(
                out=KT[:, 2048 * h + 1024:2048 * h + 2048],
                in_=kpark[:, 1024 * h:1024 * h + 1024]),
                waits=w_unp, inc="qu", inc_by=16)
            pl.mark("unp%d" % h, "qu")

            b0 = own_mm(wbuf[sl % 2], sl)
            pl.mark("komm%d" % h, "t")

            def f_ev(x, h=h, b0=b0):
                x.copy(KT[:, 2048 * h:2048 * h + 512], p[b0][:, :])
                return x.copy(KT[:, 2048 * h + 512:2048 * h + 1024], p[b0 + 1][:, :])
            we = [pl.m("komm%d" % h), pl.m("unp%d" % h)]
            if h == 0:
                we += [pl.m("xnT7"), pl.m("xf7"), pl.m("ccstage")]
            pl.op("scalar", f_ev, waits=we, inc="s")
            pl.mark("ktevf%d" % h, "s")
            pair_last[pi[0] % 2] = "ktevf%d" % h
            pi[0] += 1

        if not DBG.get("nov"):
            for h in range(H):
                sl = ctr["sl"]; ctr["sl"] += 1
                pl.op("sync", f_wl_mk(wv, h, wbuf[sl % 2]),
                      waits=[pl.m("slcd%d" % (sl - 2))], inc="qw", inc_by=64, inc_each=True)
                pl.mark("wl%d" % sl, "qw")

                b0 = own_mm(wbuf[sl % 2], sl)
                pl.mark("vomm%d" % h, "t")

                def f_sv(x, b0=b0):
                    x.copy(s_vt[:, 0:512], p[b0][:, :])
                    return x.copy(s_vt[:, 512:1024], p[b0 + 1][:, :])
                w_sv = [pl.m("vomm%d" % h),
                        pl.m("votr%d_7" % (h - 1)) if h else pl.m("vptr15_7")]
                if h == 0:
                    w_sv.append(pl.m("kpark15"))
                pl.op("scalar", f_sv, waits=w_sv, inc="s")
                pl.mark("vosv%d" % h, "s")
                pair_last[pi[0] % 2] = "vosv%d" % h
                pi[0] += 1

                tb = 4 + h % 2
                def f_vtr(t, tb=tb):
                    r = None
                    for tt in range(8):
                        r = t.transpose(p[tb][:, 64 * tt:64 * tt + 64].bitcast(BF16)[:, 0:128],
                                        s_vt[:, 128 * tt:128 * tt + 128], c_identb[:, :])
                    return r
                w_tr = [pl.m("vosv%d" % h),
                        pl.m("voevall%d" % (h - 2)) if h >= 2 else
                        pl.m("vpevall%d" % (14 + h % 2))]
                pl.op("tensor", f_vtr, waits=w_tr, inc="t")
                pl.mark("votr%d_7" % h, "t")

                def f_vev(x, h=h, tb=tb):
                    r = None
                    for tt in range(8):
                        r = x.copy(V[:, 2048 * tt + 128 * h:2048 * tt + 128 * h + 128],
                                   p[tb][:, 64 * tt:64 * tt + 64].bitcast(BF16)[:, 0:128])
                    return r
                pl.op("scalar", f_vev, waits=[pl.m("votr%d_7" % h)], inc="s")
                pl.mark("voevall%d" % h, "s")
                for tt in range(8):
                    pl.mark("vev_%d_%d" % (tt, h), "s")

                if not DBG.get("novsuf"):
                    def f_vs(t, h=h):
                        r = None
                        for e in range(NT):
                            r = t.matmul(p[6][:, e:e + 1],
                                         V[:, 2048 * e + 128 * h:2048 * e + 128 * h + 128],
                                         c_onesb[:, :], start=True, stop=True)
                        return r
                    waits = [pl.m("vev_%d_%d" % (e, h)) for e in range(NT)]
                    waits.append(pl.m("vsufev%d" % (h - 1)) if h else pl.m("vpsv15"))
                    pl.op("tensor", f_vs, waits=waits, inc="t")
                    pl.mark("vsufmm%d" % h, "t")

                    def f_vse(v, h=h):
                        v.tensor_copy(s_tsum[:, :], p[6][:, 0:16])
                        v.memset(s_vsuf[:, 8 * h:8 * h + 1], 0.0)
                        r = None
                        for k in range(1, 8):
                            v.drain()
                            v.tensor_tensor(s_vsuf[:, 8 * h + k:8 * h + k + 1],
                                            s_vsuf[:, 8 * h + k - 1:8 * h + k],
                                            s_tsum[:, k - 1:k], ALU.add)
                            v.drain()
                            r = v.tensor_tensor(s_vsuf[:, 8 * h + k:8 * h + k + 1],
                                                s_vsuf[:, 8 * h + k:8 * h + k + 1],
                                                s_tsum[:, 8 + k - 1:8 + k], ALU.add)
                        return r
                    pl.op("vector", f_vse, waits=[pl.m("vsufmm%d" % h)], inc="v")
                    pl.mark("vsufev%d" % h, "v")
            pl.mark("valldone", "s")
        pl.mark("vsufall", "v")

        if not DBG.get("noq"):
            for h in range(H):
                sl = ctr["sl"]; ctr["sl"] += 1
                pl.op("sync", f_wl_mk(wq, h, wbuf[sl % 2]),
                      waits=[pl.m("slcd%d" % (sl - 2)), pl.m("wconv2")],
                      inc="qw", inc_by=64, inc_each=True)
                pl.mark("wl%d" % sl, "qw")

                b0 = own_mm(wbuf[sl % 2], sl)
                pl.mark("qomm%d" % h, "t")

                def f_ev(x, h=h, b0=b0):
                    x.copy(QT[:, 1024 * h:1024 * h + 512], p[b0][:, :])
                    return x.copy(QT[:, 1024 * h + 512:1024 * h + 1024], p[b0 + 1][:, :])
                pl.op("scalar", f_ev, waits=[pl.m("qomm%d" % h)], inc="s")
                pl.mark("qev_%d" % h, "s")
                pair_last[pi[0] % 2] = "qev_%d" % h
                pi[0] += 1

    if stage >= 3:
        for h in range(H):
            def mk_sc(h, e):
                cnt = CNT[e]; n = 128 * cnt; sb_ = 2 * (e % 2)
                def f_sc(t, h=h, e=e, n=n, sb_=sb_):
                    r = t.matmul(p[sb_][:, 0:min(n, 512)],
                                 KT[:, 2048 * h + 128 * e:2048 * h + 128 * e + 128],
                                 QT[:, 1024 * h:1024 * h + min(n, 512)],
                                 start=True, stop=True)
                    if n > 512:
                        r = t.matmul(p[sb_ + 1][:, 0:n - 512],
                                     KT[:, 2048 * h + 128 * e:2048 * h + 128 * e + 128],
                                     QT[:, 1024 * h + 512:1024 * h + n],
                                     start=True, stop=True)
                    return r
                waits = [pl.m("exp_%d_%d" % (h, e - 2)) if e >= 2 else
                         pl.m("exp_%d_%d" % (h - 1, 14 + e)),
                         pl.m("ktevf%d" % h), pl.m("qev_%d" % h)]
                if h == 0 and e < 2:
                    waits += [pl.m("qev_15"), pl.m("valldone"), pl.m("vsufall")]
                pl.op("tensor", f_sc, waits=waits, inc="t")
                pl.mark("sc_%d_%d" % (h, e), "t")

            def mk_exp(h, e):
                cnt = CNT[e]; sb_ = 2 * (e % 2); ptb = pt3[e % 3]
                def f_exp(s, e=e, cnt=cnt, sb_=sb_, ptb=ptb):
                    n = 128 * cnt
                    pwide = pw01 if sb_ == 0 else pw23
                    if e < 8:
                        return s.activation(ptb[:, 0:n], pwide[:, 0:n],
                                            AF.Exp, scale=INV_SQRT_T)
                    dlo = 128 * (cnt - 1)
                    r = None
                    if dlo > 0:
                        r = s.activation(ptb[:, 0:dlo], pwide[:, 0:dlo],
                                         AF.Exp, scale=INV_SQRT_T)
                    r = s.activation(ptb[:, dlo:dlo + 128], pwide[:, dlo:dlo + 128],
                                     AF.Exp, scale=s_pm[:, e - 8:e - 7])
                    return r
                pl.op("scalar", f_exp,
                      waits=[pl.m("sc_%d_%d" % (h, e)),
                             pl.m("av_%d_%d" % (h, e - 3)) if e >= 3 else
                             pl.m("av_%d_%d" % (h - 1, 13 + e)),
                             pl.m("bcast")],
                      inc="s")
                pl.mark("exp_%d_%d" % (h, e), "s")
                if e < 8:
                    def f_fix(v, cnt=cnt, ptb=ptb):
                        dtile = ptb[:, 128 * (cnt - 1):128 * cnt]
                        v.tensor_tensor(dtile, dtile, c_trilb[:, :], ALU.mult)
                        return v.tensor_tensor(dtile, dtile, c_triu1[:, :], ALU.add)
                    pl.op("vector", f_fix, waits=[pl.m("exp_%d_%d" % (h, e))], inc="v")
                    pl.mark("fix_%d_%d" % (h, e), "v")

            def mk_av(h, e):
                cnt = CNT[e]; n = 128 * cnt; ptb = pt3[e % 3]
                def f_av(t, h=h, e=e, n=n, ptb=ptb):
                    vt = V[:, 2048 * e + 128 * h:2048 * e + 128 * h + 128]
                    nlo = min(n, 512)
                    t.matmul(p[4][:, 0:nlo], vt, ptb[:, 0:nlo], start=(e == 0), stop=(e == 15))
                    if n > 512:
                        t.matmul(p[5][:, 0:n - 512], vt, ptb[:, 512:n], start=(e == 4), stop=(e == 15))
                    if e == 15:
                        t.matmul(p[6][0:1, 0:512], c_onesb[:, :], QT[:, 0:512],
                                 start=False, stop=False)
                        t.matmul(p[7][0:1, 0:512], c_onesb[:, :], QT[:, 512:1024],
                                 start=False, stop=False)
                    r = t.matmul(p[6][0:1, 0:nlo], c_onesb[:, :], ptb[:, 0:nlo],
                                 start=(e == 0), stop=(e == 15))
                    if n > 512:
                        r = t.matmul(p[7][0:1, 0:n - 512], c_onesb[:, :], ptb[:, 512:n],
                                     start=(e == 4), stop=(e == 15))
                    return r
                avw = [pl.m("exp_%d_%d" % (h, e))]
                if e == 15:
                    avw.append(pl.m("corrpat"))
                if e < 8:
                    avw.append(pl.m("fix_%d_%d" % (h, e)))
                if e == 0:
                    avw += [pl.m("avadd%d" % (h - 1)), pl.m("zev%d" % (h - 1))]
                pl.op("tensor", f_av, waits=avw, inc="t")
                pl.mark("av_%d_%d" % (h, e), "t")

            def mk_zrec(hz):
                pl.op("sync", lambda sy: sy.dma_start(out=zscr[0:1, :], in_=rz_rep[0:1, :]),
                      waits=[pl.m("zev%d" % hz)], inc="qz", inc_by=16)
                pl.mark("zd3_%d" % hz, "qz")
                pl.op("sync", lambda sy: sy.dma_start(
                    out=rz_rep[:, :], in_=zscr[0:1, :].to_broadcast([P, 1024])),
                    waits=[pl.m("zd3_%d" % hz)], inc="qz", inc_by=16)
                pl.mark("zd4_%d" % hz, "qz")

            def mk_avsc(hz):
                def f_avsc(v, hz=hz):
                    return v.tensor_tensor(attT[:, 1024 * hz:1024 * hz + 1024],
                                           attT[:, 1024 * hz:1024 * hz + 1024],
                                           rz_rep[:, :], ALU.mult)
                pl.op("vector", f_avsc, waits=[pl.m("zd4_%d" % hz), pl.m("avadd%d" % hz)],
                      inc="v")
                pl.mark("avsc%d" % hz, "v")

            mk_sc(h, 0); mk_exp(h, 0)
            mk_sc(h, 1); mk_exp(h, 1)
            mk_sc(h, 2); mk_exp(h, 2)
            for e in range(3, NT):
                mk_sc(h, e)
                mk_av(h, e - 3)
                mk_exp(h, e)
                if e == 5 and h > 0:
                    mk_zrec(h - 1)
                if e == 8 and h > 0:
                    mk_avsc(h - 1)
            mk_av(h, 13)
            mk_av(h, 14)
            if h == 0:
                pl.op("sync", lambda sy: sy.dma_start(out=QT[:, 0:1024], in_=corrpat),
                      waits=[pl.m("sc_0_15")], inc="qz", inc_by=16)
                pl.mark("corrpat", "qz")
            mk_av(h, 15)

            def f_avadd(v, h=h):
                r = None
                for k in range(8):
                    bank = 4 if k < 4 else 5
                    col = 128 * (k % 4)
                    r = v.tensor_scalar(attT[:, 1024 * h + 128 * k:1024 * h + 128 * k + 128],
                                        p[bank][:, col:col + 128],
                                        s_vsuf[:, 8 * h + k:8 * h + k + 1], None, ALU.add)
                return r
            pl.op("vector", f_avadd, waits=[pl.m("av_%d_15" % h)], inc="v")
            pl.mark("avadd%d" % h, "v")

            def f_zev(s):
                s.activation(rz_rep[0:1, 0:512], p[6][0:1, :], AF.Ln)
                s.activation(rz_rep[0:1, 512:1024], p[7][0:1, :], AF.Ln)
                s.drain()
                return s.activation(rz_rep[0:1, :], rz_rep[0:1, :], AF.Exp, scale=-1.0)
            pl.op("scalar", f_zev, waits=[pl.m("av_%d_15" % h), pl.m("zd4_%d" % (h - 1)),
                                          pl.m("avsc%d" % (h - 1))],
                  inc="s")
            pl.mark("zev%d" % h, "s")

            pl.op("gpsimd", lambda g, h=h: g.dma_start(
                out=WoB[:, 2048 * h:2048 * h + 2048], in_=wo[128 * h:128 * h + 128, :]),
                waits=[pl.m("sc_%d_15" % h)], inc="qw", inc_by=16)
            pl.mark("wol%d" % h, "qw")
        mk_zrec(H - 1)
        mk_avsc(H - 1)
        pl.mark("p4done", "v")


    if stage >= 4:
        def f_tabload(g):
            return [g.dma_start(out=c_gthr[:, :], in_=gthrc),
                    g.dma_start(out=s_tab[:, 0:512], in_=bases_i),
                    g.dma_start(out=s_tab[:, 512:1024], in_=slopes_i),
                    g.dma_start(out=s_tab[:, 1026:1027], in_=oscale_i[0:1, :]),
                    g.dma_start(out=s_tab[:, 1024:1025], in_=cc_out[0:1, 0:64]),
                    g.dma_start(out=s_tab[:, 1025:1026], in_=cc_out[0:1, 64:128])]
        pl.op("gpsimd", f_tabload, waits=[pl.m("ccdone"), pl.m("p4done")],
              inc="qs", inc_by=96, inc_each=True)
        pl.mark("ccread", "qs")
        pl.op("vector", lambda v: v.tensor_scalar(s_tab[:, 1025:1026], s_tab[:, 1025:1026],
                                                  -1.0, None, ALU.mult),
              waits=[pl.m("ccread"), pl.m("p4done")], inc="v")
        pl.mark("tabready", "v")

        def emit_ep_pre(i):
            pl.op("sync", lambda sy, i=i: sy.dma_start(
                out=p5_x[i % 2][:, :], in_=x_own[128 * i:128 * i + 128, :]),
                waits=[pl.m("ep_xused%d" % (i - 2)), pl.m("p4done")], inc="qx", inc_by=16)
            pl.mark("xre%d" % i, "qx")

            def f_womm(t, i=i):
                qb = 4 * (i % 2)
                r = None
                for c in range(NC):
                    for q in range(4):
                        r = t.matmul(p[qb + q][:, :],
                                     attT[:, 1024 * c + 128 * i:1024 * c + 128 * i + 128],
                                     WoB[:, 2048 * c + 512 * q:2048 * c + 512 * q + 512],
                                     start=(c == 0), stop=(c == NC - 1))
                return r
            waits = [pl.m("p4done"), pl.m("wol15"), pl.m("woev%d" % (i - 2))]
            if i == 0:
                waits.append(pl.m("exp_15_15"))
            if i == 1:
                waits.append(pl.m("f2ev7"))
            pl.op("tensor", f_womm, waits=waits, inc="t")
            pl.mark("womm%d" % i, "t")

            pl.op("sync", lambda sy, i=i: sy.dma_start(
                out=p5_xn2[i % 2][:, :], in_=xnstash[128 * i:128 * i + 128, :]),
                waits=[pl.m("ep_xused%d" % (i - 2)), pl.m("p4done")], inc="qx", inc_by=16)
            pl.mark("xnld%d" % i, "qx")

        for i in range(8):
            pl.op("vector", lambda v, i=i: v.tensor_tensor(
                f2_e64[:, :], c_iota64[:, :], s_tidx[:, i:i + 1].to_broadcast([P, 64]),
                ALU.is_equal),
                waits=[pl.m("tabready"), pl.m("f2tr%d" % (i - 1))], inc="v")
            pl.mark("f2oh%d" % i, "v")

            pl.op("tensor", lambda t, i=i: t.transpose(
                p[6 + i // 4][0:64, 128 * (i % 4):128 * (i % 4) + 128],
                f2_e64[:, :], c_ident[:, :]),
                waits=[pl.m("f2oh%d" % i)], inc="t")
            pl.mark("f2tr%d" % i, "t")

            pl.op("vector", lambda v, i=i: v.tensor_copy(
                f2_ohT8[:, 128 * i:128 * i + 128],
                p[6 + i // 4][0:64, 128 * (i % 4):128 * (i % 4) + 128]),
                waits=[pl.m("f2tr%d" % i)], inc="v")
            pl.mark("f2ohev%d" % i, "v")

        def f_mn(t):
            r = None
            for i in range(8):
                r = t.matmul(p[5][:, 3 * i:3 * i + 3], f2_ohT8[:, 128 * i:128 * i + 128],
                             s_tab[:, 1024:1027], start=True, stop=True)
            return r
        pl.op("tensor", f_mn, waits=[pl.m("f2ohev7")], inc="t")
        pl.mark("f2mn", "t")
        pl.op("vector", lambda v: v.tensor_copy(s_mn8[:, :], p[5][:, 0:24]),
              waits=[pl.m("f2mn")], inc="v")
        pl.mark("f2mnev", "v")

        for i in range(8):
            bb, sb_b = (4, 5) if i % 2 == 0 else (6, 7)

            def f_gmm(t, i=i, bb=bb, sb_b=sb_b):
                t.matmul(p[bb][:, :], f2_ohT8[:, 128 * i:128 * i + 128],
                         s_tab[:, 0:512], start=True, stop=True)
                return t.matmul(p[sb_b][:, :], f2_ohT8[:, 128 * i:128 * i + 128],
                                s_tab[:, 512:1024], start=True, stop=True)
            w_g = [pl.m("f2mnev"), pl.m("f2ev%d" % (i - 2))]
            pl.op("tensor", f_gmm, waits=w_g, inc="t")
            pl.mark("f2ge%d" % i, "t")

            def f_spl(v, i=i, bb=bb, sb_b=sb_b):
                v.tensor_copy(f2_bsel[:, :], p[bb][:, :])
                v.tensor_copy(f2_ssel[:, :], p[sb_b][:, :])
                v.drain()
                xsl = s_xsl[:, DSL * i:DSL * i + DSL]
                v.tensor_tensor(f2_den[:, :], s_mn8[:, 3 * i + 1:3 * i + 2],
                                s_mn8[:, 3 * i:3 * i + 1], ALU.subtract)
                v.drain()
                v.tensor_scalar(f2_den[:, :], f2_den[:, :], 1e-8, None, ALU.add)
                v.drain()
                v.reciprocal(f2_den[:, :], f2_den[:, :])
                v.tensor_tensor(f2_u[:, :], xsl,
                                s_mn8[:, 3 * i:3 * i + 1].to_broadcast([P, DSL]), ALU.subtract)
                v.drain()
                v.tensor_tensor(f2_u[:, :], f2_u[:, :], f2_den[:, 0:1].to_broadcast([P, DSL]), ALU.mult)
                v.drain()
                v.tensor_scalar(f2_u[:, :], f2_u[:, :], 0.0, CLIP_HI, ALU.max, ALU.min)
                v.drain()
                v.tensor_scalar(f2_w[:, :], f2_u[:, :], float(GRID), None, ALU.mult)
                v.drain()
                v.tensor_tensor(f2_eq16[:, :],
                                f2_w[:, :].rearrange("p (a b) -> p a b", b=1).to_broadcast([P, DSL, GRID]),
                                c_gthr[:, :].rearrange("p (a b) -> p a b", b=GRID), ALU.is_ge)
                v.tensor_reduce(f2_gi[:, :].rearrange("p (a b) -> p a b", b=1),
                                f2_eq16[:, :].rearrange("p (a b) -> p a b", b=GRID),
                                AX.X, ALU.add)
                v.drain()
                v.tensor_scalar(f2_gi[:, :], f2_gi[:, :], 0.0, float(GRID - 1),
                                ALU.max, ALU.min)
                v.drain()
                v.tensor_tensor(f2_xl[:, :], f2_w[:, :], f2_gi[:, :], ALU.subtract)
                v.tensor_scalar(f2_t512[:, 0:DSL], f2_gi[:, :], 1.0, None, ALU.add)
                v.drain()
                v.tensor_tensor(f2_eq16[:, :],
                                f2_t512[:, 0:DSL].rearrange("p (a b) -> p a b", b=1).to_broadcast([P, DSL, GRID]),
                                c_gthr[:, :].rearrange("p (a b) -> p a b", b=GRID), ALU.is_equal)
                v.tensor_tensor(f2_t512[:, :], f2_bsel[:, :], f2_eq16[:, :], ALU.mult)
                v.tensor_reduce(f2_u[:, :].rearrange("p (a b) -> p a b", b=1),
                                f2_t512[:, :].rearrange("p (a b) -> p a b", b=GRID),
                                AX.X, ALU.add)
                v.drain()
                v.tensor_tensor(f2_t512[:, :], f2_ssel[:, :], f2_eq16[:, :], ALU.mult)
                v.tensor_reduce(f2_w[:, :].rearrange("p (a b) -> p a b", b=1),
                                f2_t512[:, :].rearrange("p (a b) -> p a b", b=GRID),
                                AX.X, ALU.add)
                v.drain()
                v.tensor_tensor(f2_xl[:, :], f2_w[:, :], f2_xl[:, :], ALU.mult)
                v.drain()
                v.tensor_tensor(f2_xl[:, :], f2_u[:, :], f2_xl[:, :], ALU.add)
                v.drain()
                return v.tensor_tensor(s_delta[:, DSL * i:DSL * i + DSL], f2_xl[:, :],
                                       s_mn8[:, 3 * i + 2:3 * i + 3].to_broadcast([P, DSL]),
                                       ALU.mult)
            pl.op("vector", f_spl, waits=[pl.m("f2ge%d" % i)], inc="v")
            pl.mark("f2ev%d" % i, "v")
            pl.mark("f2d%d" % i, "v")
            if i == 1:
                emit_ep_pre(0)


        for i in range(8):
            if i > 0:
                emit_ep_pre(i)

            def f_ep(v, i=i):
                ob = p5_o[i % 2]
                qb = 4 * (i % 2)
                for q in range(4):
                    v.tensor_tensor(ob[:, 512 * q:512 * q + 512], p[qb + q][:, :],
                                    p5_x[i % 2][:, 512 * q:512 * q + 512], ALU.add)
                v.tensor_tensor(ob[:, :], ob[:, :], p5_xn2[i % 2][:, :], ALU.add)
                tmp = f2_t512[:, 0:256]
                for m in range(8):
                    v.tensor_scalar(tmp[:, 32 * m:32 * m + 32],
                                    s_delta[:, DSL * i:DSL * i + DSL],
                                    s_eqm[:, 8 * i + m:8 * i + m + 1], None, ALU.mult)
                r = None
                for c8 in range(8):
                    r = v.scalar_tensor_tensor(ob[:, 256 * c8:256 * c8 + 256], tmp,
                                               s_eqc[:, 8 * i + c8:8 * i + c8 + 1],
                                               ob[:, 256 * c8:256 * c8 + 256],
                                               ALU.mult, ALU.add)
                return r
            pl.op("vector", f_ep, waits=[pl.m("xnld%d" % i), pl.m("xre%d" % i),
                                         pl.m("f2d7"), pl.m("womm%d" % i),
                                         pl.m("ep_oused%d" % (i - 2))], inc="v")
            pl.mark("woev%d" % i, "v")
            pl.mark("ep_xused%d" % i, "v")
            pl.mark("epd%d" % i, "v")

            pl.op("sync", lambda sy, i=i: sy.dma_start(
                out=y[128 * i:128 * i + 128, :], in_=p5_o[i % 2][:, :]),
                waits=[pl.m("epd%d" % i)], inc="st", inc_by=16)
            pl.mark("yst%d" % i, "st")
            pl.mark("ep_oused%d" % i, "st")

    specs = {
        "xnT": (xnT, [P, 32768], BF16, "xnT15"),
        "tidx": (s_tidx, [P, 8], F32, "f1ev7"),
        "xsl": (s_xsl, [P, 256], F32, "f1ev7"),
        "eqc": (s_eqc, [P, 64], F32, "f1ev7"),
        "eqm": (s_eqm, [P, 64], F32, "f1ev7"),
        "acc": (f1_acc, [P, 1], F32, "f1ev7"),
        "tab": (s_tab, [64, 1027], F32, "ccread"),
        "KT": (KT, [P, H * 2048], BF16, "ktevf15"),
        "V": (V, [P, NT * 2048], BF16, "valldone"),
        "QT": (QT, [P, H * 1024], BF16, "qev_15"),
        "vsuf": (s_vsuf, [P, H * 8], F32, "vsufall"),
        "attT": (attT, [P, H * 1024], BF16, "p4done"),
        "delta": (s_delta, [P, 256], F32, "f2d7"),
    }
    for name in taps:
        if name not in specs:
            continue
        tn, shape, dt, mk = specs[name]
        if pl.m(mk) is None:
            continue
        ap = dtap(name, shape, dt)
        pl.op("sync", lambda sy, ap=ap, tn=tn: sy.dma_start(out=ap, in_=tn[:, :]),
              waits=[pl.m(mk), pl.at("v"), pl.at("s"), pl.at("t"), pl.at("g")],
              inc="st", inc_by=16)

    pl.op("sync", lambda sy: None, waits=[pl.at("st")])

    _emit(nc, pl)
    return nc, dbg


def _emit(nc, pl):
    import contextlib
    ctx = contextlib.ExitStack()
    semnames = sorted(set(list(pl.cnt.keys()) +
                          [w[0] for ops in pl.ops.values()
                           for (_, ws, _, _, _) in ops for w in ws]))
    sems = {s: ctx.enter_context(nc.semaphore(s)) for s in semnames}
    blk = ctx.enter_context(nc.Block())

    def mkbody(engname):
        ops = pl.ops[engname]
        lastw = {}

        def run(eng):
            for fn, ws, inc, inc_by, inc_each in ops:
                for s, v in ws:
                    if lastw.get(s, -1) < v:
                        eng.wait_ge(sems[s], v)
                        lastw[s] = v
                r = fn(eng)
                if inc is not None:
                    if inc_each:
                        rs = r if isinstance(r, (list, tuple)) else [r]
                        amt = inc_by // len(rs)
                        for ri in rs:
                            ri.then_inc(sems[inc], amt)
                    elif r is None:
                        eng.nop().then_inc(sems[inc], inc_by)
                    else:
                        r.then_inc(sems[inc], inc_by)
        return run

    for name, deco in [("sync", blk.sync), ("vector", blk.vector),
                       ("scalar", blk.scalar), ("tensor", blk.tensor),
                       ("gpsimd", blk.gpsimd)]:
        if pl.ops[name]:
            deco(mkbody(name))
    ctx.close()


def _host_layout(inputs):
    x = np.asarray(inputs["L0_attn_x"])
    per_core = []
    for c in range(8):
        b, par = c // 2, c % 2
        g_own, g_peer = G_PAR[par], G_PAR[1 - par]
        xo = np.concatenate([x[b, 128 * g:128 * g + 128, :] for g in g_own], axis=0)
        xp = np.concatenate([x[b, 128 * g:128 * g + 128, :] for g in g_peer], axis=0)
        pm = np.asarray([1.0 if g_peer[k] < g_own[k] else 0.0 for k in range(8)],
                        dtype=np.float32) * np.float32(INV_SQRT_T)
        per_core.append({
            "x_own": np.ascontiguousarray(xo),
            "x_peer": np.ascontiguousarray(xp),
            "wq": np.asarray(inputs["L0_attn_Wq"]),
            "wk": np.asarray(inputs["L0_attn_Wk"]),
            "wv": np.asarray(inputs["L0_attn_Wv"]),
            "wo": np.asarray(inputs["L0_attn_Wo"]),
            "gamma": np.ascontiguousarray(np.asarray(inputs["L0_ffn_ln_gamma"])[None, :]),
            "beta": np.ascontiguousarray(np.asarray(inputs["L0_ffn_ln_beta"])[None, :]),
            "bases": np.ascontiguousarray(np.asarray(inputs["L0_ffn_spline_bases"]).reshape(64, DSL * GRID)),
            "slopes": np.ascontiguousarray(np.asarray(inputs["L0_ffn_spline_slopes"]).reshape(64, DSL * GRID)),
            "oscale": np.ascontiguousarray(np.asarray(inputs["L0_ffn_output_scale"])[None, :]),
            "pm_scaled": pm[None, :],
        })
    return per_core


def kernel(**inputs):
    in_maps = _host_layout(inputs)
    nc, dbg = build_program()
    res = run_bass_kernel_spmd(nc, in_maps, core_ids=list(range(8)))
    if DBG:
        DBG["results"] = res
    out = np.zeros((4, 2048, D), dtype=np.float32)
    for c in range(8):
        b, par = c // 2, c % 2
        yc = res.results[c]["y"]
        for k, g in enumerate(G_PAR[par]):
            out[b, 128 * g:128 * g + 128, :] = yc[128 * k:128 * k + 128, :]
    return out
